# version 05
# baseline (speedup 1.0000x reference)
import numpy as np
from contextlib import ExitStack

import concourse.bass as bass
import concourse.mybir as mybir
from concourse.bass_utils import run_bass_kernel_spmd

F32 = mybir.dt.float32
BF16 = mybir.dt.bfloat16
AF = mybir.ActivationFunctionType
ALU = mybir.AluOpType
AX = mybir.AxisListType

ENGS = ("tensor", "vector", "scalar", "gpsimd", "sync")

D, T = 2048, 2048
NCH, NT = 16, 4
CA, NCA, HA = 128, 16, 16
CB, NCB, HB = 512, 4, 8
NE, HID, HS = 16, 1024, 4096

CPV = [1e-5, 1e-6, -0.5, 1.0, -1.0, 0.0, 0.2, 1.0 / 2048.0]
CP = {v: i for i, v in enumerate(CPV)}


class Sem:
    __slots__ = ("h", "n")

    def __init__(self, h):
        self.h = h
        self.n = 0


class Strm:

    def __init__(self, sem, sem2=None):
        self.s = sem
        self.s2 = sem2 if sem2 is not None else sem
        self.base = sem.n
        self.base2 = self.s2.n
        self.k = 0
        self.n01 = [0, 0]

    def inc(self, inst, k=1):
        inst.then_inc(self.s.h, k)
        self.s.n += k

    def _sem(self, p):
        return self.s if p == 0 else self.s2

    def dma(self, q, dst, src):
        p = self.k % 2
        sem = self._sem(p)
        base = self.base if p == 0 else self.base2
        if self.s2 is not self.s and self.n01[p] >= 1:
            q.wait_ge(sem.h, base + 16 * self.n01[p])
        q.dma_start(dst, src).then_inc(sem.h, 16)
        sem.n += 16
        self.n01[p] += 1
        self.k += 1

    def dma_batched(self, q, dst, src, depth=8):
        if (self.s.n - self.base) // 16 % depth == 0 and self.s.n > self.base:
            q.wait_ge(self.s.h, self.s.n)
        p_save = self.k
        q.dma_start(dst, src).then_inc(self.s.h, 16)
        self.s.n += 16
        self.k += 1

    def drain(self, q):
        q.wait_ge(self.s.h, self.s.n)

    def wdma(self, eng, j):
        ne = j // 2 + 1
        no = (j + 1) // 2
        if ne > 0:
            eng.wait_ge(self.s.h, self.base + 16 * ne)
        if no > 0 and self.s2 is not self.s:
            eng.wait_ge(self.s2.h, self.base2 + 16 * no)

    def waitall(self, eng):
        eng.wait_ge(self.s.h, self.s.n)
        if self.s2 is not self.s:
            eng.wait_ge(self.s2.h, self.s2.n)

    def wait(self, eng, i):
        eng.wait_ge(self.s.h, self.base + i)


class KB:
    def __init__(self, nc, es):
        self.nc = nc
        self.es = es
        self.pi = 0
        self.pb = Sem(es.enter_context(nc.semaphore("pb")))
        self.aux = Sem(es.enter_context(nc.semaphore("aux")))
        self.pool = [Sem(es.enter_context(nc.semaphore(f"g{i}")))
                     for i in range(56)]
        self.pooli = 0

    def strm(self):
        s = Strm(self.pool[self.pooli % len(self.pool)])
        self.pooli += 1
        return s

    def dstrm(self):
        a = self.pool[self.pooli % len(self.pool)]
        self.pooli += 1
        b = self.pool[self.pooli % len(self.pool)]
        self.pooli += 1
        return Strm(a, b)

    def sb(self, name, shape, dt):
        return self.es.enter_context(self.nc.sbuf_tensor(name, shape, dt))

    def phase(self, name, **bodies):
        i = self.pi
        pb = self.pb
        with self.nc.Block(f"ph{i}_{name}") as blk:
            for en in ENGS:
                fn = bodies.get(en)
                pre = bodies.get("pre_" + en)

                def mk(en=en, fn=fn, pre=pre):
                    def body(eng):
                        if pre is not None:
                            pre(eng)
                        eng.wait_ge(pb.h, 5 * i)
                        if fn is not None:
                            fn(eng)
                        eng.drain()
                        eng.sem_inc(pb.h, 1)
                    return body
                getattr(blk, en)(mk())
        pb.n += 5
        self.pi += 1


def build_program(stop_after=None, dump=None):
    nc = bass.Bass("TRN2", target_bir_lowering=False)

    xin = nc.dram_tensor("x", [2, D, T], F32, kind="ExternalInput").ap()
    out = nc.dram_tensor("out", [2, D, T], F32, kind="ExternalOutput").ap()
    xw = nc.dram_tensor("xw", [2, D, T], F32, kind="Internal").ap()

    wshapes = {
        "wq": [D, D], "wk": [D, D], "wv": [D, D], "wo": [D, D],
        "bwq": [D, D], "bwk": [D, D], "bwv": [D, D],
        "we1": [NE * D, HID], "we3": [NE * D, HID], "we2": [NE * HID, D],
        "ws1": [D, HS], "ws3": [D, HS], "ws2": [HS, D],
    }
    wf32 = {k: nc.dram_tensor(k, s, F32, kind="ExternalInput").ap()
            for k, s in wshapes.items()}
    wbfd = {k: nc.dram_tensor(k + "_bf", s, BF16, kind="Internal").ap()
            for k, s in wshapes.items()}

    wfm = nc.dram_tensor("wfm", [128, 256], F32, kind="ExternalInput").ap()
    wrm = nc.dram_tensor("wrm", [128, 256], F32, kind="ExternalInput").ap()
    normwm = nc.dram_tensor("normwm", [128, 16], F32, kind="ExternalInput").ap()
    biasm = nc.dram_tensor("biasm", [2, 128, 16], F32, kind="ExternalInput").ap()
    biasrow = nc.dram_tensor("biasrow", [2, D], F32, kind="ExternalInput").ap()
    identm = nc.dram_tensor("identm", [128, 128], F32, kind="ExternalInput").ap()
    dbTm = nc.dram_tensor("dbTm", [HB, CB, CB], F32, kind="ExternalInput").ap()
    csr = nc.dram_tensor("csr", [1, HB * CB], F32, kind="ExternalInput").ap()
    ksbtm = nc.dram_tensor("ksbtm", [128, 32], F32, kind="ExternalInput").ap()

    def mkslab(name):
        return nc.dram_tensor(name, [NCH, NT, 128, 512], BF16, kind="Internal").ap()
    sA, sB, sC, sD, sE = (mkslab(n) for n in ["sA", "sB", "sC", "sD", "sE"])
    ge_sl = nc.dram_tensor("ge", [NE, 8, NT, 128, 512], BF16, kind="Internal").ap()
    gs_sl = nc.dram_tensor("gs", [32, NT, 128, 512], BF16, kind="Internal").ap()
    facd = nc.dram_tensor("facd", [3, HA, T], F32, kind="Internal").ap()
    rowd = nc.dram_tensor("rowd", [16, 16], F32, kind="Internal").ap()
    cwd = nc.dram_tensor("cwd", [NE, T], F32, kind="Internal").ap()
    xlnd = nc.dram_tensor("xlnd", [128, NCH * T], BF16, kind="Internal").ap()

    dbg = None
    if dump is not None:
        dshape = {"xw": [2 * D, T], "sA": [NCH * NT * 128, 512],
                  "sB": [NCH * NT * 128, 512], "sC": [NCH * NT * 128, 512],
                  "sD": [NCH * NT * 128, 512], "sE": [NCH * NT * 128, 512],
                  "facd": [3 * HA, T], "cwd": [NE, T], "xlnd": [128, NCH * T],
                  "ge": [NE * 8 * NT * 128, 512], "gs": [32 * NT * 128, 512]}[dump]
        dbg = nc.dram_tensor("dbg", dshape, F32, kind="ExternalOutput").ap()

    es = ExitStack()
    with es:
        kb = KB(nc, es)
        P = [es.enter_context(nc.psum_tensor(f"pk{i}", [128, 512], F32))
             for i in range(8)]

        xarena = kb.sb("xarena", [128, NCH * T], BF16)
        xln = xarena
        wbuf = kb.sb("wbuf", [128, 5 * 8192], BF16)
        ones_r_bf = kb.sb("ones_r_bf", [1, 128], BF16)
        ones_r_f = kb.sb("ones_r_f", [1, 128], F32)
        ones_c_bf = kb.sb("ones_c_bf", [128, 1], BF16)
        cpool = kb.sb("cpool", [128, 8], F32)
        ident = kb.sb("ident", [128, 128], F32)
        stage = kb.sb("stage", [1, 2 * 512], F32)
        nwm_sb = kb.sb("nwm_sb", [128, 16], F32)
        bqm_sb = kb.sb("bqm_sb", [128, 32], F32)
        brow_sb = kb.sb("brow_sb", [1, 2 * D], BF16)
        wfm_sb = kb.sb("wfm_sb", [128, 256], BF16)
        wrm_sb = kb.sb("wrm_sb", [128, 256], BF16)
        smal = kb.sb("smal", [16, 64], F32)
        smalT = kb.sb("smalT", [16, 16], F32)
        zz16 = kb.sb("zz16", [16, 128], F32)
        fkr_tm = kb.sb("fkr_tm", [128, NCA * 16], F32)
        etot_b = kb.sb("etot_b", [128, NCA * 16], F32)
        S_a = kb.sb("S_a", [128, HA * 128], F32)
        S_ab = kb.sb("S_ab", [128, HA * 128], BF16)
        ksbt_sb = kb.sb("ksbt_sb", [128, 32], F32)
        SM_S, SM_TOT, SM_ETOT, SM_NEGS = 0, 16, 32, 48

        def init_g(g):
            g.memset(ones_r_bf[0:1, :], 1.0)
            g.memset(ones_r_f[0:1, :], 1.0)
            g.memset(ones_c_bf[:, 0:1], 1.0)
            g.memset(zz16[:, :], 0.0)
            for v, i in CP.items():
                g.memset(cpool[:, i:i + 1], float(v))
            a = Strm(kb.aux)
            a.dma(g, ident[:, :], identm[:, :])
            a.dma(g, nwm_sb[:, :], normwm[:, :])
            a.dma(g, bqm_sb[:, 0:16], biasm[0])
            a.dma(g, bqm_sb[:, 16:32], biasm[1])
            a.dma(g, brow_sb[0:1, :], biasrow.rearrange("a b -> (a b)"))
            a.dma(g, wfm_sb[:, :], wfm[:, :])
            a.dma(g, wrm_sb[:, :], wrm[:, :])
            a.dma(g, ksbt_sb[:, :], ksbtm[:, :])
            g.wait_ge(a.s.h, a.s.n)

        kb.phase("init", gpsimd=init_g)

        wc_done = {}

        def w0_g(g):
            a = Strm(kb.aux)
            for k in ["wq", "wk", "wv", "wo", "ws1", "ws3", "we1", "we3",
                      "we2", "ws2", "bwq", "bwk", "bwv"]:
                src, dst = wf32[k], wbfd[k]
                n = int(np.prod(wshapes[k]))
                fs, fd = src.rearrange("a b -> (a b)"), dst.rearrange("a b -> (a b)")
                CH = 1 << 20
                for o in range(0, n, CH):
                    a.dma_batched(g, fd[o:min(n, o + CH)], fs[o:min(n, o + CH)])
                a.drain(g)
                wc_done[k] = kb.aux.n

        kb.phase("w0", gpsimd=w0_g)

        SLOTW = 8192

        def load_half(q, st, wt, kdim, c0, colw, slot):
            kch = kdim // 128
            assert kch * colw <= SLOTW
            ap = wt[:, c0:c0 + colw].rearrange("(k p) n -> p k n", p=128)
            step = max(1, kch // 4)
            for ks in range(0, kch, step):
                ke = min(kch, ks + step)
                st.dma(q, wbuf[:, slot * SLOTW + ks * colw:slot * SLOTW + ke * colw],
                       ap[:, ks:ke, :])

        def wtile(slot, kchi, colw, c0, w=128):
            o = slot * SLOTW + kchi * colw + c0
            return wbuf[:, o:o + w]

        def ln_phase(tag, b, src):
            with ExitStack() as ps:
                xtile = ps.enter_context(nc.sbuf_tensor(f"xt{tag}", [128, 2 * 512], F32))
                ntmp = ps.enter_context(nc.sbuf_tensor(f"ntp{tag}", [128, 512], F32))
                xbf = wbuf[:, 2 * SLOTW:2 * SLOTW + 8192]
                x2 = wbuf[:, 3 * SLOTW:3 * SLOTW + 8192]
                srow = ps.enter_context(nc.sbuf_tensor(f"srw{tag}", [1, 4 * 512], F32))
                Sx = kb.dstrm()
                Scast, Ssq, Smm, Ssrow, Svar, Srstd, Snb, Sbc, Snorm = (
                    kb.strm() for _ in range(9))

                def f_sync(sy):
                    for nt in range(NT):
                        for dch in range(NCH):
                            i = nt * NCH + dch
                            if i >= 2:
                                Scast.wait(sy, i - 1)
                                Ssq.wait(sy, i - 1)
                            Sx.dma(sy, xtile[:, (i % 2) * 512:(i % 2) * 512 + 512],
                                   src[b, dch * 128:(dch + 1) * 128,
                                       nt * 512:(nt + 1) * 512])

                def f_vec(v):
                    for nt in range(NT):
                        for dch in range(NCH):
                            i = nt * NCH + dch
                            Sx.wdma(v, i)
                            sl = slice((i % 2) * 512, (i % 2) * 512 + 512)
                            Scast.inc(v.tensor_copy(
                                xbf[:, dch * 512:(dch + 1) * 512], xtile[:, sl]))
                        Ssrow.wait(v, nt + 1)
                        Svar.inc(v.tensor_sub(srow[0:1, 512:1024],
                                              srow[0:1, 512:1024],
                                              srow[0:1, 2 * 512:3 * 512]))
                        Srstd.wait(v, nt + 1)
                        Snb.inc(v.scalar_tensor_tensor(
                            srow[0:1, 3 * 512:4 * 512], srow[0:1, 0:512], -1.0,
                            srow[0:1, 2 * 512:3 * 512], ALU.mult, ALU.mult))
                        Sbc.wait(v, nt + 1)
                        for dch in range(NCH):
                            v.tensor_tensor(ntmp[:, 0:512],
                                            xbf[:, dch * 512:(dch + 1) * 512],
                                            P[6][:, 0:512], ALU.mult)
                            o = dch * T + nt * 512
                            li = v.tensor_tensor(xln[:, o:o + 512], ntmp[:, 0:512],
                                                 P[7][:, 0:512], ALU.add)
                        Snorm.inc(li)

                def f_scal(s):
                    for nt in range(NT):
                        for dch in range(NCH):
                            i = nt * NCH + dch
                            Sx.wdma(s, i)
                            sl = slice((i % 2) * 512, (i % 2) * 512 + 512)
                            Ssq.inc(s.activation(x2[:, dch * 512:(dch + 1) * 512],
                                                 xtile[:, sl], AF.Square))
                        Smm.wait(s, 2 * (nt + 1))
                        if nt > 0:
                            Snb.wait(s, nt)
                            Sbc.wait(s, nt)
                        s.mul(srow[0:1, 0:512], P[4][0:1, 0:512], 1.0 / 2048.0)
                        s.mul(srow[0:1, 512:1024], P[5][0:1, 0:512], 1.0 / 2048.0)
                        Ssrow.inc(s.activation(srow[0:1, 2 * 512:3 * 512],
                                               srow[0:1, 0:512], AF.Square))
                        Svar.wait(s, nt + 1)
                        s.activation(srow[0:1, 2 * 512:3 * 512],
                                     srow[0:1, 512:1024], AF.Ln,
                                     bias=cpool[0:1, CP[1e-5]:CP[1e-5] + 1])
                        Srstd.inc(s.activation(
                            srow[0:1, 2 * 512:3 * 512], srow[0:1, 2 * 512:3 * 512],
                            AF.Exp, scale=cpool[0:1, CP[-0.5]:CP[-0.5] + 1]))

                def f_tens(t):
                    for nt in range(NT):
                        Scast.wait(t, NCH * (nt + 1))
                        Ssq.wait(t, NCH * (nt + 1))
                        if nt > 0:
                            Ssrow.wait(t, nt)
                        for dch in range(NCH):
                            li = t.matmul(P[4][0:1, 0:512], ones_c_bf[:, 0:1],
                                          xbf[:, dch * 512:(dch + 1) * 512],
                                          start=(dch == 0), stop=(dch == NCH - 1))
                        Smm.inc(li)
                        for dch in range(NCH):
                            li = t.matmul(P[5][0:1, 0:512], ones_c_bf[:, 0:1],
                                          x2[:, dch * 512:(dch + 1) * 512],
                                          start=(dch == 0), stop=(dch == NCH - 1))
                        Smm.inc(li)
                        Snb.wait(t, nt + 1)
                        if nt > 0:
                            Snorm.wait(t, nt)
                        t.matmul(P[6][:, 0:512], ones_r_f[0:1, :],
                                 srow[0:1, 2 * 512:3 * 512], start=True, stop=True)
                        Sbc.inc(t.matmul(P[7][:, 0:512], ones_r_f[0:1, :],
                                         srow[0:1, 3 * 512:4 * 512],
                                         start=True, stop=True))

                kb.phase(f"ln{tag}", sync=f_sync, vector=f_vec, scalar=f_scal,
                         tensor=f_tens)

        def a_ld_phase(b):
            with ExitStack() as ps:
                wfree = wbuf[:, 0:32768].bitcast(F32)
                ldq = [ps.enter_context(nc.sbuf_tensor(f"ldq{i}_{b}", [16, T], F32))
                       for i in range(2)]
                ldq += [wfree[0:16, i * T:(i + 1) * T] for i in range(4)]
                LD, CUM, FQS, FKS, FQI, FKR = range(6)
                Smm = kb.strm()
                Sz = kb.strm()
                Sls = kb.strm()
                Ssm = kb.strm()
                Sfq = kb.strm()
                Str = kb.strm()
                Sev = kb.strm()
                Srow = kb.dstrm()
                Sbc2 = kb.strm()
                Sebc = kb.strm()
                Sst = kb.dstrm()

                def f_tens(t):
                    for nt in range(NT):
                        if nt >= 2:
                            Sz.wait(t, nt - 1)
                        for kch in range(NCH):
                            li = t.matmul(P[nt % 2][0:16, 0:512],
                                          wfm_sb[:, kch * 16:(kch + 1) * 16],
                                          xln[:, kch * T + nt * 512:
                                              kch * T + nt * 512 + 512],
                                          start=(kch == 0), stop=(kch == NCH - 1))
                        Smm.inc(li)
                    Sfq.wait(t, NCA + 1)
                    for c in range(NCA):
                        if c >= 2:
                            Sev.wait(t, c - 1)
                        Str.inc(t.transpose(P[2 + c % 2][0:128, 0:16],
                                              ldq[FKR][:, c * 128:(c + 1) * 128],
                                              ident[0:16, 0:16]))
                    Sev.wait(t, NCA)
                    Str.inc(t.transpose(P[2][0:16, 0:16],
                                         smal[:, SM_ETOT:SM_ETOT + 16],
                                         ident[0:16, 0:16]))
                    for c in range(NCA):
                        Srow.wdma(t, c)
                        if c >= 2:
                            Sebc.wait(t, c - 1)
                        Sbc2.inc(t.matmul(P[4 + c % 2][:, 0:16], ones_r_f[0:1, :],
                                           stage[0:1, (c % 2) * 512:
                                                 (c % 2) * 512 + 16],
                                           start=True, stop=True))

                def f_scal(s):
                    for nt in range(NT):
                        Smm.wait(s, nt + 1)
                        sl = slice(nt * 512, (nt + 1) * 512)
                        Sz.inc(s.copy(ldq[LD][:, sl], P[nt % 2][0:16, 0:512]))
                    s.activation(ldq[FQS][:, :], ldq[LD][:, :], AF.Abs)
                    s.activation(ldq[FQS][:, :], ldq[FQS][:, :], AF.Exp,
                                 scale=cpool[0:16, CP[-1.0]:CP[-1.0] + 1])
                    Sls.inc(s.activation(ldq[FQS][:, :], ldq[FQS][:, :], AF.Ln,
                                         bias=cpool[0:16, CP[1.0]:CP[1.0] + 1]))
                    Ssm.wait(s, 1)
                    s.activation(ldq[FQI][:, :], ldq[CUM][:, :], AF.Exp)
                    Sfq.inc(s.activation(smal[:, SM_ETOT:SM_ETOT + 16],
                                         smal[:, SM_TOT:SM_TOT + 16], AF.Exp))
                    for c in range(NCA):
                        sl = slice(c * 128, (c + 1) * 128)
                        s.activation(ldq[FQS][:, sl], ldq[CUM][:, sl], AF.Exp,
                                     bias=smal[:, SM_NEGS + c:SM_NEGS + c + 1])
                        s.activation(ldq[FKS][:, sl], ldq[CUM][:, sl], AF.Exp,
                                     scale=cpool[0:16, CP[-1.0]:CP[-1.0] + 1],
                                     bias=smal[:, SM_S + c:SM_S + c + 1])
                        Sfq.inc(s.activation(
                            ldq[FKR][:, sl], ldq[CUM][:, sl], AF.Exp,
                            scale=cpool[0:16, CP[-1.0]:CP[-1.0] + 1],
                            bias=smal[:, SM_TOT + c:SM_TOT + c + 1]))

                def f_vec(v):
                    Sls.wait(v, 1)
                    v.tensor_scalar_min(ldq[FKS][:, :], ldq[LD][:, :], 0.0)
                    v.tensor_sub(ldq[LD][:, :], ldq[FKS][:, :], ldq[FQS][:, :])
                    for c in range(NCA):
                        sl = slice(c * 128, (c + 1) * 128)
                        v.tensor_tensor_scan(ldq[CUM][:, sl], ldq[LD][:, sl],
                                             zz16[:, :], 0.0, ALU.add, ALU.add)
                    v.tensor_copy(smal[:, SM_S:SM_S + 16], ldq[CUM][:, 63:T:128])
                    v.tensor_copy(smal[:, SM_TOT:SM_TOT + 16], ldq[CUM][:, 127:T:128])
                    Ssm.inc(v.tensor_scalar_mul(smal[:, SM_NEGS:SM_NEGS + 16],
                                                smal[:, SM_S:SM_S + 16], -1.0))
                    for c in range(NCA):
                        Str.wait(v, c + 1)
                        Sev.inc(v.tensor_copy(fkr_tm[:, c * 16:(c + 1) * 16],
                                              P[2 + c % 2][0:128, 0:16]))
                    Str.wait(v, NCA + 1)
                    Sev.inc(v.tensor_copy(smalT[:, :], P[2][0:16, 0:16]))
                    for c in range(NCA):
                        Sbc2.wait(v, c + 1)
                        Sebc.inc(v.tensor_copy(etot_b[:, c * 16:(c + 1) * 16],
                                               P[4 + c % 2][:, 0:16]))

                def f_sync(sy):
                    Sfq.wait(sy, NCA + 1)
                    Sst.dma(sy, facd[0], ldq[FQS][:, :])
                    Sst.dma(sy, facd[1], ldq[FKS][:, :])
                    Sst.dma(sy, facd[2], ldq[FQI][:, :])
                    Sev.wait(sy, NCA + 1)
                    Sst.dma(sy, rowd[:, :], smalT[:, :])
                    Sst.waitall(sy)
                    for c in range(NCA):
                        if c >= 2:
                            Sbc2.wait(sy, c - 1)
                        Srow.dma(sy, stage[0:1, (c % 2) * 512:(c % 2) * 512 + 16],
                                 rowd[c:c + 1, :])
                kb.phase(f"ald{b}", tensor=f_tens, scalar=f_scal, vector=f_vec,
                         sync=f_sync)

        def proj_fm(tag, wname, kind, slabs):
            with ExitStack() as ps:
                tmp = ps.enter_context(nc.sbuf_tensor(f"tp{tag}", [128, 4 * 512], F32))
                osb = ps.enter_context(nc.sbuf_tensor(f"os{tag}", [128, 4 * 512], BF16))
                Swl, Sst, Srow = kb.dstrm(), kb.dstrm(), kb.dstrm()
                Smm, Ssil, Sev, Sbc = (kb.strm() for _ in range(4))
                nfac = 2 if kind == "qd2" else (1 if kind == "kd" else 0)
                fmap = [0, 2] if kind == "qd2" else [1]

                def f_sync(sy):
                    sy.wait_ge(kb.aux.h, wc_done[wname])
                    load_half(sy, Swl, wbfd[wname], D, 0, 512, 0)
                    rows_done = [0]

                    def rows(hi):
                        for i in range(rows_done[0], min(64 * nfac, hi)):
                            if i >= 2:
                                Sbc.wait(sy, i - 1)
                            ti, f = i // nfac, i % nfac
                            m, nt = ti // NT, ti % NT
                            Srow.dma(sy,
                                     stage[0:1, (i % 2) * 512:(i % 2) * 512 + 512],
                                     facd[fmap[f], m:m + 1,
                                          nt * 512:(nt + 1) * 512])
                        rows_done[0] = max(rows_done[0], min(64 * nfac, hi))
                    if nfac:
                        rows(2)
                    load_half(sy, Swl, wbfd[wname], D, 512, 512, 1)
                    for ti in range(64):
                        if ti >= 2:
                            store_tile(sy, ti - 2)
                        if nfac:
                            rows((ti + 3) * nfac)
                        if ti % 16 == 15 and ti // 16 + 2 < 4:
                            p = ti // 16 + 2
                            Smm.wait(sy, 16 * (p - 1))
                            load_half(sy, Swl, wbfd[wname], D, p * 512, 512, p % 2)
                    store_tile(sy, 62)
                    store_tile(sy, 63)
                    Sst.waitall(sy)
                    Swl.waitall(sy)
                    if nfac:
                        Srow.waitall(sy)

                def f_tens(t):
                    for piece in range(4):
                        Swl.wdma(t, 4 * (piece + 1) - 1)
                        for m4 in range(4):
                            m = piece * 4 + m4
                            for nt in range(NT):
                                ti = m * NT + nt
                                bank = ti % 6
                                if ti >= 6:
                                    Ssil.wait(t, ti - 5)
                                for kch in range(NCH):
                                    li = t.matmul(P[bank][:, 0:512],
                                                  wtile(piece % 2, kch, 512, m4 * 128),
                                                  xln[:, kch * T + nt * 512:
                                                      kch * T + nt * 512 + 512],
                                                  start=(kch == 0),
                                                  stop=(kch == NCH - 1))
                                Smm.inc(li)
                                for f in range(nfac):
                                    i = ti * nfac + f
                                    Srow.wdma(t, i)
                                    if i >= 2:
                                        Sev.wait(t, i - 1)
                                    Sbc.inc(t.matmul(
                                        P[6 + i % 2][:, 0:512], ones_r_f[0:1, :],
                                        stage[0:1, (i % 2) * 512:
                                              (i % 2) * 512 + 512],
                                        start=True, stop=True))

                def f_scal(s):
                    for ti in range(64):
                        bank = ti % 6
                        Smm.wait(s, ti + 1)
                        if nfac:
                            if ti >= 4:
                                Sev.wait(s, (ti - 3) * nfac)
                            Ssil.inc(s.activation(
                                tmp[:, (ti % 4) * 512:(ti % 4) * 512 + 512],
                                P[bank][:, 0:512], AF.Silu))
                        else:
                            if ti >= 4:
                                Sst.wdma(s, ti - 4)
                            m = ti // NT
                            bc = (0 if kind == "biasq" else 16) + m
                            Ssil.inc(s.activation(
                                osb[:, (ti % 4) * 512:(ti % 4) * 512 + 512],
                                P[bank][:, 0:512], AF.Identity,
                                bias=bqm_sb[:, bc:bc + 1]))

                def f_vec(v):
                    if not nfac:
                        return
                    for ti in range(64):
                        Ssil.wait(v, ti + 1)
                        if ti >= 2:
                            Sst.wdma(v, nfac * (ti - 1) - 1)
                        for f in range(nfac):
                            i = ti * nfac + f
                            Sbc.wait(v, i + 1)
                            Sev.inc(v.tensor_tensor(
                                osb[:, ((ti % 2) * 2 + f) * 512:
                                    ((ti % 2) * 2 + f) * 512 + 512],
                                tmp[:, (ti % 4) * 512:(ti % 4) * 512 + 512],
                                P[6 + i % 2][:, 0:512], ALU.mult))

                def store_tile(sy, ti):
                    m, nt = ti // NT, ti % NT
                    if nfac:
                        Sev.wait(sy, (ti + 1) * nfac)
                        for f in range(nfac):
                            Sst.dma(sy, slabs[f][m][nt],
                                    osb[:, ((ti % 2) * 2 + f) * 512:
                                        ((ti % 2) * 2 + f) * 512 + 512])
                    else:
                        Ssil.wait(sy, ti + 1)
                        Sst.dma(sy, slabs[0][m][nt],
                                osb[:, (ti % 4) * 512:(ti % 4) * 512 + 512])

                kb.phase(f"pfm{tag}", sync=f_sync, tensor=f_tens,
                         scalar=f_scal, vector=f_vec)

        def proj_tm(tag, wname, kind, slabo, browi=0):
            with ExitStack() as ps:
                tmp = ps.enter_context(nc.sbuf_tensor(f"tp{tag}", [128, 4 * 512], F32))
                osb = ps.enter_context(nc.sbuf_tensor(f"os{tag}", [128, 4 * 512], BF16))
                Swl, Sst = kb.dstrm(), kb.dstrm()
                Smm, Ssil, Sev = (kb.strm() for _ in range(3))

                def f_sync(sy):
                    sy.wait_ge(kb.aux.h, wc_done[wname])
                    load_half(sy, Swl, wbfd[wname], D, 0, 512, 0)
                    load_half(sy, Swl, wbfd[wname], D, 512, 512, 1)
                    for ti in range(64):
                        ntd, tokch = ti // 16, ti % 16
                        if kind == "kdr":
                            Sev.wait(sy, ti + 1)
                        else:
                            Ssil.wait(sy, ti + 1)
                        Sst.dma(sy, slabo[tokch][ntd],
                                osb[:, (ti % 4) * 512:(ti % 4) * 512 + 512])
                        if ti % 16 == 1 and 2 <= ti // 16 + 1 < 4:
                            p = ti // 16 + 1
                            load_half(sy, Swl, wbfd[wname], D, p * 512, 512, p % 2)
                    Sst.waitall(sy)
                    Swl.waitall(sy)

                def f_tens(t):
                    for ntd in range(4):
                        Swl.wdma(t, 4 * (ntd + 1) - 1)
                        for tokch in range(16):
                            ti = ntd * 16 + tokch
                            bank = ti % 6
                            if ti >= 6:
                                if kind == "kdr":
                                    Sev.wait(t, ti - 5)
                                else:
                                    Ssil.wait(t, ti - 5)
                            for kch in range(NCH):
                                st = (kch == 0)
                                sp = (kch == NCH - 1) and kind != "btm"
                                li = t.matmul(P[bank][:, 0:512],
                                              xln[:, kch * T + tokch * 128:
                                                  kch * T + tokch * 128 + 128],
                                              wtile(ntd % 2, kch, 512, 0, 512),
                                              start=st, stop=sp)
                            if kind == "btm":
                                li = t.matmul(P[bank][:, 0:512], ones_r_bf[0:1, :],
                                              brow_sb[0:1, browi * D + ntd * 512:
                                                      browi * D + (ntd + 1) * 512],
                                              start=False, stop=True)
                            Smm.inc(li)

                def f_scal(s):
                    for ti in range(64):
                        bank = ti % 6
                        Smm.wait(s, ti + 1)
                        if kind == "kdr":
                            if ti >= 4:
                                Sev.wait(s, ti - 3)
                            Ssil.inc(s.activation(
                                tmp[:, (ti % 4) * 512:(ti % 4) * 512 + 512],
                                P[bank][:, 0:512], AF.Silu))
                        else:
                            if ti >= 4:
                                Sst.wdma(s, ti - 4)
                            Ssil.inc(s.activation(
                                osb[:, (ti % 4) * 512:(ti % 4) * 512 + 512],
                                P[bank][:, 0:512], AF.Identity))

                def f_vec(v):
                    if kind != "kdr":
                        return
                    for ti in range(64):
                        Ssil.wait(v, ti + 1)
                        if ti >= 4:
                            Sst.wdma(v, ti - 4)
                        ntd, tokch = ti // 16, ti % 16
                        for hh in range(4):
                            h = ntd * 4 + hh
                            li = v.tensor_scalar(
                                osb[:, (ti % 4) * 512 + hh * 128:
                                    (ti % 4) * 512 + (hh + 1) * 128],
                                tmp[:, (ti % 4) * 512 + hh * 128:
                                    (ti % 4) * 512 + (hh + 1) * 128],
                                fkr_tm[:, tokch * 16 + h:tokch * 16 + h + 1],
                                None, ALU.mult)
                        Sev.inc(li)

                kb.phase(f"ptm{tag}", sync=f_sync, tensor=f_tens,
                         scalar=f_scal, vector=f_vec)


        def a_chunk_phase(b):
            with ExitStack() as ps:
                fmb = xarena[:, 0:24576]
                tmb = xarena[:, 24576:32768]
                scf = ps.enter_context(nc.sbuf_tensor(f"scf{b}", [128, 2 * 128], F32))
                scm = ps.enter_context(nc.sbuf_tensor(f"scm{b}", [128, 2 * 128], BF16))
                o_buf = ps.enter_context(nc.sbuf_tensor(f"obf{b}", [128, 16 * 128], F32))
                o2b = ps.enter_context(nc.sbuf_tensor(f"o2b{b}", [128, 16 * 128], BF16))
                onb = wbuf[:, 4 * SLOTW:5 * SLOTW]
                rrow = ps.enter_context(nc.sbuf_tensor(f"rr{b}", [1, 128], F32))
                ntp = ps.enter_context(nc.sbuf_tensor(f"ntp{b}", [128, 2 * 128], F32))
                xres = ps.enter_context(nc.sbuf_tensor(f"xr{b}", [128, 2 * 512], F32))
                xout = ps.enter_context(nc.sbuf_tensor(f"xo{b}", [128, 2 * 512], F32))

                Sfm, Stm, Swo, Sxr, Sxw = (kb.dstrm() for _ in range(5))
                Smm1, Ses, Seg, Smm2, Smm3 = (kb.strm() for _ in range(5))
                Sov, Svu, Ssq2, Smm4, Srs, Sbc3, Sno = (kb.strm() for _ in range(7))
                Smm5, Svr, Sz0 = (kb.strm() for _ in range(3))

                def fmtile(tno, h, c):
                    o = (tno * 16 + h) * 512 + (c % 4) * 128
                    return fmb[:, o:o + 128]

                def tmtile(tno, c, h):
                    o = (c % 2) * 2 * 2048 + tno * 2048 + (h // 4) * 512 + (h % 4) * 128
                    return tmb[:, o:o + 128]

                def f_sync(sy):
                    for g in range(4):
                        if g >= 1:
                            Smm3.wait(sy, 64 * g)
                        for tno, sl in enumerate([sA, sB, sC]):
                            for h in range(16):
                                Sfm.dma(sy, fmb[:, (tno * 16 + h) * 512:
                                                (tno * 16 + h) * 512 + 512],
                                        sl[h][g])
                        if g == 0:
                            for c in (0, 1):
                                for tno, sl in enumerate([sD, sE]):
                                    for dblk in range(4):
                                        o = (c % 2) * 4096 + tno * 2048 + dblk * 512
                                        Stm.dma(sy, tmb[:, o:o + 512], sl[c][dblk])
                            sy.wait_ge(kb.aux.h, wc_done["wo"])
                            for p in range(4):
                                load_half(sy, Swo, wbfd["wo"], D, p * 512, 512, p)
                        for c in range(4 * g, 4 * g + 4):
                            cc = c + 2
                            if cc < NCA:
                                Smm3.wait(sy, 16 * (cc - 1))
                                for tno, sl in enumerate([sD, sE]):
                                    for dblk in range(4):
                                        o = (cc % 2) * 4096 + tno * 2048 + dblk * 512
                                        Stm.dma(sy, tmb[:, o:o + 512], sl[cc][dblk])
                        for dc in range(16):
                            i = 16 * g + dc
                            if i >= 2:
                                Svr.wait(sy, i - 1)
                            Sxr.dma(sy, xres[:, (dc % 2) * 512:(dc % 2) * 512 + 512],
                                    xin[b, dc * 128:(dc + 1) * 128,
                                        g * 512:(g + 1) * 512])
                            if i >= 1:
                                Svr.wait(sy, i)
                                pdc = (dc - 1) % 16
                                pg = g if dc >= 1 else g - 1
                                Sxw.dma(sy, xw[b, pdc * 128:(pdc + 1) * 128,
                                               pg * 512:(pg + 1) * 512],
                                        xout[:, (pdc % 2) * 512:(pdc % 2) * 512 + 512])
                    Svr.wait(sy, 64)
                    Sxw.dma(sy, xw[b, 15 * 128:16 * 128, 3 * 512:4 * 512],
                            xout[:, 512:1024])
                    Sxw.waitall(sy)
                    Sfm.waitall(sy)
                    Stm.waitall(sy)
                    Swo.waitall(sy)
                    Sxr.waitall(sy)

                def f_gp(g_):
                    g_.memset(S_a[:, :], 0.0)
                    Sz0.inc(g_.memset(S_ab[:, :], 0.0))
                    for c in range(NCA):
                        for h in range(HA):
                            i = c * 16 + h
                            Ses.wait(g_, i + 1)
                            Seg.inc(g_.affine_select(
                                scm[:, (h % 2) * 128:(h % 2) * 128 + 128],
                                scf[:, (h % 2) * 128:(h % 2) * 128 + 128],
                                [[1, 128]], ALU.is_ge, 0.0, base=0,
                                channel_multiplier=-1))

                def f_tens(t):
                    Sz0.wait(t, 1)
                    for c in range(NCA):
                        g = c // 4
                        Sfm.wdma(t, 48 * (g + 1) - 1)
                        Stm.wdma(t, 8 * (c + 1) - 1)
                        for h in range(HA):
                            i = c * 16 + h
                            if i >= 2:
                                Ses.wait(t, i - 1)
                            Smm1.inc(t.matmul(P[h % 2][:, 0:128], fmtile(2, h, c),
                                               fmtile(0, h, c), start=True, stop=True))
                            Seg.wait(t, i + 1)
                            if c > 0:
                                Svu.wait(t, (c - 1) * 16 + h + 1)
                            if i >= 2:
                                Sov.wait(t, i - 1)
                            t.matmul(P[2 + h % 2][:, 0:128], tmtile(1, c, h),
                                     scm[:, (h % 2) * 128:(h % 2) * 128 + 128],
                                     start=True, stop=False)
                            Smm2.inc(t.matmul(P[2 + h % 2][:, 0:128],
                                               S_ab[:, h * 128:(h + 1) * 128],
                                               fmtile(1, h, c), start=False, stop=True))
                            if i >= 2:
                                Svu.wait(t, i - 1)
                            Smm3.inc(t.matmul(P[4 + h % 2][:, 0:128], tmtile(0, c, h),
                                               tmtile(1, c, h), start=True, stop=True))
                        Ssq2.wait(t, 16 * (c + 1))
                        if c > 0:
                            Srs.wait(t, c)
                        if c % 4 == 0 and c > 0:
                            Svr.wait(t, 16 * (c // 4))
                        for h in range(HA):
                            li = t.matmul(P[6][0:1, 0:128], ones_c_bf[:, 0:1],
                                          o2b[:, h * 128:(h + 1) * 128],
                                          start=(h == 0), stop=(h == HA - 1))
                        Smm4.inc(li)
                        Srs.wait(t, c + 1)
                        if c > 0:
                            Sno.wait(t, 16 * c)
                        Sbc3.inc(t.matmul(P[7][:, 0:128], ones_r_f[0:1, :],
                                           rrow[0:1, :], start=True, stop=True))
                        if c % 4 == 3:
                            g = c // 4
                            Sno.wait(t, 64 * (g + 1))
                            Swo.waitall(t)
                            for dc in range(16):
                                i = 16 * g + dc
                                if i >= 2:
                                    Svr.wait(t, i - 1)
                                for kch in range(NCH):
                                    li = t.matmul(P[6 + dc % 2][:, 0:512],
                                                  wtile(dc // 4, kch, 512,
                                                        (dc % 4) * 128),
                                                  onb[:, kch * 512:(kch + 1) * 512],
                                                  start=(kch == 0),
                                                  stop=(kch == NCH - 1))
                                Smm5.inc(li)

                def f_scal(s):
                    for c in range(NCA):
                        for h in range(HA):
                            i = c * 16 + h
                            Smm1.wait(s, i + 1)
                            if i >= 2:
                                Seg.wait(s, i - 1)
                            Ses.inc(s.copy(scf[:, (h % 2) * 128:(h % 2) * 128 + 128],
                                           P[h % 2][:, 0:128]))
                        Sov.wait(s, 16 * (c + 1))
                        for h in range(HA):
                            Ssq2.inc(s.activation(o2b[:, h * 128:(h + 1) * 128],
                                                  o_buf[:, h * 128:(h + 1) * 128],
                                                  AF.Square))
                        Smm4.wait(s, c + 1)
                        s.copy(rrow[0:1, :], P[6][0:1, 0:128])
                        s.activation(rrow[0:1, :], rrow[0:1, :], AF.Ln,
                                     scale=cpool[0:1, CP[1.0 / 2048.0]:
                                                 CP[1.0 / 2048.0] + 1],
                                     bias=cpool[0:1, CP[1e-6]:CP[1e-6] + 1])
                        Srs.inc(s.activation(rrow[0:1, :], rrow[0:1, :], AF.Exp,
                                             scale=cpool[0:1, CP[-0.5]:CP[-0.5] + 1]))

                def f_vec(v):
                    Sz0.wait(v, 1)
                    for c in range(NCA):
                        g = c // 4
                        if c >= 1:
                            Ssq2.wait(v, 16 * c)
                        for h in range(HA):
                            i = c * 16 + h
                            Smm2.wait(v, i + 1)
                            Sov.inc(v.tensor_copy(o_buf[:, h * 128:(h + 1) * 128],
                                                  P[2 + h % 2][:, 0:128]))
                            Smm3.wait(v, i + 1)
                            v.scalar_tensor_tensor(
                                S_a[:, h * 128:(h + 1) * 128],
                                S_a[:, h * 128:(h + 1) * 128],
                                etot_b[:, c * 16 + h:c * 16 + h + 1],
                                P[4 + h % 2][:, 0:128], ALU.mult, ALU.add)
                            Svu.inc(v.tensor_copy(S_ab[:, h * 128:(h + 1) * 128],
                                                  S_a[:, h * 128:(h + 1) * 128]))
                        Sbc3.wait(v, c + 1)
                        if c >= 4:
                            Smm5.wait(v, 16 * (c // 4))
                        for dch in range(16):
                            v.tensor_tensor(ntp[:, (dch % 2) * 128:(dch % 2) * 128 + 128],
                                            o_buf[:, dch * 128:(dch + 1) * 128],
                                            P[7][:, 0:128], ALU.mult)
                            Sno.inc(v.tensor_scalar(
                                onb[:, dch * 512 + (c % 4) * 128:
                                    dch * 512 + (c % 4) * 128 + 128],
                                ntp[:, (dch % 2) * 128:(dch % 2) * 128 + 128],
                                nwm_sb[:, dch:dch + 1], None, ALU.mult))
                        if c % 4 == 3:
                            for dc in range(16):
                                i = 16 * g + dc
                                Smm5.wait(v, i + 1)
                                Sxr.wdma(v, i)
                                if i >= 2:
                                    Sxw.wdma(v, i - 2)
                                Svr.inc(v.tensor_tensor(
                                    xout[:, (dc % 2) * 512:(dc % 2) * 512 + 512],
                                    xres[:, (dc % 2) * 512:(dc % 2) * 512 + 512],
                                    P[6 + dc % 2][:, 0:512], ALU.add))

                kb.phase(f"ach{b}", sync=f_sync, gpsimd=f_gp, tensor=f_tens,
                         scalar=f_scal, vector=f_vec)


        def router_phase(tag, b):
            with ExitStack() as ps:
                cw = ps.enter_context(nc.sbuf_tensor(f"cwk{tag}", [128, 12 * 16], F32))
                cwt = ps.enter_context(nc.sbuf_tensor(f"cwt{tag}", [128, 16 * 16], F32))
                cwT = ps.enter_context(nc.sbuf_tensor(f"cwT{tag}", [16, T], F32))
                LG, LGE, MX, NMX, SM_, CUR, MASK, MI, PM, DEN, RD = range(11)

                def sl(i, w=16):
                    return cw[:, i * 16:i * 16 + w]

                def col(i):
                    return cw[:, i * 16:i * 16 + 1]

                Sst = kb.dstrm()
                Smm, Sv1, Ss1, Sv2, Str, Sv3 = (kb.strm() for _ in range(6))

                def f_tens(t):
                    def transpose_of(tc):
                        Sv2.wait(t, tc + 1)
                        if tc >= 2:
                            Sv3.wait(t, tc - 1)
                        Str.inc(t.transpose(P[4 + tc % 2][0:16, 0:128],
                                             cwt[:, tc * 16:(tc + 1) * 16],
                                             ident[:, :]))

                    for tc in range(16):
                        if tc >= 2:
                            Sv1.wait(t, tc - 1)
                        for kch in range(NCH):
                            li = t.matmul(P[tc % 2][:, 0:16],
                                          xln[:, kch * T + tc * 128:
                                              kch * T + tc * 128 + 128],
                                          wrm_sb[:, kch * 16:(kch + 1) * 16],
                                          start=(kch == 0), stop=(kch == NCH - 1))
                        Smm.inc(li)
                        if tc >= 1:
                            transpose_of(tc - 1)
                    transpose_of(15)

                def f_scal(s):
                    for tc in range(16):
                        Sv1.wait(s, tc + 1)
                        Ss1.inc(s.activation(sl(LGE), sl(LG), AF.Exp,
                                             bias=col(NMX)))

                def f_vec(v):
                    Svx = kb.strm()
                    nch_ = [0]

                    def ch(inst, strm=None):
                        if strm is not None:
                            strm.inc(inst)
                            v.wait_ge(strm.s.h, strm.s.n)
                        else:
                            nch_[0] += 1
                            Svx.inc(inst)
                            Svx.wait(v, nch_[0])

                    for tc in range(16):
                        Smm.wait(v, tc + 1)
                        ch(v.tensor_copy(sl(LG), P[tc % 2][:, 0:16]))
                        ch(v.reduce_max(col(MX), sl(LG), axis=AX.X))
                        ch(v.tensor_scalar_mul(col(NMX), col(MX), -1.0), Sv1)
                        Ss1.wait(v, tc + 1)
                        ch(v.reduce_sum(col(SM_), sl(LGE), axis=AX.X))
                        ch(v.reciprocal(col(RD), col(SM_)))
                        ch(v.tensor_scalar(sl(LG), sl(LGE), col(RD), None, ALU.mult))
                        ch(v.tensor_copy(sl(CUR), sl(LG)))
                        ch(v.tensor_scalar_mul(sl(MASK), sl(LG), 0.0))
                        for _ in range(4):
                            ch(v.reduce_max(col(MX), sl(CUR), axis=AX.X))
                            ch(v.tensor_scalar(sl(MI), sl(CUR), col(MX), None,
                                               ALU.is_ge))
                            ch(v.tensor_add(sl(MASK), sl(MASK), sl(MI)))
                            ch(v.scalar_tensor_tensor(sl(CUR), sl(MI), -2.0,
                                                      sl(CUR), ALU.mult, ALU.add))
                        ch(v.tensor_mul(sl(PM), sl(LG), sl(MASK)))
                        ch(v.reduce_sum(col(DEN), sl(PM), axis=AX.X))
                        ch(v.reciprocal(col(RD), col(DEN)))
                        ch(v.tensor_scalar_mul(col(RD), col(RD), 0.8))
                        ch(v.tensor_scalar(cwt[:, tc * 16:(tc + 1) * 16],
                                           sl(PM), col(RD), None, ALU.mult), Sv2)
                        Str.wait(v, tc + 1)
                        Sv3.inc(v.tensor_copy(cwT[0:16, tc * 128:(tc + 1) * 128],
                                              P[4 + tc % 2][0:16, 0:128]))

                def f_sync(sy):
                    Sv3.wait(sy, 16)
                    Sst.dma(sy, cwd[:, :], cwT[:, :])
                    Sst.waitall(sy)

                kb.phase(f"rt{tag}", tensor=f_tens, scalar=f_scal, vector=f_vec,
                         sync=f_sync)

        def g_gen_phase(tag, shared):
            with ExitStack() as ps:
                tmp = ps.enter_context(nc.sbuf_tensor(f"tpg{tag}", [128, 4 * 512], F32))
                tmp2 = ps.enter_context(nc.sbuf_tensor(f"t2g{tag}", [128, 2 * 512], F32))
                osb = ps.enter_context(nc.sbuf_tensor(f"osg{tag}", [128, 4 * 512], BF16))
                cwb = None if shared else ps.enter_context(
                    nc.sbuf_tensor(f"cwb{tag}", [128, 4 * 512], F32))
                Swl, Sst, Srw = kb.dstrm(), kb.dstrm(), kb.dstrm()
                Smm, Ssil, Sev, Sev2 = (kb.strm() for _ in range(4))
                Sbc4, Scw = (kb.strm() for _ in range(2))
                if shared:
                    Sev2 = Sev
                npc = 8 if shared else 32
                w1n, w3n = ("ws1", "ws3") if shared else ("we1", "we3")

                def wsrc(wn, q):
                    wt = wbfd[wn]
                    if shared:
                        return wt, q * 512
                    e, p = q // 2, q % 2
                    return wt[e * D:(e + 1) * D, :], p * 512

                def dst_of(gi):
                    qq, tt = gi // 16, gi % 16
                    m4, nt = tt // 4, tt % 4
                    if shared:
                        return gs_sl[qq * 4 + m4][nt]
                    return ge_sl[qq // 2][(qq % 2) * 4 + m4][nt]

                def f_sync(sy):
                    sy.wait_ge(kb.aux.h, max(wc_done[w1n], wc_done[w3n]))
                    for q in range(npc):
                        if q >= 2:
                            Smm.wait(sy, 16 * (q - 1))
                        t1, c1 = wsrc(w1n, q)
                        t3, c3 = wsrc(w3n, q)
                        load_half(sy, Swl, t1, D, c1, 512, (2 * q) % 4)
                        load_half(sy, Swl, t3, D, c3, 512, (2 * q + 1) % 4)
                        if not shared and q % 2 == 0:
                            e = q // 2
                            for nt in range(NT):
                                i = e * NT + nt
                                if i >= 2:
                                    Sbc4.wait(sy, i - 1)
                                Srw.dma(sy, stage[0:1, (i % 2) * 512:
                                                  (i % 2) * 512 + 512],
                                        cwd[e:e + 1, nt * 512:(nt + 1) * 512])
                        for ti in range(16):
                            gi = q * 16 + ti
                            if gi >= 4:
                                Sev2.wait(sy, gi - 3)
                                Sst.dma(sy, dst_of(gi - 4),
                                        osb[:, ((gi - 4) % 4) * 512:
                                            ((gi - 4) % 4) * 512 + 512])
                    for gi in range(npc * 16 - 4, npc * 16):
                        Sev2.wait(sy, gi + 1)
                        Sst.dma(sy, dst_of(gi),
                                osb[:, (gi % 4) * 512:(gi % 4) * 512 + 512])
                    Sst.waitall(sy)
                    Swl.waitall(sy)
                    if not shared:
                        Srw.waitall(sy)

                def f_tens(t):
                    for q in range(npc):
                        Swl.wdma(t, 8 * (q + 1) - 1)
                        if not shared and q % 2 == 0:
                            e = q // 2
                            for nt in range(NT):
                                i = e * NT + nt
                                Srw.wdma(t, i)
                                if i >= 2:
                                    Scw.wait(t, i - 1)
                                Sbc4.inc(t.matmul(
                                    P[6 + i % 2][:, 0:512], ones_r_f[0:1, :],
                                    stage[0:1, (i % 2) * 512:(i % 2) * 512 + 512],
                                    start=True, stop=True))
                        for ti in range(16):
                            gi = q * 16 + ti
                            m4, nt = ti // 4, ti % 4
                            b1, b3 = gi % 3, 3 + gi % 3
                            if gi >= 3:
                                Ssil.wait(t, gi - 2)
                            for kch in range(NCH):
                                t.matmul(P[b1][:, 0:512],
                                         wtile((2 * q) % 4, kch, 512, m4 * 128),
                                         xln[:, kch * T + nt * 512:
                                             kch * T + nt * 512 + 512],
                                         start=(kch == 0), stop=(kch == NCH - 1))
                            if gi >= 3:
                                Sev.wait(t, gi - 2)
                            for kch in range(NCH):
                                li = t.matmul(P[b3][:, 0:512],
                                              wtile((2 * q + 1) % 4, kch, 512,
                                                    m4 * 128),
                                              xln[:, kch * T + nt * 512:
                                                  kch * T + nt * 512 + 512],
                                              start=(kch == 0),
                                              stop=(kch == NCH - 1))
                            Smm.inc(li)

                def f_scal(s):
                    for gi in range(npc * 16):
                        Smm.wait(s, gi + 1)
                        if gi >= 4:
                            Sev2.wait(s, gi - 3)
                        Ssil.inc(s.activation(tmp[:, (gi % 4) * 512:(gi % 4) * 512 + 512],
                                              P[gi % 3][:, 0:512], AF.Silu))

                def f_vec(v):
                    for gi in range(npc * 16):
                        q, ti = gi // 16, gi % 16
                        nt = ti % 4
                        b3 = 3 + gi % 3
                        if not shared and ti == 0 and q % 2 == 0:
                            e = q // 2
                            for i in range(e * NT, (e + 1) * NT):
                                Sbc4.wait(v, i + 1)
                                Scw.inc(v.tensor_copy(
                                    cwb[:, (i % 4) * 512:(i % 4) * 512 + 512],
                                    P[6 + i % 2][:, 0:512]))
                        Smm.wait(v, gi + 1)
                        if gi >= 4:
                            Sst.wdma(v, gi - 4)
                        if shared:
                            Ssil.wait(v, gi + 1)
                            li = v.scalar_tensor_tensor(
                                osb[:, (gi % 4) * 512:(gi % 4) * 512 + 512],
                                tmp[:, (gi % 4) * 512:(gi % 4) * 512 + 512], 0.2,
                                P[b3][:, 0:512], ALU.mult, ALU.mult)
                            Sev.inc(li)
                        else:
                            Sev.inc(v.tensor_tensor(
                                tmp2[:, (gi % 2) * 512:(gi % 2) * 512 + 512],
                                P[b3][:, 0:512],
                                cwb[:, nt * 512:(nt + 1) * 512], ALU.mult))
                            Ssil.wait(v, gi + 1)
                            Sev2.inc(v.tensor_tensor(
                                osb[:, (gi % 4) * 512:(gi % 4) * 512 + 512],
                                tmp[:, (gi % 4) * 512:(gi % 4) * 512 + 512],
                                tmp2[:, (gi % 2) * 512:(gi % 2) * 512 + 512],
                                ALU.mult))

                kb.phase(f"gg{tag}", sync=f_sync, tensor=f_tens, scalar=f_scal,
                         vector=f_vec)

        def f_out_phase(tag, b, dst):
            with ExitStack() as ps:
                gbuf = ps.enter_context(nc.sbuf_tensor(f"gb{tag}", [128, 4 * 512], BF16))
                xres = ps.enter_context(nc.sbuf_tensor(f"xrf{tag}", [128, 2 * 512], F32))
                xout = ps.enter_context(nc.sbuf_tensor(f"xof{tag}", [128, 2 * 512], F32))
                Swl, Sgl, Sxr, Sxw = (kb.dstrm() for _ in range(4))
                Smm, Sev = (kb.strm() for _ in range(2))
                WUP = 80

                def f_sync(sy):
                    sy.wait_ge(kb.aux.h, max(wc_done["ws2"], wc_done["we2"]))
                    for pi in range(8):
                        half, nt = pi // 4, pi % 4
                        mb = pi * 160
                        for qq in range(2):
                            if pi > 0:
                                Smm.wait(sy, (pi - 1) * 160 + (136 if qq == 0 else 152))
                            c0 = half * 1024 + qq * 512
                            load_half(sy, Swl, wbfd["ws2"][0:2048, :], 2048, c0,
                                      512, qq * 2)
                            load_half(sy, Swl, wbfd["ws2"][2048:4096, :], 2048, c0,
                                      512, qq * 2 + 1)
                        for j in range(160):
                            if j >= 4:
                                Smm.wait(sy, mb + j - 3)
                            if j < 32:
                                src_ = gs_sl[j][nt]
                            else:
                                e, kch = (j - 32) // 8, (j - 32) % 8
                                src_ = ge_sl[e][kch][nt]
                            Sgl.dma(sy, gbuf[:, (j % 4) * 512:(j % 4) * 512 + 512],
                                    src_)
                            if j >= 32 and (j - 32) % 8 == 0:
                                e = (j - 32) // 8 + 1
                                if e < NE:
                                    if e >= 5:
                                        Smm.wait(sy, mb + 32 + 8 * (e - 4))
                                    elif e >= 1:
                                        Smm.wait(sy, mb + 32)
                                    load_half(sy, Swl,
                                              wbfd["we2"][e * HID:(e + 1) * HID, :],
                                              HID, half * 1024, 1024, (4 + e) % 5)
                            if j == 0:
                                if pi > 0:
                                    Smm.wait(sy, pi * 160)
                                load_half(sy, Swl, wbfd["we2"][0:HID, :],
                                          HID, half * 1024, 1024, 4)
                        for dc8 in range(8):
                            i = pi * 8 + dc8
                            dch = half * 8 + dc8
                            if i >= 2:
                                Sev.wait(sy, i - 1)
                            Sxr.dma(sy, xres[:, (i % 2) * 512:(i % 2) * 512 + 512],
                                    xw[b, dch * 128:(dch + 1) * 128,
                                       nt * 512:(nt + 1) * 512])
                            if i >= 1:
                                Sev.wait(sy, i)
                                pi2, pd = (i - 1) // 8, (i - 1) % 8
                                pdch = (pi2 // 4) * 8 + pd
                                pnt = pi2 % 4
                                Sxw.dma(sy, dst[b, pdch * 128:(pdch + 1) * 128,
                                                pnt * 512:(pnt + 1) * 512],
                                        xout[:, ((i - 1) % 2) * 512:
                                             ((i - 1) % 2) * 512 + 512])
                    Sev.wait(sy, 64)
                    Sxw.dma(sy, dst[b, 15 * 128:16 * 128, 3 * 512:4 * 512],
                            xout[:, 512:1024])
                    Sxw.waitall(sy)
                    Sgl.waitall(sy)
                    Swl.waitall(sy)
                    Sxr.waitall(sy)

                def f_tens(t):
                    for pi in range(8):
                        half, nt = pi // 4, pi % 4
                        wb = WUP * pi
                        if pi >= 1:
                            Sev.wait(t, 8 * pi)
                        for j in range(160):
                            Sgl.wdma(t, pi * 160 + j)
                            if j < 32:
                                kch = j
                                Swl.wdma(t, wb + 15)
                                lhs = lambda dc8, kch=kch: wtile(
                                    (dc8 // 4) * 2 + (0 if kch < 16 else 1),
                                    kch % 16, 512, (dc8 % 4) * 128)
                                dcs = range(8)
                            else:
                                e, kch = (j - 32) // 8, (j - 32) % 8
                                Swl.wdma(t, wb + 16 + 4 * (e + 1) - 1)
                                lhs = lambda dc8, e=e, kch=kch: wtile(
                                    (4 + e) % 5, kch, 1024, dc8 * 128)
                                dcs = range(8)
                            for dc8 in dcs:
                                li = t.matmul(P[dc8][:, 0:512], lhs(dc8),
                                              gbuf[:, (j % 4) * 512:
                                                  (j % 4) * 512 + 512],
                                              start=(j == 0), stop=(j == 159))
                            Smm.inc(li)

                def f_vec(v):
                    for dc8 in range(8):
                        v.memset(P[dc8][:, 0:512], 0.0)
                    for pi in range(8):
                        for dc8 in range(8):
                            i = pi * 8 + dc8
                            Smm.wait(v, pi * 160 + 160)
                            Sxr.wdma(v, i)
                            if i >= 2:
                                Sxw.wdma(v, i - 2)
                            Sev.inc(v.tensor_tensor(
                                xout[:, (dc8 % 2) * 512:(dc8 % 2) * 512 + 512],
                                xres[:, (dc8 % 2) * 512:(dc8 % 2) * 512 + 512],
                                P[dc8][:, 0:512], ALU.add))

                kb.phase(f"fo{tag}", sync=f_sync, tensor=f_tens, vector=f_vec)


        def b_chunk_phase(b):
            with ExitStack() as ps:
                abf = xarena
                qf = abf[:, 0:2048]
                kf = abf[:, 2048:4096]
                ktm = abf[:, 4096:6144]
                vtm = abf[:, 6144:8192]
                scb = abf[:, 8192:10240]
                kksb = abf[:, 10240:11264]
                qdb = abf[:, 11264:12288]
                S_bb = abf[:, 12288:16384]
                wfree = wbuf[:, 0:40960].bitcast(F32)
                dtb = wfree[:, 0:4096]
                csb2 = wfree[:, 4096:8192]
                S_b = wfree[:, 8192:12288]
                xres = ps.enter_context(nc.sbuf_tensor(f"xrb{b}", [128, 2 * 512], F32))
                xout = ps.enter_context(nc.sbuf_tensor(f"xob{b}", [128, 2 * 512], F32))
                idx = np.arange(CB, dtype=np.float64)
                betas = np.linspace(np.log2(1.0 / 32.0), np.log2(1.0 / 512.0), HB)
                gdec = 1.0 - np.exp2(betas)
                cdv = [float((gdec[h] ** CB)) for h in range(HB)]

                Sld, Sxr, Sxw, Scr = (kb.dstrm() for _ in range(4))
                Sbc5, Scs, Sz = (kb.strm() for _ in range(3))
                Smm1, Svd, Smm2, Svo, Smm3, Svu, Sqk = (kb.strm() for _ in range(7))

                def f_gp(g_):
                    g_.memset(S_b[:, :], 0.0)
                    Sz.inc(g_.memset(S_bb[:, :], 0.0))

                def f_sync(sy):
                    for hh in range(HB):
                        if hh >= 2:
                            Sbc5.wait(sy, hh - 1)
                        Scr.dma(sy, stage[0:1, (hh % 2) * 512:(hh % 2) * 512 + 512],
                                csr[0:1, hh * 512:(hh + 1) * 512])
                    for ch in range(NCB * HB):
                        c, h = ch // HB, ch % HB
                        r = ch % 2
                        if ch >= 2:
                            Svu.wait(sy, 2 * (ch - 1))
                        for es_ in range(2):
                            Sld.dma(sy, qf[:, (r * 2 + es_) * 512:(r * 2 + es_) * 512 + 512],
                                    sA[2 * h + es_][c])
                            Sld.dma(sy, kf[:, (r * 2 + es_) * 512:(r * 2 + es_) * 512 + 512],
                                    sB[2 * h + es_][c])
                        for jt in range(4):
                            Sld.dma(sy, ktm[:, r * 1024 + jt * 256:r * 1024 + jt * 256 + 256],
                                    sC[4 * c + jt][h // 2][:, (h % 2) * 256:(h % 2) * 256 + 256])
                            Sld.dma(sy, vtm[:, r * 1024 + jt * 256:r * 1024 + jt * 256 + 256],
                                    sD[4 * c + jt][h // 2][:, (h % 2) * 256:(h % 2) * 256 + 256])
                            Sld.dma(sy, dtb[:, r * 2048 + jt * 512:r * 2048 + jt * 512 + 512],
                                    dbTm[h][jt * 128:(jt + 1) * 128, :])
                        for fs in range(2):
                            i = ch * 2 + fs
                            dch = 2 * h + fs
                            if i >= 2:
                                Svo.wait(sy, i - 1)
                            Sxr.dma(sy, xres[:, (i % 2) * 512:(i % 2) * 512 + 512],
                                    xw[b, dch * 128:(dch + 1) * 128,
                                       c * 512:(c + 1) * 512])
                            if i >= 1:
                                Svo.wait(sy, i)
                                pch, pfs = (i - 1) // 2, (i - 1) % 2
                                pc, ph = pch // HB, pch % HB
                                pdch = 2 * ph + pfs
                                Sxw.dma(sy, xw[b, pdch * 128:(pdch + 1) * 128,
                                               pc * 512:(pc + 1) * 512],
                                        xout[:, ((i - 1) % 2) * 512:
                                             ((i - 1) % 2) * 512 + 512])
                    Svo.wait(sy, 64)
                    Sxw.dma(sy, xw[b, 15 * 128:16 * 128, 3 * 512:4 * 512],
                            xout[:, 512:1024])
                    Sxw.waitall(sy)
                    Sld.waitall(sy)
                    Scr.waitall(sy)
                    Sxr.waitall(sy)

                def f_tens(t):
                    Sz.wait(t, 1)
                    for hh in range(HB):
                        Scr.wdma(t, hh)
                        if hh >= 2:
                            Scs.wait(t, hh - 1)
                        Sbc5.inc(t.matmul(
                            P[6 + hh % 2][:, 0:512], ones_r_f[0:1, :],
                            stage[0:1, (hh % 2) * 512:(hh % 2) * 512 + 512],
                            start=True, stop=True))
                    for ch in range(NCB * HB):
                        c, h = ch // HB, ch % HB
                        r = ch % 2
                        Sld.wdma(t, 16 * (ch + 1) - 1)
                        Sqk.wait(t, 6 * (ch + 1))
                        if ch >= 1:
                            Svd.wait(t, 4 * ch)
                        for jt in range(4):
                            for es_ in range(2):
                                li = t.matmul(P[jt][:, 0:512],
                                              kf[:, (r * 2 + es_) * 512 + jt * 128:
                                                 (r * 2 + es_) * 512 + jt * 128 + 128],
                                              qf[:, (r * 2 + es_) * 512:
                                                 (r * 2 + es_) * 512 + 512],
                                              start=(es_ == 0), stop=(es_ == 1))
                            Smm1.inc(li)
                        Svd.wait(t, 4 * (ch + 1))
                        if c >= 1:
                            Svu.wait(t, 2 * ((c - 1) * HB + h + 1))
                        if ch >= 1:
                            Svo.wait(t, 2 * ch)
                        for fs in range(2):
                            for jt in range(4):
                                t.matmul(P[4 + fs][:, 0:512],
                                         vtm[:, r * 1024 + jt * 256 + fs * 128:
                                             r * 1024 + jt * 256 + fs * 128 + 128],
                                         scb[:, jt * 512:(jt + 1) * 512],
                                         start=(jt == 0), stop=False)
                            for es_ in range(2):
                                li = t.matmul(P[4 + fs][:, 0:512],
                                              S_bb[:, (h * 2 + es_) * 256 + fs * 128:
                                                   (h * 2 + es_) * 256 + fs * 128 + 128],
                                              qdb[:, es_ * 512:(es_ + 1) * 512],
                                              start=False, stop=(es_ == 1))
                            Smm2.inc(li)
                        if ch >= 1:
                            Svu.wait(t, 2 * ch)
                        for es_ in range(2):
                            for jt in range(4):
                                li = t.matmul(P[6 + es_][:, 0:256],
                                              kksb[:, jt * 256 + es_ * 128:
                                                   jt * 256 + es_ * 128 + 128],
                                              vtm[:, r * 1024 + jt * 256:
                                                  r * 1024 + jt * 256 + 256],
                                              start=(jt == 0), stop=(jt == 3))
                            Smm3.inc(li)

                def f_vec(v):
                    Sz.wait(v, 1)
                    for hh in range(HB):
                        Sbc5.wait(v, hh + 1)
                        Scs.inc(v.tensor_copy(csb2[:, hh * 512:(hh + 1) * 512],
                                              P[6 + hh % 2][:, 0:512]))
                    for ch in range(NCB * HB):
                        c, h = ch // HB, ch % HB
                        r = ch % 2
                        Sld.wdma(v, 16 * (ch + 1) - 1)
                        if ch >= 1:
                            Smm2.wait(v, 2 * ch)
                            Smm3.wait(v, 2 * ch)
                        for es_ in range(2):
                            v.tensor_tensor(qdb[:, es_ * 512:(es_ + 1) * 512],
                                            qf[:, (r * 2 + es_) * 512:
                                               (r * 2 + es_) * 512 + 512],
                                            csb2[:, h * 512:(h + 1) * 512], ALU.mult)
                        for jt in range(4):
                            li = v.tensor_scalar(kksb[:, jt * 256:(jt + 1) * 256],
                                                 ktm[:, r * 1024 + jt * 256:
                                                     r * 1024 + jt * 256 + 256],
                                                 ksbt_sb[:, jt * 8 + h:jt * 8 + h + 1],
                                                 None, ALU.mult)
                        Sqk.inc(li, 6)
                        for jt in range(4):
                            Smm1.wait(v, 4 * ch + jt + 1)
                            Svd.inc(v.tensor_tensor(
                                scb[:, jt * 512:(jt + 1) * 512], P[jt][:, 0:512],
                                dtb[:, r * 2048 + jt * 512:r * 2048 + jt * 512 + 512],
                                ALU.mult))
                        for fs in range(2):
                            i = ch * 2 + fs
                            Smm2.wait(v, 2 * ch + fs + 1)
                            Sxr.wdma(v, i)
                            if i >= 2:
                                Sxw.wdma(v, i - 2)
                            Svo.inc(v.tensor_tensor(
                                xout[:, (fs % 2) * 512:(fs % 2) * 512 + 512],
                                xres[:, (i % 2) * 512:(i % 2) * 512 + 512],
                                P[4 + fs][:, 0:512], ALU.add))
                        for es_ in range(2):
                            Smm3.wait(v, 2 * ch + es_ + 1)
                            o = (h * 2 + es_) * 256
                            v.scalar_tensor_tensor(S_b[:, o:o + 256], S_b[:, o:o + 256],
                                                   cdv[h], P[6 + es_][:, 0:256],
                                                   ALU.mult, ALU.add)
                            Svu.inc(v.tensor_copy(S_bb[:, o:o + 256], S_b[:, o:o + 256]))

                kb.phase(f"bch{b}", gpsimd=f_gp, sync=f_sync, tensor=f_tens,
                         vector=f_vec)

        def finish(src_for_out=None):
            def f_g(g):
                a = Strm(kb.aux)
                if src_for_out is not None:
                    fs = src_for_out.rearrange("a b c -> (a b c)")
                    fd = out.rearrange("a b c -> (a b c)")
                    n = 2 * D * T
                    CH = 1 << 20
                    for o in range(0, n, CH):
                        a.dma_batched(g, fd[o:min(n, o + CH)],
                                      fs[o:min(n, o + CH)])
                if dbg is not None:
                    dsrc = {"xw": xw, "sA": sA, "sB": sB, "sC": sC, "sD": sD,
                            "sE": sE, "facd": facd, "cwd": cwd, "ge": ge_sl,
                            "gs": gs_sl, "xlnd": xlnd}[dump]
                    nel = int(np.prod(dbg.shape))
                    r = {1: "a -> (a)", 2: "a b -> (a b)", 3: "a b c -> (a b c)",
                         4: "a b c d -> (a b c d)", 5: "a b c d e -> (a b c d e)"}
                    fs = dsrc.rearrange(r[len(dsrc.shape)])
                    fd = dbg.rearrange(r[len(dbg.shape)])
                    CH = 1 << 20
                    for o in range(0, nel, CH):
                        a.dma_batched(g, fd[o:min(nel, o + CH)],
                                      fs[o:min(nel, o + CH)])
                g.wait_ge(a.s.h, a.s.n)
            kb.phase("finish", gpsimd=f_g)

        def dump_xln():
            if dump == "xlnd":
                def f_g(g):
                    a = Strm(kb.aux)
                    a.dma(g, xlnd[:, :], xln[:, :])
                    g.wait_ge(a.s.h, a.s.n)
                kb.phase("dumpxln", gpsimd=f_g)

        seq = []

        def step(name, fn):
            seq.append((name, fn))

        for b in range(2):
            step(f"aln{b}", lambda b=b: ln_phase(f"a{b}", b, xin))
            step(f"axd{b}", lambda b=b: dump_xln())
            step(f"ald{b}", lambda b=b: a_ld_phase(b))
            step(f"aqd{b}", lambda b=b: proj_fm(f"qd{b}", "wq", "qd2", [sA, sB]))
            step(f"akd{b}", lambda b=b: proj_fm(f"kd{b}", "wk", "kd", [sC]))
            step(f"akr{b}", lambda b=b: proj_tm(f"kr{b}", "wk", "kdr", sD))
            step(f"avt{b}", lambda b=b: proj_tm(f"vt{b}", "wv", "vtm", sE))
            step(f"ach{b}", lambda b=b: a_chunk_phase(b))
            for r in range(2):
                step(f"fln{b}{r}", lambda b=b, r=r: ln_phase(f"f{b}{r}", b, xw))
                step(f"frt{b}{r}", lambda b=b, r=r: router_phase(f"{b}{r}", b))
                step(f"fgs{b}{r}", lambda b=b, r=r: g_gen_phase(f"s{b}{r}", True))
                step(f"fge{b}{r}", lambda b=b, r=r: g_gen_phase(f"e{b}{r}", False))
                step(f"fo3{b}{r}", lambda b=b, r=r: f_out_phase(
                    f"{b}{r}", b, xw if r == 0 else out))
                if r == 0:
                    step(f"bln{b}", lambda b=b: ln_phase(f"b{b}", b, xw))
                    step(f"bqf{b}", lambda b=b: proj_fm(f"bq{b}", "bwq", "biasq", [sA]))
                    step(f"bkf{b}", lambda b=b: proj_fm(f"bk{b}", "bwk", "biask", [sB]))
                    step(f"bkt{b}", lambda b=b: proj_tm(f"bkt{b}", "bwk", "btm", sC, 0))
                    step(f"bvt{b}", lambda b=b: proj_tm(f"bvt{b}", "bwv", "btm", sD, 1))
                    step(f"bch{b}", lambda b=b: b_chunk_phase(b))

        if stop_after == "w0":
            seq = []
        ran = []
        for name, fn in seq:
            fn()
            ran.append(name)
            if stop_after is not None and name == stop_after:
                break
        finish(src_for_out=xw if stop_after is not None else None)
    return nc


def _consts():
    idx = np.arange(CB, dtype=np.float64)
    betas = np.linspace(np.log2(1.0 / 32.0), np.log2(1.0 / 512.0), HB)
    g = (1.0 - np.exp2(betas))
    Dm = np.where(idx[:, None] >= idx[None, :],
                  g[:, None, None] ** (idx[:, None] - idx[None, :]), 0.0)
    dbT = np.ascontiguousarray(Dm.transpose(0, 2, 1)).astype(np.float32)
    csr = (g[:, None] ** (idx + 1)).astype(np.float32).reshape(1, -1)
    ks = (g[:, None] ** (CB - 1 - idx)).astype(np.float32)
    ksbt = np.zeros((128, 32), np.float32)
    for js in range(4):
        for h in range(HB):
            ksbt[:, js * 8 + h] = ks[h, js * 128:(js + 1) * 128]
    cd = (g ** CB).astype(np.float32)
    return dbT, csr, ksbt, cd


def prep_core_inputs(inputs, core):
    b0 = core * 2
    x = np.ascontiguousarray(
        np.transpose(inputs["L0_attn_a_x"][b0:b0 + 2], (0, 2, 1))).astype(np.float32)
    dbT, csr, ksbt, _cd = _consts()
    wfmv = np.ascontiguousarray(
        inputs["L0_attn_a_Wf"].reshape(16, 128, 16).transpose(1, 0, 2).reshape(128, 256))
    wrmv = np.ascontiguousarray(
        inputs["L0_ffn_w_router"].reshape(16, 128, 16).transpose(1, 0, 2).reshape(128, 256))
    normwm = np.ascontiguousarray(inputs["L0_attn_a_norm_w"].reshape(16, 128).T)
    biasm = np.stack([np.ascontiguousarray(inputs["L0_attn_b_bq"].reshape(16, 128).T),
                      np.ascontiguousarray(inputs["L0_attn_b_bk"].reshape(16, 128).T)])
    biasrow = np.stack([inputs["L0_attn_b_bk"], inputs["L0_attn_b_bv"]]).astype(np.float32)
    im = {
        "x": x,
        "wq": inputs["L0_attn_a_Wq"], "wk": inputs["L0_attn_a_Wk"],
        "wv": inputs["L0_attn_a_Wv"], "wo": inputs["L0_attn_a_Wo"],
        "bwq": inputs["L0_attn_b_Wq"], "bwk": inputs["L0_attn_b_Wk"],
        "bwv": inputs["L0_attn_b_Wv"],
        "we1": inputs["L0_ffn_we1"].reshape(NE * D, HID),
        "we3": inputs["L0_ffn_we3"].reshape(NE * D, HID),
        "we2": inputs["L0_ffn_we2"].reshape(NE * HID, D),
        "ws1": inputs["L0_ffn_ws1"], "ws3": inputs["L0_ffn_ws3"],
        "ws2": inputs["L0_ffn_ws2"],
        "wfm": wfmv, "wrm": wrmv, "normwm": normwm,
        "biasm": biasm.astype(np.float32), "biasrow": biasrow,
        "identm": np.eye(128, dtype=np.float32),
        "dbTm": dbT, "csr": csr, "ksbtm": ksbt,
    }
    return {k: np.ascontiguousarray(np.asarray(v, dtype=v.dtype if hasattr(v, "dtype") else np.float32))
            for k, v in im.items()}


def kernel(**inputs):
    nc = build_program()
    in_maps = [prep_core_inputs(inputs, c) for c in range(8)]
    res = run_bass_kernel_spmd(nc, in_maps, core_ids=list(range(8)))
    outs = []
    for c in range(8):
        o = res.results[c]["out"]
        outs.append(np.transpose(o, (0, 2, 1)))
    return np.ascontiguousarray(np.concatenate(outs, axis=0))


# version 07 (final)
# speedup vs baseline: 1.0602x; 1.0602x over previous
import numpy as np
from contextlib import ExitStack

import concourse.bass as bass
import concourse.mybir as mybir
from concourse.bass_utils import run_bass_kernel_spmd

F32 = mybir.dt.float32
BF16 = mybir.dt.bfloat16
AF = mybir.ActivationFunctionType
ALU = mybir.AluOpType
AX = mybir.AxisListType

ENGS = ("tensor", "vector", "scalar", "gpsimd", "sync")

D, T = 2048, 2048
NCH, NT = 16, 4
CA, NCA, HA = 128, 16, 16
CB, NCB, HB = 512, 4, 8
NE, HID, HS = 16, 1024, 4096

CPV = [1e-5, 1e-6, -0.5, 1.0, -1.0, 0.0, 0.2, 1.0 / 2048.0]
CP = {v: i for i, v in enumerate(CPV)}


class Sem:
    __slots__ = ("h", "n")

    def __init__(self, h):
        self.h = h
        self.n = 0


class Strm:

    def __init__(self, sems):
        if not isinstance(sems, list):
            sems = [sems]
        self.sems = sems
        self.bases = [s.n for s in sems]
        self.s = sems[0]
        self.base = self.bases[0]
        self.k = 0
        self.cnt = [0] * len(sems)

    def inc(self, inst, k=1):
        inst.then_inc(self.s.h, k)
        self.s.n += k

    def dma(self, q, dst, src):
        n = len(self.sems)
        p = self.k % n
        sem = self.sems[p]
        if n > 1 and self.cnt[p] >= 1:
            q.wait_ge(sem.h, self.bases[p] + 16 * self.cnt[p])
        q.dma_start(dst, src).then_inc(sem.h, 16)
        sem.n += 16
        self.cnt[p] += 1
        self.k += 1

    def dma_batched(self, q, dst, src, depth=8):
        if (self.s.n - self.base) // 16 % depth == 0 and self.s.n > self.base:
            q.wait_ge(self.s.h, self.s.n)
        q.dma_start(dst, src).then_inc(self.s.h, 16)
        self.s.n += 16
        self.k += 1

    def drain(self, q):
        q.wait_ge(self.s.h, self.s.n)

    def wdma(self, eng, j):
        n = len(self.sems)
        for p in range(n):
            c = (j - p) // n + 1 if j >= p else 0
            if c > 0:
                eng.wait_ge(self.sems[p].h, self.bases[p] + 16 * c)

    def waitall(self, eng):
        for s in self.sems:
            eng.wait_ge(s.h, s.n)

    def wait(self, eng, i):
        eng.wait_ge(self.s.h, self.base + i)


class KB:
    def __init__(self, nc, es):
        self.nc = nc
        self.es = es
        self.pi = 0
        self.pb = Sem(es.enter_context(nc.semaphore("pb")))
        self.aux = Sem(es.enter_context(nc.semaphore("aux")))
        self.pool = [Sem(es.enter_context(nc.semaphore(f"g{i}")))
                     for i in range(56)]
        self.pooli = 0

    def strm(self):
        s = Strm(self.pool[self.pooli % len(self.pool)])
        self.pooli += 1
        return s

    def dstrm(self, n=2):
        sems = []
        for _ in range(n):
            sems.append(self.pool[self.pooli % len(self.pool)])
            self.pooli += 1
        return Strm(sems)

    def qstrm(self):
        return self.dstrm(4)

    def sb(self, name, shape, dt):
        return self.es.enter_context(self.nc.sbuf_tensor(name, shape, dt))

    def phase(self, name, **bodies):
        i = self.pi
        pb = self.pb
        with self.nc.Block(f"ph{i}_{name}") as blk:
            for en in ENGS:
                fn = bodies.get(en)
                pre = bodies.get("pre_" + en)

                def mk(en=en, fn=fn, pre=pre):
                    def body(eng):
                        if pre is not None:
                            pre(eng)
                        eng.wait_ge(pb.h, 5 * i)
                        if fn is not None:
                            fn(eng)
                        eng.drain()
                        eng.sem_inc(pb.h, 1)
                    return body
                getattr(blk, en)(mk())
        pb.n += 5
        self.pi += 1


def build_program(stop_after=None, dump=None):
    nc = bass.Bass("TRN2", target_bir_lowering=False)

    xin = nc.dram_tensor("x", [2, D, T], F32, kind="ExternalInput").ap()
    out = nc.dram_tensor("out", [2, D, T], F32, kind="ExternalOutput").ap()
    xw = nc.dram_tensor("xw", [2, D, T], F32, kind="Internal").ap()

    wshapes = {
        "wq": [D, D], "wk": [D, D], "wv": [D, D], "wo": [D, D],
        "bwq": [D, D], "bwk": [D, D], "bwv": [D, D],
        "we1": [NE * D, HID], "we3": [NE * D, HID], "we2": [NE * HID, D],
        "ws1": [D, HS], "ws3": [D, HS], "ws2": [HS, D],
    }
    wf32 = {k: nc.dram_tensor(k, s, F32, kind="ExternalInput").ap()
            for k, s in wshapes.items()}
    wbfd = {k: nc.dram_tensor(k + "_bf", s, BF16, kind="Internal").ap()
            for k, s in wshapes.items()}

    wfm = nc.dram_tensor("wfm", [128, 256], F32, kind="ExternalInput").ap()
    wrm = nc.dram_tensor("wrm", [128, 256], F32, kind="ExternalInput").ap()
    normwm = nc.dram_tensor("normwm", [128, 16], F32, kind="ExternalInput").ap()
    biasm = nc.dram_tensor("biasm", [2, 128, 16], F32, kind="ExternalInput").ap()
    biasrow = nc.dram_tensor("biasrow", [2, D], F32, kind="ExternalInput").ap()
    identm = nc.dram_tensor("identm", [128, 128], F32, kind="ExternalInput").ap()
    dbTm = nc.dram_tensor("dbTm", [HB, CB, CB], F32, kind="ExternalInput").ap()
    csr = nc.dram_tensor("csr", [1, HB * CB], F32, kind="ExternalInput").ap()
    ksbtm = nc.dram_tensor("ksbtm", [128, 32], F32, kind="ExternalInput").ap()

    def mkslab(name):
        return nc.dram_tensor(name, [NCH, NT, 128, 512], BF16, kind="Internal").ap()
    sA, sB, sC, sD, sE = (mkslab(n) for n in ["sA", "sB", "sC", "sD", "sE"])
    ge_sl = nc.dram_tensor("ge", [NE, NT, 8, 128, 512], BF16, kind="Internal").ap()
    gs_sl = nc.dram_tensor("gs", [NT, 32, 128, 512], BF16, kind="Internal").ap()
    facd = nc.dram_tensor("facd", [3, HA, T], F32, kind="Internal").ap()
    rowd = nc.dram_tensor("rowd", [16, 16], F32, kind="Internal").ap()
    cwd = nc.dram_tensor("cwd", [NE, T], F32, kind="Internal").ap()
    xlnd = nc.dram_tensor("xlnd", [128, NCH * T], BF16, kind="Internal").ap()

    dbg = None
    if dump is not None:
        dshape = {"xw": [2 * D, T], "sA": [NCH * NT * 128, 512],
                  "sB": [NCH * NT * 128, 512], "sC": [NCH * NT * 128, 512],
                  "sD": [NCH * NT * 128, 512], "sE": [NCH * NT * 128, 512],
                  "facd": [3 * HA, T], "cwd": [NE, T], "xlnd": [128, NCH * T],
                  "ge": [NE * NT * 8 * 128, 512], "gs": [NT * 32 * 128, 512]}[dump]
        dbg = nc.dram_tensor("dbg", dshape, F32, kind="ExternalOutput").ap()

    es = ExitStack()
    with es:
        kb = KB(nc, es)
        P = [es.enter_context(nc.psum_tensor(f"pk{i}", [128, 512], F32))
             for i in range(8)]

        xarena = kb.sb("xarena", [128, NCH * T], BF16)
        xln = xarena
        wbuf = kb.sb("wbuf", [128, 5 * 8192], BF16)
        ones_r_bf = kb.sb("ones_r_bf", [1, 128], BF16)
        ones_r_f = kb.sb("ones_r_f", [1, 128], F32)
        ones_c_bf = kb.sb("ones_c_bf", [128, 1], BF16)
        cpool = kb.sb("cpool", [128, 8], F32)
        ident = kb.sb("ident", [128, 128], F32)
        stage = kb.sb("stage", [1, 2 * 512], F32)
        nwm_sb = kb.sb("nwm_sb", [128, 16], F32)
        bqm_sb = kb.sb("bqm_sb", [128, 32], F32)
        brow_sb = kb.sb("brow_sb", [1, 2 * D], BF16)
        wfm_sb = kb.sb("wfm_sb", [128, 256], BF16)
        wrm_sb = kb.sb("wrm_sb", [128, 256], BF16)
        smal = kb.sb("smal", [16, 64], F32)
        smalT = kb.sb("smalT", [16, 16], F32)
        zz16 = kb.sb("zz16", [16, 128], F32)
        fkr_tm = kb.sb("fkr_tm", [128, NCA * 16], F32)
        etot_b = kb.sb("etot_b", [128, NCA * 16], F32)
        S_a = kb.sb("S_a", [128, HA * 128], F32)
        S_ab = kb.sb("S_ab", [128, HA * 128], BF16)
        ksbt_sb = kb.sb("ksbt_sb", [128, 32], F32)
        SM_S, SM_TOT, SM_ETOT, SM_NEGS = 0, 16, 32, 48

        def init_g(g):
            g.memset(ones_r_bf[0:1, :], 1.0)
            g.memset(ones_r_f[0:1, :], 1.0)
            g.memset(ones_c_bf[:, 0:1], 1.0)
            g.memset(zz16[:, :], 0.0)
            for v, i in CP.items():
                g.memset(cpool[:, i:i + 1], float(v))
            a = Strm(kb.aux)
            a.dma(g, ident[:, :], identm[:, :])
            a.dma(g, nwm_sb[:, :], normwm[:, :])
            a.dma(g, bqm_sb[:, 0:16], biasm[0])
            a.dma(g, bqm_sb[:, 16:32], biasm[1])
            a.dma(g, brow_sb[0:1, :], biasrow.rearrange("a b -> (a b)"))
            a.dma(g, wfm_sb[:, :], wfm[:, :])
            a.dma(g, wrm_sb[:, :], wrm[:, :])
            a.dma(g, ksbt_sb[:, :], ksbtm[:, :])
            g.wait_ge(a.s.h, a.s.n)

        kb.phase("init", gpsimd=init_g)

        wc_done = {}

        def w0_g(g):
            a = Strm(kb.aux)
            for k in ["wq", "wk", "wv", "wo", "ws1", "ws3", "we1", "we3",
                      "we2", "ws2", "bwq", "bwk", "bwv"]:
                src, dst = wf32[k], wbfd[k]
                n = int(np.prod(wshapes[k]))
                fs, fd = src.rearrange("a b -> (a b)"), dst.rearrange("a b -> (a b)")
                CH = 1 << 20
                for o in range(0, n, CH):
                    a.dma_batched(g, fd[o:min(n, o + CH)], fs[o:min(n, o + CH)])
                a.drain(g)
                wc_done[k] = kb.aux.n

        kb.phase("w0", gpsimd=w0_g)

        SLOTW = 8192

        def load_half(q, st, wt, kdim, c0, colw, slot):
            kch = kdim // 128
            assert kch * colw <= SLOTW
            ap = wt[:, c0:c0 + colw].rearrange("(k p) n -> p k n", p=128)
            step = max(1, kch // 4)
            for ks in range(0, kch, step):
                ke = min(kch, ks + step)
                st.dma(q, wbuf[:, slot * SLOTW + ks * colw:slot * SLOTW + ke * colw],
                       ap[:, ks:ke, :])

        def wtile(slot, kchi, colw, c0, w=128):
            o = slot * SLOTW + kchi * colw + c0
            return wbuf[:, o:o + w]

        def ln_phase(tag, b, src):
            with ExitStack() as ps:
                xtile = ps.enter_context(nc.sbuf_tensor(f"xt{tag}", [128, 2 * 512], F32))
                ntmp = ps.enter_context(nc.sbuf_tensor(f"ntp{tag}", [128, 512], F32))
                xbf = wbuf[:, 2 * SLOTW:2 * SLOTW + 8192]
                x2 = wbuf[:, 3 * SLOTW:3 * SLOTW + 8192]
                srow = ps.enter_context(nc.sbuf_tensor(f"srw{tag}", [1, 4 * 512], F32))
                Sx = kb.qstrm()
                Scast, Ssq, Smm, Ssrow, Svar, Srstd, Snb, Sbc, Snorm = (
                    kb.strm() for _ in range(9))

                def f_sync(sy):
                    for nt in range(NT):
                        for dch in range(NCH):
                            i = nt * NCH + dch
                            if i >= 2:
                                Scast.wait(sy, i - 1)
                                Ssq.wait(sy, i - 1)
                            Sx.dma(sy, xtile[:, (i % 2) * 512:(i % 2) * 512 + 512],
                                   src[b, dch * 128:(dch + 1) * 128,
                                       nt * 512:(nt + 1) * 512])

                def f_vec(v):
                    for nt in range(NT):
                        for dch in range(NCH):
                            i = nt * NCH + dch
                            Sx.wdma(v, i)
                            sl = slice((i % 2) * 512, (i % 2) * 512 + 512)
                            Scast.inc(v.tensor_copy(
                                xbf[:, dch * 512:(dch + 1) * 512], xtile[:, sl]))
                        Ssrow.wait(v, nt + 1)
                        Svar.inc(v.tensor_sub(srow[0:1, 512:1024],
                                              srow[0:1, 512:1024],
                                              srow[0:1, 2 * 512:3 * 512]))
                        Srstd.wait(v, nt + 1)
                        Snb.inc(v.scalar_tensor_tensor(
                            srow[0:1, 3 * 512:4 * 512], srow[0:1, 0:512], -1.0,
                            srow[0:1, 2 * 512:3 * 512], ALU.mult, ALU.mult))
                        Sbc.wait(v, nt + 1)
                        for dch in range(NCH):
                            v.tensor_tensor(ntmp[:, 0:512],
                                            xbf[:, dch * 512:(dch + 1) * 512],
                                            P[6][:, 0:512], ALU.mult)
                            o = dch * T + nt * 512
                            li = v.tensor_tensor(xln[:, o:o + 512], ntmp[:, 0:512],
                                                 P[7][:, 0:512], ALU.add)
                        Snorm.inc(li)

                def f_scal(s):
                    for nt in range(NT):
                        for dch in range(NCH):
                            i = nt * NCH + dch
                            Sx.wdma(s, i)
                            sl = slice((i % 2) * 512, (i % 2) * 512 + 512)
                            Ssq.inc(s.activation(x2[:, dch * 512:(dch + 1) * 512],
                                                 xtile[:, sl], AF.Square))
                        Smm.wait(s, 2 * (nt + 1))
                        if nt > 0:
                            Snb.wait(s, nt)
                            Sbc.wait(s, nt)
                        s.mul(srow[0:1, 0:512], P[4][0:1, 0:512], 1.0 / 2048.0)
                        s.mul(srow[0:1, 512:1024], P[5][0:1, 0:512], 1.0 / 2048.0)
                        Ssrow.inc(s.activation(srow[0:1, 2 * 512:3 * 512],
                                               srow[0:1, 0:512], AF.Square))
                        Svar.wait(s, nt + 1)
                        s.activation(srow[0:1, 2 * 512:3 * 512],
                                     srow[0:1, 512:1024], AF.Ln,
                                     bias=cpool[0:1, CP[1e-5]:CP[1e-5] + 1])
                        Srstd.inc(s.activation(
                            srow[0:1, 2 * 512:3 * 512], srow[0:1, 2 * 512:3 * 512],
                            AF.Exp, scale=cpool[0:1, CP[-0.5]:CP[-0.5] + 1]))

                def f_tens(t):
                    for nt in range(NT):
                        Scast.wait(t, NCH * (nt + 1))
                        Ssq.wait(t, NCH * (nt + 1))
                        if nt > 0:
                            Ssrow.wait(t, nt)
                        for dch in range(NCH):
                            li = t.matmul(P[4][0:1, 0:512], ones_c_bf[:, 0:1],
                                          xbf[:, dch * 512:(dch + 1) * 512],
                                          start=(dch == 0), stop=(dch == NCH - 1))
                        Smm.inc(li)
                        for dch in range(NCH):
                            li = t.matmul(P[5][0:1, 0:512], ones_c_bf[:, 0:1],
                                          x2[:, dch * 512:(dch + 1) * 512],
                                          start=(dch == 0), stop=(dch == NCH - 1))
                        Smm.inc(li)
                        Snb.wait(t, nt + 1)
                        if nt > 0:
                            Snorm.wait(t, nt)
                        t.matmul(P[6][:, 0:512], ones_r_f[0:1, :],
                                 srow[0:1, 2 * 512:3 * 512], start=True, stop=True)
                        Sbc.inc(t.matmul(P[7][:, 0:512], ones_r_f[0:1, :],
                                         srow[0:1, 3 * 512:4 * 512],
                                         start=True, stop=True))

                kb.phase(f"ln{tag}", sync=f_sync, vector=f_vec, scalar=f_scal,
                         tensor=f_tens)

        def a_ld_phase(b):
            with ExitStack() as ps:
                wfree = wbuf[:, 0:32768].bitcast(F32)
                ldq = [ps.enter_context(nc.sbuf_tensor(f"ldq{i}_{b}", [16, T], F32))
                       for i in range(2)]
                ldq += [wfree[0:16, i * T:(i + 1) * T] for i in range(4)]
                LD, CUM, FQS, FKS, FQI, FKR = range(6)
                Smm = kb.strm()
                Sz = kb.strm()
                Sls = kb.strm()
                Ssm = kb.strm()
                Sfq = kb.strm()
                Str = kb.strm()
                Sev = kb.strm()
                Srow = kb.dstrm()
                Sbc2 = kb.strm()
                Sebc = kb.strm()
                Sst = kb.dstrm()

                def f_tens(t):
                    for nt in range(NT):
                        if nt >= 2:
                            Sz.wait(t, nt - 1)
                        for kch in range(NCH):
                            li = t.matmul(P[nt % 2][0:16, 0:512],
                                          wfm_sb[:, kch * 16:(kch + 1) * 16],
                                          xln[:, kch * T + nt * 512:
                                              kch * T + nt * 512 + 512],
                                          start=(kch == 0), stop=(kch == NCH - 1))
                        Smm.inc(li)
                    Sfq.wait(t, NCA + 1)
                    for c in range(NCA):
                        if c >= 2:
                            Sev.wait(t, c - 1)
                        Str.inc(t.transpose(P[2 + c % 2][0:128, 0:16],
                                              ldq[FKR][:, c * 128:(c + 1) * 128],
                                              ident[0:16, 0:16]))
                    Sev.wait(t, NCA)
                    Str.inc(t.transpose(P[2][0:16, 0:16],
                                         smal[:, SM_ETOT:SM_ETOT + 16],
                                         ident[0:16, 0:16]))
                    for c in range(NCA):
                        Srow.wdma(t, c)
                        if c >= 2:
                            Sebc.wait(t, c - 1)
                        Sbc2.inc(t.matmul(P[4 + c % 2][:, 0:16], ones_r_f[0:1, :],
                                           stage[0:1, (c % 2) * 512:
                                                 (c % 2) * 512 + 16],
                                           start=True, stop=True))

                def f_scal(s):
                    for nt in range(NT):
                        Smm.wait(s, nt + 1)
                        sl = slice(nt * 512, (nt + 1) * 512)
                        Sz.inc(s.copy(ldq[LD][:, sl], P[nt % 2][0:16, 0:512]))
                    s.activation(ldq[FQS][:, :], ldq[LD][:, :], AF.Abs)
                    s.activation(ldq[FQS][:, :], ldq[FQS][:, :], AF.Exp,
                                 scale=cpool[0:16, CP[-1.0]:CP[-1.0] + 1])
                    Sls.inc(s.activation(ldq[FQS][:, :], ldq[FQS][:, :], AF.Ln,
                                         bias=cpool[0:16, CP[1.0]:CP[1.0] + 1]))
                    Ssm.wait(s, 1)
                    s.activation(ldq[FQI][:, :], ldq[CUM][:, :], AF.Exp)
                    Sfq.inc(s.activation(smal[:, SM_ETOT:SM_ETOT + 16],
                                         smal[:, SM_TOT:SM_TOT + 16], AF.Exp))
                    for c in range(NCA):
                        sl = slice(c * 128, (c + 1) * 128)
                        s.activation(ldq[FQS][:, sl], ldq[CUM][:, sl], AF.Exp,
                                     bias=smal[:, SM_NEGS + c:SM_NEGS + c + 1])
                        s.activation(ldq[FKS][:, sl], ldq[CUM][:, sl], AF.Exp,
                                     scale=cpool[0:16, CP[-1.0]:CP[-1.0] + 1],
                                     bias=smal[:, SM_S + c:SM_S + c + 1])
                        Sfq.inc(s.activation(
                            ldq[FKR][:, sl], ldq[CUM][:, sl], AF.Exp,
                            scale=cpool[0:16, CP[-1.0]:CP[-1.0] + 1],
                            bias=smal[:, SM_TOT + c:SM_TOT + c + 1]))

                def f_vec(v):
                    Sls.wait(v, 1)
                    v.tensor_scalar_min(ldq[FKS][:, :], ldq[LD][:, :], 0.0)
                    v.tensor_sub(ldq[LD][:, :], ldq[FKS][:, :], ldq[FQS][:, :])
                    for c in range(NCA):
                        sl = slice(c * 128, (c + 1) * 128)
                        v.tensor_tensor_scan(ldq[CUM][:, sl], ldq[LD][:, sl],
                                             zz16[:, :], 0.0, ALU.add, ALU.add)
                    v.tensor_copy(smal[:, SM_S:SM_S + 16], ldq[CUM][:, 63:T:128])
                    v.tensor_copy(smal[:, SM_TOT:SM_TOT + 16], ldq[CUM][:, 127:T:128])
                    Ssm.inc(v.tensor_scalar_mul(smal[:, SM_NEGS:SM_NEGS + 16],
                                                smal[:, SM_S:SM_S + 16], -1.0))
                    for c in range(NCA):
                        Str.wait(v, c + 1)
                        Sev.inc(v.tensor_copy(fkr_tm[:, c * 16:(c + 1) * 16],
                                              P[2 + c % 2][0:128, 0:16]))
                    Str.wait(v, NCA + 1)
                    Sev.inc(v.tensor_copy(smalT[:, :], P[2][0:16, 0:16]))
                    for c in range(NCA):
                        Sbc2.wait(v, c + 1)
                        Sebc.inc(v.tensor_copy(etot_b[:, c * 16:(c + 1) * 16],
                                               P[4 + c % 2][:, 0:16]))

                def f_sync(sy):
                    Sfq.wait(sy, NCA + 1)
                    Sst.dma(sy, facd[0], ldq[FQS][:, :])
                    Sst.dma(sy, facd[1], ldq[FKS][:, :])
                    Sst.dma(sy, facd[2], ldq[FQI][:, :])
                    Sev.wait(sy, NCA + 1)
                    Sst.dma(sy, rowd[:, :], smalT[:, :])
                    Sst.waitall(sy)
                    for c in range(NCA):
                        if c >= 2:
                            Sbc2.wait(sy, c - 1)
                        Srow.dma(sy, stage[0:1, (c % 2) * 512:(c % 2) * 512 + 16],
                                 rowd[c:c + 1, :])
                kb.phase(f"ald{b}", tensor=f_tens, scalar=f_scal, vector=f_vec,
                         sync=f_sync)

        def proj_fm(tag, wname, kind, slabs):
            with ExitStack() as ps:
                tmp = ps.enter_context(nc.sbuf_tensor(f"tp{tag}", [128, 4 * 512], F32))
                osb = ps.enter_context(nc.sbuf_tensor(f"os{tag}", [128, 4 * 512], BF16))
                Swl, Sst, Srow = kb.qstrm(), kb.qstrm(), kb.qstrm()
                Smm, Ssil, Sev, Sbc = (kb.strm() for _ in range(4))
                nfac = 2 if kind == "qd2" else (1 if kind == "kd" else 0)
                fmap = [0, 2] if kind == "qd2" else [1]

                def f_sync(sy):
                    sy.wait_ge(kb.aux.h, wc_done[wname])
                    load_half(sy, Swl, wbfd[wname], D, 0, 512, 0)
                    rows_done = [0]

                    def rows(hi):
                        for i in range(rows_done[0], min(64 * nfac, hi)):
                            if i >= 2:
                                Sbc.wait(sy, i - 1)
                            ti, f = i // nfac, i % nfac
                            m, nt = ti // NT, ti % NT
                            Srow.dma(sy,
                                     stage[0:1, (i % 2) * 512:(i % 2) * 512 + 512],
                                     facd[fmap[f], m:m + 1,
                                          nt * 512:(nt + 1) * 512])
                        rows_done[0] = max(rows_done[0], min(64 * nfac, hi))
                    if nfac:
                        rows(2)
                    load_half(sy, Swl, wbfd[wname], D, 512, 512, 1)
                    for ti in range(64):
                        if ti >= 2:
                            store_tile(sy, ti - 2)
                        if nfac:
                            rows((ti + 3) * nfac)
                        if ti % 16 == 15 and ti // 16 + 2 < 4:
                            p = ti // 16 + 2
                            Smm.wait(sy, 16 * (p - 1))
                            load_half(sy, Swl, wbfd[wname], D, p * 512, 512, p % 2)
                    store_tile(sy, 62)
                    store_tile(sy, 63)
                    Sst.waitall(sy)
                    Swl.waitall(sy)
                    if nfac:
                        Srow.waitall(sy)

                def f_tens(t):
                    for piece in range(4):
                        Swl.wdma(t, 4 * (piece + 1) - 1)
                        for m4 in range(4):
                            m = piece * 4 + m4
                            for nt in range(NT):
                                ti = m * NT + nt
                                bank = ti % 6
                                if ti >= 6:
                                    Ssil.wait(t, ti - 5)
                                for kch in range(NCH):
                                    li = t.matmul(P[bank][:, 0:512],
                                                  wtile(piece % 2, kch, 512, m4 * 128),
                                                  xln[:, kch * T + nt * 512:
                                                      kch * T + nt * 512 + 512],
                                                  start=(kch == 0),
                                                  stop=(kch == NCH - 1))
                                Smm.inc(li)
                                for f in range(nfac):
                                    i = ti * nfac + f
                                    Srow.wdma(t, i)
                                    if i >= 2:
                                        Sev.wait(t, i - 1)
                                    Sbc.inc(t.matmul(
                                        P[6 + i % 2][:, 0:512], ones_r_f[0:1, :],
                                        stage[0:1, (i % 2) * 512:
                                              (i % 2) * 512 + 512],
                                        start=True, stop=True))

                def f_scal(s):
                    for ti in range(64):
                        bank = ti % 6
                        Smm.wait(s, ti + 1)
                        if nfac:
                            if ti >= 4:
                                Sev.wait(s, (ti - 3) * nfac)
                            Ssil.inc(s.activation(
                                tmp[:, (ti % 4) * 512:(ti % 4) * 512 + 512],
                                P[bank][:, 0:512], AF.Silu))
                        else:
                            if ti >= 4:
                                Sst.wdma(s, ti - 4)
                            m = ti // NT
                            bc = (0 if kind == "biasq" else 16) + m
                            Ssil.inc(s.activation(
                                osb[:, (ti % 4) * 512:(ti % 4) * 512 + 512],
                                P[bank][:, 0:512], AF.Identity,
                                bias=bqm_sb[:, bc:bc + 1]))

                def f_vec(v):
                    if not nfac:
                        return
                    for ti in range(64):
                        Ssil.wait(v, ti + 1)
                        if ti >= 2:
                            Sst.wdma(v, nfac * (ti - 1) - 1)
                        for f in range(nfac):
                            i = ti * nfac + f
                            Sbc.wait(v, i + 1)
                            Sev.inc(v.tensor_tensor(
                                osb[:, ((ti % 2) * 2 + f) * 512:
                                    ((ti % 2) * 2 + f) * 512 + 512],
                                tmp[:, (ti % 4) * 512:(ti % 4) * 512 + 512],
                                P[6 + i % 2][:, 0:512], ALU.mult))

                def store_tile(sy, ti):
                    m, nt = ti // NT, ti % NT
                    if nfac:
                        Sev.wait(sy, (ti + 1) * nfac)
                        for f in range(nfac):
                            Sst.dma(sy, slabs[f][4 * nt + m // 4][m % 4],
                                    osb[:, ((ti % 2) * 2 + f) * 512:
                                        ((ti % 2) * 2 + f) * 512 + 512])
                    else:
                        Ssil.wait(sy, ti + 1)
                        Sst.dma(sy, slabs[0][4 * nt + m // 4][m % 4],
                                osb[:, (ti % 4) * 512:(ti % 4) * 512 + 512])

                kb.phase(f"pfm{tag}", sync=f_sync, tensor=f_tens,
                         scalar=f_scal, vector=f_vec)

        def proj_tm(tag, wname, kind, slabo, browi=0, swap=False):
            with ExitStack() as ps:
                tmp = ps.enter_context(nc.sbuf_tensor(f"tp{tag}", [128, 4 * 512], F32))
                osb = ps.enter_context(nc.sbuf_tensor(f"os{tag}", [128, 4 * 512], BF16))
                Swl, Sst = kb.qstrm(), kb.qstrm()
                Smm, Ssil, Sev = (kb.strm() for _ in range(3))

                def f_sync(sy):
                    sy.wait_ge(kb.aux.h, wc_done[wname])
                    load_half(sy, Swl, wbfd[wname], D, 0, 512, 0)
                    load_half(sy, Swl, wbfd[wname], D, 512, 512, 1)
                    for ti in range(64):
                        ntd, tokch = ti // 16, ti % 16
                        if kind == "kdr":
                            Sev.wait(sy, ti + 1)
                        else:
                            Ssil.wait(sy, ti + 1)
                        dst_ = (slabo[4 * ntd + tokch // 4][tokch % 4] if swap
                                else slabo[tokch][ntd])
                        Sst.dma(sy, dst_,
                                osb[:, (ti % 4) * 512:(ti % 4) * 512 + 512])
                        if ti % 16 == 1 and 2 <= ti // 16 + 1 < 4:
                            p = ti // 16 + 1
                            load_half(sy, Swl, wbfd[wname], D, p * 512, 512, p % 2)
                    Sst.waitall(sy)
                    Swl.waitall(sy)

                def f_tens(t):
                    for ntd in range(4):
                        Swl.wdma(t, 4 * (ntd + 1) - 1)
                        for tokch in range(16):
                            ti = ntd * 16 + tokch
                            bank = ti % 6
                            if ti >= 6:
                                if kind == "kdr":
                                    Sev.wait(t, ti - 5)
                                else:
                                    Ssil.wait(t, ti - 5)
                            for kch in range(NCH):
                                st = (kch == 0)
                                sp = (kch == NCH - 1) and kind != "btm"
                                li = t.matmul(P[bank][:, 0:512],
                                              xln[:, kch * T + tokch * 128:
                                                  kch * T + tokch * 128 + 128],
                                              wtile(ntd % 2, kch, 512, 0, 512),
                                              start=st, stop=sp)
                            if kind == "btm":
                                li = t.matmul(P[bank][:, 0:512], ones_r_bf[0:1, :],
                                              brow_sb[0:1, browi * D + ntd * 512:
                                                      browi * D + (ntd + 1) * 512],
                                              start=False, stop=True)
                            Smm.inc(li)

                def f_scal(s):
                    for ti in range(64):
                        bank = ti % 6
                        Smm.wait(s, ti + 1)
                        if kind == "kdr":
                            if ti >= 4:
                                Sev.wait(s, ti - 3)
                            Ssil.inc(s.activation(
                                tmp[:, (ti % 4) * 512:(ti % 4) * 512 + 512],
                                P[bank][:, 0:512], AF.Silu))
                        else:
                            if ti >= 4:
                                Sst.wdma(s, ti - 4)
                            Ssil.inc(s.activation(
                                osb[:, (ti % 4) * 512:(ti % 4) * 512 + 512],
                                P[bank][:, 0:512], AF.Identity))

                def f_vec(v):
                    if kind != "kdr":
                        return
                    for ti in range(64):
                        Ssil.wait(v, ti + 1)
                        if ti >= 4:
                            Sst.wdma(v, ti - 4)
                        ntd, tokch = ti // 16, ti % 16
                        for hh in range(4):
                            h = ntd * 4 + hh
                            li = v.tensor_scalar(
                                osb[:, (ti % 4) * 512 + hh * 128:
                                    (ti % 4) * 512 + (hh + 1) * 128],
                                tmp[:, (ti % 4) * 512 + hh * 128:
                                    (ti % 4) * 512 + (hh + 1) * 128],
                                fkr_tm[:, tokch * 16 + h:tokch * 16 + h + 1],
                                None, ALU.mult)
                        Sev.inc(li)

                kb.phase(f"ptm{tag}", sync=f_sync, tensor=f_tens,
                         scalar=f_scal, vector=f_vec)


        def a_chunk_phase(b):
            with ExitStack() as ps:
                fmb = xarena[:, 0:24576]
                tmb = xarena[:, 24576:32768]
                scf = ps.enter_context(nc.sbuf_tensor(f"scf{b}", [128, 2 * 128], F32))
                scm = ps.enter_context(nc.sbuf_tensor(f"scm{b}", [128, 2 * 128], BF16))
                o_buf = ps.enter_context(nc.sbuf_tensor(f"obf{b}", [128, 16 * 128], F32))
                o2b = ps.enter_context(nc.sbuf_tensor(f"o2b{b}", [128, 16 * 128], BF16))
                onb = wbuf[:, 4 * SLOTW:5 * SLOTW]
                rrow = ps.enter_context(nc.sbuf_tensor(f"rr{b}", [1, 128], F32))
                ntp = ps.enter_context(nc.sbuf_tensor(f"ntp{b}", [128, 2 * 128], F32))
                xres = ps.enter_context(nc.sbuf_tensor(f"xr{b}", [128, 2 * 512], F32))
                xout = ps.enter_context(nc.sbuf_tensor(f"xo{b}", [128, 2 * 512], F32))

                Sfm, Stm, Swo, Sxr, Sxw = (kb.qstrm() for _ in range(5))
                Smm1, Ses, Seg, Smm2, Smm3 = (kb.strm() for _ in range(5))
                Sov, Svu, Ssq2, Smm4, Srs, Sbc3, Sno = (kb.strm() for _ in range(7))
                Smm5, Svr, Sz0 = (kb.strm() for _ in range(3))

                def fmtile(tno, h, c):
                    o = (tno * 16 + h) * 512 + (c % 4) * 128
                    return fmb[:, o:o + 128]

                def tmtile(tno, c, h):
                    o = (c % 2) * 2 * 2048 + tno * 2048 + (h // 4) * 512 + (h % 4) * 128
                    return tmb[:, o:o + 128]

                def f_sync(sy):
                    for g in range(4):
                        if g >= 1:
                            Smm3.wait(sy, 64 * g)
                        for tno, sl in enumerate([sA, sB, sC]):
                            Sfm.dma(sy, fmb[:, tno * 16 * 512:(tno + 1) * 16 * 512],
                                    sl[4 * g:4 * g + 4].rearrange("a b p c -> p a b c"))
                        if g == 0:
                            for c in (0, 1):
                                for tno, sl in enumerate([sD, sE]):
                                    o = (c % 2) * 4096 + tno * 2048
                                    Stm.dma(sy, tmb[:, o:o + 2048],
                                            sl[c].rearrange("b p c -> p b c"))
                            sy.wait_ge(kb.aux.h, wc_done["wo"])
                            for p in range(4):
                                load_half(sy, Swo, wbfd["wo"], D, p * 512, 512, p)
                        for c in range(4 * g, 4 * g + 4):
                            cc = c + 2
                            if cc < NCA:
                                Smm3.wait(sy, 16 * (cc - 1))
                                for tno, sl in enumerate([sD, sE]):
                                    o = (cc % 2) * 4096 + tno * 2048
                                    Stm.dma(sy, tmb[:, o:o + 2048],
                                            sl[cc].rearrange("b p c -> p b c"))
                        for dc in range(16):
                            i = 16 * g + dc
                            if i >= 2:
                                Svr.wait(sy, i - 1)
                            Sxr.dma(sy, xres[:, (dc % 2) * 512:(dc % 2) * 512 + 512],
                                    xin[b, dc * 128:(dc + 1) * 128,
                                        g * 512:(g + 1) * 512])
                            if i >= 1:
                                Svr.wait(sy, i)
                                pdc = (dc - 1) % 16
                                pg = g if dc >= 1 else g - 1
                                Sxw.dma(sy, xw[b, pdc * 128:(pdc + 1) * 128,
                                               pg * 512:(pg + 1) * 512],
                                        xout[:, (pdc % 2) * 512:(pdc % 2) * 512 + 512])
                    Svr.wait(sy, 64)
                    Sxw.dma(sy, xw[b, 15 * 128:16 * 128, 3 * 512:4 * 512],
                            xout[:, 512:1024])
                    Sxw.waitall(sy)
                    Sfm.waitall(sy)
                    Stm.waitall(sy)
                    Swo.waitall(sy)
                    Sxr.waitall(sy)

                def f_gp(g_):
                    g_.memset(S_a[:, :], 0.0)
                    Sz0.inc(g_.memset(S_ab[:, :], 0.0))
                    for c in range(NCA):
                        for h in range(HA):
                            i = c * 16 + h
                            Ses.wait(g_, i + 1)
                            Seg.inc(g_.affine_select(
                                scm[:, (h % 2) * 128:(h % 2) * 128 + 128],
                                scf[:, (h % 2) * 128:(h % 2) * 128 + 128],
                                [[1, 128]], ALU.is_ge, 0.0, base=0,
                                channel_multiplier=-1))

                def f_tens(t):
                    Sz0.wait(t, 1)
                    for c in range(NCA):
                        g = c // 4
                        Sfm.wdma(t, 3 * (g + 1) - 1)
                        Stm.wdma(t, 2 * (c + 1) - 1)
                        for h in range(HA):
                            i = c * 16 + h
                            if i >= 2:
                                Ses.wait(t, i - 1)
                            Smm1.inc(t.matmul(P[h % 2][:, 0:128], fmtile(2, h, c),
                                               fmtile(0, h, c), start=True, stop=True))
                            Seg.wait(t, i + 1)
                            if c > 0:
                                Svu.wait(t, (c - 1) * 16 + h + 1)
                            if i >= 2:
                                Sov.wait(t, i - 1)
                            t.matmul(P[2 + h % 2][:, 0:128], tmtile(1, c, h),
                                     scm[:, (h % 2) * 128:(h % 2) * 128 + 128],
                                     start=True, stop=False)
                            Smm2.inc(t.matmul(P[2 + h % 2][:, 0:128],
                                               S_ab[:, h * 128:(h + 1) * 128],
                                               fmtile(1, h, c), start=False, stop=True))
                            if i >= 2:
                                Svu.wait(t, i - 1)
                            Smm3.inc(t.matmul(P[4 + h % 2][:, 0:128], tmtile(0, c, h),
                                               tmtile(1, c, h), start=True, stop=True))
                        Ssq2.wait(t, 16 * (c + 1))
                        if c > 0:
                            Srs.wait(t, c)
                        if c % 4 == 0 and c > 0:
                            Svr.wait(t, 16 * (c // 4))
                        for h in range(HA):
                            li = t.matmul(P[6][0:1, 0:128], ones_c_bf[:, 0:1],
                                          o2b[:, h * 128:(h + 1) * 128],
                                          start=(h == 0), stop=(h == HA - 1))
                        Smm4.inc(li)
                        Srs.wait(t, c + 1)
                        if c > 0:
                            Sno.wait(t, 16 * c)
                        Sbc3.inc(t.matmul(P[7][:, 0:128], ones_r_f[0:1, :],
                                           rrow[0:1, :], start=True, stop=True))
                        if c % 4 == 3:
                            g = c // 4
                            Sno.wait(t, 64 * (g + 1))
                            Swo.waitall(t)
                            for dc in range(16):
                                i = 16 * g + dc
                                if i >= 2:
                                    Svr.wait(t, i - 1)
                                for kch in range(NCH):
                                    li = t.matmul(P[6 + dc % 2][:, 0:512],
                                                  wtile(dc // 4, kch, 512,
                                                        (dc % 4) * 128),
                                                  onb[:, kch * 512:(kch + 1) * 512],
                                                  start=(kch == 0),
                                                  stop=(kch == NCH - 1))
                                Smm5.inc(li)

                def f_scal(s):
                    for c in range(NCA):
                        for h in range(HA):
                            i = c * 16 + h
                            Smm1.wait(s, i + 1)
                            if i >= 2:
                                Seg.wait(s, i - 1)
                            Ses.inc(s.copy(scf[:, (h % 2) * 128:(h % 2) * 128 + 128],
                                           P[h % 2][:, 0:128]))
                        Sov.wait(s, 16 * (c + 1))
                        for h in range(HA):
                            Ssq2.inc(s.activation(o2b[:, h * 128:(h + 1) * 128],
                                                  o_buf[:, h * 128:(h + 1) * 128],
                                                  AF.Square))
                        Smm4.wait(s, c + 1)
                        s.copy(rrow[0:1, :], P[6][0:1, 0:128])
                        s.activation(rrow[0:1, :], rrow[0:1, :], AF.Ln,
                                     scale=cpool[0:1, CP[1.0 / 2048.0]:
                                                 CP[1.0 / 2048.0] + 1],
                                     bias=cpool[0:1, CP[1e-6]:CP[1e-6] + 1])
                        Srs.inc(s.activation(rrow[0:1, :], rrow[0:1, :], AF.Exp,
                                             scale=cpool[0:1, CP[-0.5]:CP[-0.5] + 1]))

                def f_vec(v):
                    Sz0.wait(v, 1)
                    for c in range(NCA):
                        g = c // 4
                        if c >= 1:
                            Ssq2.wait(v, 16 * c)
                        for h in range(HA):
                            i = c * 16 + h
                            Smm2.wait(v, i + 1)
                            Sov.inc(v.tensor_copy(o_buf[:, h * 128:(h + 1) * 128],
                                                  P[2 + h % 2][:, 0:128]))
                            Smm3.wait(v, i + 1)
                            v.scalar_tensor_tensor(
                                S_a[:, h * 128:(h + 1) * 128],
                                S_a[:, h * 128:(h + 1) * 128],
                                etot_b[:, c * 16 + h:c * 16 + h + 1],
                                P[4 + h % 2][:, 0:128], ALU.mult, ALU.add)
                            Svu.inc(v.tensor_copy(S_ab[:, h * 128:(h + 1) * 128],
                                                  S_a[:, h * 128:(h + 1) * 128]))
                        Sbc3.wait(v, c + 1)
                        if c >= 4:
                            Smm5.wait(v, 16 * (c // 4))
                        for dch in range(16):
                            v.tensor_tensor(ntp[:, (dch % 2) * 128:(dch % 2) * 128 + 128],
                                            o_buf[:, dch * 128:(dch + 1) * 128],
                                            P[7][:, 0:128], ALU.mult)
                            Sno.inc(v.tensor_scalar(
                                onb[:, dch * 512 + (c % 4) * 128:
                                    dch * 512 + (c % 4) * 128 + 128],
                                ntp[:, (dch % 2) * 128:(dch % 2) * 128 + 128],
                                nwm_sb[:, dch:dch + 1], None, ALU.mult))
                        if c % 4 == 3:
                            for dc in range(16):
                                i = 16 * g + dc
                                Smm5.wait(v, i + 1)
                                Sxr.wdma(v, i)
                                if i >= 2:
                                    Sxw.wdma(v, i - 2)
                                Svr.inc(v.tensor_tensor(
                                    xout[:, (dc % 2) * 512:(dc % 2) * 512 + 512],
                                    xres[:, (dc % 2) * 512:(dc % 2) * 512 + 512],
                                    P[6 + dc % 2][:, 0:512], ALU.add))

                kb.phase(f"ach{b}", sync=f_sync, gpsimd=f_gp, tensor=f_tens,
                         scalar=f_scal, vector=f_vec)


        def router_phase(tag, b):
            with ExitStack() as ps:
                cw = ps.enter_context(nc.sbuf_tensor(f"cwk{tag}", [128, 12 * 16], F32))
                cwt = ps.enter_context(nc.sbuf_tensor(f"cwt{tag}", [128, 16 * 16], F32))
                cwT = ps.enter_context(nc.sbuf_tensor(f"cwT{tag}", [16, T], F32))
                LG, LGE, MX, NMX, SM_, CUR, MASK, MI, PM, DEN, RD = range(11)

                def sl(i, w=16):
                    return cw[:, i * 16:i * 16 + w]

                def col(i):
                    return cw[:, i * 16:i * 16 + 1]

                Sst = kb.dstrm()
                Smm, Sv1, Ss1, Sv2, Str, Sv3 = (kb.strm() for _ in range(6))

                def f_tens(t):
                    def transpose_of(tc):
                        Sv2.wait(t, tc + 1)
                        if tc >= 2:
                            Sv3.wait(t, tc - 1)
                        Str.inc(t.transpose(P[4 + tc % 2][0:16, 0:128],
                                             cwt[:, tc * 16:(tc + 1) * 16],
                                             ident[:, :]))

                    for tc in range(16):
                        if tc >= 2:
                            Sv1.wait(t, tc - 1)
                        for kch in range(NCH):
                            li = t.matmul(P[tc % 2][:, 0:16],
                                          xln[:, kch * T + tc * 128:
                                              kch * T + tc * 128 + 128],
                                          wrm_sb[:, kch * 16:(kch + 1) * 16],
                                          start=(kch == 0), stop=(kch == NCH - 1))
                        Smm.inc(li)
                        if tc >= 1:
                            transpose_of(tc - 1)
                    transpose_of(15)

                def f_scal(s):
                    for tc in range(16):
                        Sv1.wait(s, tc + 1)
                        Ss1.inc(s.activation(sl(LGE), sl(LG), AF.Exp,
                                             bias=col(NMX)))

                def f_vec(v):
                    Svx = kb.strm()
                    nch_ = [0]

                    def ch(inst, strm=None):
                        if strm is not None:
                            strm.inc(inst)
                            v.wait_ge(strm.s.h, strm.s.n)
                        else:
                            nch_[0] += 1
                            Svx.inc(inst)
                            Svx.wait(v, nch_[0])

                    for tc in range(16):
                        Smm.wait(v, tc + 1)
                        ch(v.tensor_copy(sl(LG), P[tc % 2][:, 0:16]))
                        ch(v.reduce_max(col(MX), sl(LG), axis=AX.X))
                        ch(v.tensor_scalar_mul(col(NMX), col(MX), -1.0), Sv1)
                        Ss1.wait(v, tc + 1)
                        ch(v.reduce_sum(col(SM_), sl(LGE), axis=AX.X))
                        ch(v.reciprocal(col(RD), col(SM_)))
                        ch(v.tensor_scalar(sl(LG), sl(LGE), col(RD), None, ALU.mult))
                        ch(v.tensor_copy(sl(CUR), sl(LG)))
                        ch(v.tensor_scalar_mul(sl(MASK), sl(LG), 0.0))
                        for _ in range(4):
                            ch(v.reduce_max(col(MX), sl(CUR), axis=AX.X))
                            ch(v.tensor_scalar(sl(MI), sl(CUR), col(MX), None,
                                               ALU.is_ge))
                            ch(v.tensor_add(sl(MASK), sl(MASK), sl(MI)))
                            ch(v.scalar_tensor_tensor(sl(CUR), sl(MI), -2.0,
                                                      sl(CUR), ALU.mult, ALU.add))
                        ch(v.tensor_mul(sl(PM), sl(LG), sl(MASK)))
                        ch(v.reduce_sum(col(DEN), sl(PM), axis=AX.X))
                        ch(v.reciprocal(col(RD), col(DEN)))
                        ch(v.tensor_scalar_mul(col(RD), col(RD), 0.8))
                        ch(v.tensor_scalar(cwt[:, tc * 16:(tc + 1) * 16],
                                           sl(PM), col(RD), None, ALU.mult), Sv2)
                        Str.wait(v, tc + 1)
                        Sv3.inc(v.tensor_copy(cwT[0:16, tc * 128:(tc + 1) * 128],
                                              P[4 + tc % 2][0:16, 0:128]))

                def f_sync(sy):
                    Sv3.wait(sy, 16)
                    Sst.dma(sy, cwd[:, :], cwT[:, :])
                    Sst.waitall(sy)

                kb.phase(f"rt{tag}", tensor=f_tens, scalar=f_scal, vector=f_vec,
                         sync=f_sync)

        def g_gen_phase(tag, shared):
            with ExitStack() as ps:
                tmp = ps.enter_context(nc.sbuf_tensor(f"tpg{tag}", [128, 4 * 512], F32))
                tmp2 = ps.enter_context(nc.sbuf_tensor(f"t2g{tag}", [128, 2 * 512], F32))
                osb = ps.enter_context(nc.sbuf_tensor(f"osg{tag}", [128, 4 * 512], BF16))
                cwb = None if shared else ps.enter_context(
                    nc.sbuf_tensor(f"cwb{tag}", [128, 4 * 512], F32))
                Swl, Sst, Srw = kb.qstrm(), kb.qstrm(), kb.qstrm()
                Smm, Ssil, Sev, Sev2 = (kb.strm() for _ in range(4))
                Sbc4, Scw = (kb.strm() for _ in range(2))
                if shared:
                    Sev2 = Sev
                npc = 8 if shared else 32
                w1n, w3n = ("ws1", "ws3") if shared else ("we1", "we3")

                def wsrc(wn, q):
                    wt = wbfd[wn]
                    if shared:
                        return wt, q * 512
                    e, p = q // 2, q % 2
                    return wt[e * D:(e + 1) * D, :], p * 512

                def dst_of(gi):
                    qq, tt = gi // 16, gi % 16
                    m4, nt = tt // 4, tt % 4
                    if shared:
                        return gs_sl[nt][qq * 4 + m4]
                    return ge_sl[qq // 2][nt][(qq % 2) * 4 + m4]

                def f_sync(sy):
                    sy.wait_ge(kb.aux.h, max(wc_done[w1n], wc_done[w3n]))
                    for q in range(npc):
                        if q >= 2:
                            Smm.wait(sy, 16 * (q - 1))
                        t1, c1 = wsrc(w1n, q)
                        t3, c3 = wsrc(w3n, q)
                        load_half(sy, Swl, t1, D, c1, 512, (2 * q) % 4)
                        load_half(sy, Swl, t3, D, c3, 512, (2 * q + 1) % 4)
                        if not shared and q % 2 == 0:
                            e = q // 2
                            for nt in range(NT):
                                i = e * NT + nt
                                if i >= 2:
                                    Sbc4.wait(sy, i - 1)
                                Srw.dma(sy, stage[0:1, (i % 2) * 512:
                                                  (i % 2) * 512 + 512],
                                        cwd[e:e + 1, nt * 512:(nt + 1) * 512])
                        for ti in range(16):
                            gi = q * 16 + ti
                            if gi >= 4:
                                Sev2.wait(sy, gi - 3)
                                Sst.dma(sy, dst_of(gi - 4),
                                        osb[:, ((gi - 4) % 4) * 512:
                                            ((gi - 4) % 4) * 512 + 512])
                    for gi in range(npc * 16 - 4, npc * 16):
                        Sev2.wait(sy, gi + 1)
                        Sst.dma(sy, dst_of(gi),
                                osb[:, (gi % 4) * 512:(gi % 4) * 512 + 512])
                    Sst.waitall(sy)
                    Swl.waitall(sy)
                    if not shared:
                        Srw.waitall(sy)

                def f_tens(t):
                    for q in range(npc):
                        Swl.wdma(t, 8 * (q + 1) - 1)
                        if not shared and q % 2 == 0:
                            e = q // 2
                            for nt in range(NT):
                                i = e * NT + nt
                                Srw.wdma(t, i)
                                if i >= 2:
                                    Scw.wait(t, i - 1)
                                Sbc4.inc(t.matmul(
                                    P[6 + i % 2][:, 0:512], ones_r_f[0:1, :],
                                    stage[0:1, (i % 2) * 512:(i % 2) * 512 + 512],
                                    start=True, stop=True))
                        for ti in range(16):
                            gi = q * 16 + ti
                            m4, nt = ti // 4, ti % 4
                            b1, b3 = gi % 3, 3 + gi % 3
                            if gi >= 3:
                                Ssil.wait(t, gi - 2)
                            for kch in range(NCH):
                                t.matmul(P[b1][:, 0:512],
                                         wtile((2 * q) % 4, kch, 512, m4 * 128),
                                         xln[:, kch * T + nt * 512:
                                             kch * T + nt * 512 + 512],
                                         start=(kch == 0), stop=(kch == NCH - 1))
                            if gi >= 3:
                                Sev.wait(t, gi - 2)
                            for kch in range(NCH):
                                li = t.matmul(P[b3][:, 0:512],
                                              wtile((2 * q + 1) % 4, kch, 512,
                                                    m4 * 128),
                                              xln[:, kch * T + nt * 512:
                                                  kch * T + nt * 512 + 512],
                                              start=(kch == 0),
                                              stop=(kch == NCH - 1))
                            Smm.inc(li)

                def f_scal(s):
                    for gi in range(npc * 16):
                        Smm.wait(s, gi + 1)
                        if gi >= 4:
                            Sev2.wait(s, gi - 3)
                        Ssil.inc(s.activation(tmp[:, (gi % 4) * 512:(gi % 4) * 512 + 512],
                                              P[gi % 3][:, 0:512], AF.Silu))

                def f_vec(v):
                    for gi in range(npc * 16):
                        q, ti = gi // 16, gi % 16
                        nt = ti % 4
                        b3 = 3 + gi % 3
                        if not shared and ti == 0 and q % 2 == 0:
                            e = q // 2
                            for i in range(e * NT, (e + 1) * NT):
                                Sbc4.wait(v, i + 1)
                                Scw.inc(v.tensor_copy(
                                    cwb[:, (i % 4) * 512:(i % 4) * 512 + 512],
                                    P[6 + i % 2][:, 0:512]))
                        Smm.wait(v, gi + 1)
                        if gi >= 4:
                            Sst.wdma(v, gi - 4)
                        if shared:
                            Ssil.wait(v, gi + 1)
                            li = v.scalar_tensor_tensor(
                                osb[:, (gi % 4) * 512:(gi % 4) * 512 + 512],
                                tmp[:, (gi % 4) * 512:(gi % 4) * 512 + 512], 0.2,
                                P[b3][:, 0:512], ALU.mult, ALU.mult)
                            Sev.inc(li)
                        else:
                            Sev.inc(v.tensor_tensor(
                                tmp2[:, (gi % 2) * 512:(gi % 2) * 512 + 512],
                                P[b3][:, 0:512],
                                cwb[:, nt * 512:(nt + 1) * 512], ALU.mult))
                            Ssil.wait(v, gi + 1)
                            Sev2.inc(v.tensor_tensor(
                                osb[:, (gi % 4) * 512:(gi % 4) * 512 + 512],
                                tmp[:, (gi % 4) * 512:(gi % 4) * 512 + 512],
                                tmp2[:, (gi % 2) * 512:(gi % 2) * 512 + 512],
                                ALU.mult))

                kb.phase(f"gg{tag}", sync=f_sync, tensor=f_tens, scalar=f_scal,
                         vector=f_vec)

        def f_out_phase(tag, b, dst):
            with ExitStack() as ps:
                gbuf = ps.enter_context(nc.sbuf_tensor(f"gb{tag}", [128, 2 * 4096], BF16))
                xres = ps.enter_context(nc.sbuf_tensor(f"xrf{tag}", [128, 2 * 512], F32))
                xout = ps.enter_context(nc.sbuf_tensor(f"xof{tag}", [128, 2 * 512], F32))
                Swl, Sgl, Sxr, Sxw = (kb.qstrm() for _ in range(4))
                Smm, Sev = (kb.strm() for _ in range(2))
                WUP = 80

                def f_sync(sy):
                    sy.wait_ge(kb.aux.h, max(wc_done["ws2"], wc_done["we2"]))
                    for pi in range(8):
                        half, nt = pi // 4, pi % 4
                        mb = pi * 160
                        for qq in range(2):
                            if pi > 0:
                                Smm.wait(sy, (pi - 1) * 160 + (136 if qq == 0 else 152))
                            c0 = half * 1024 + qq * 512
                            load_half(sy, Swl, wbfd["ws2"][0:2048, :], 2048, c0,
                                      512, qq * 2)
                            load_half(sy, Swl, wbfd["ws2"][2048:4096, :], 2048, c0,
                                      512, qq * 2 + 1)
                        for L in range(20):
                            if pi > 0 or L >= 2:
                                Smm.wait(sy, mb + 8 * (L - 1))
                            if L < 4:
                                src_ = gs_sl[nt][8 * L:8 * L + 8]
                            else:
                                src_ = ge_sl[L - 4][nt]
                            Sgl.dma(sy, gbuf[:, (L % 2) * 4096:(L % 2) * 4096 + 4096],
                                    src_.rearrange("b p c -> p b c"))
                            if L >= 4:
                                e = L - 3
                                if e < NE:
                                    if e >= 5:
                                        Smm.wait(sy, mb + 32 + 8 * (e - 4))
                                    else:
                                        Smm.wait(sy, mb + 32)
                                    load_half(sy, Swl,
                                              wbfd["we2"][e * HID:(e + 1) * HID, :],
                                              HID, half * 1024, 1024, (4 + e) % 5)
                            if L == 0:
                                if pi > 0:
                                    Smm.wait(sy, pi * 160)
                                load_half(sy, Swl, wbfd["we2"][0:HID, :],
                                          HID, half * 1024, 1024, 4)
                        for dc8 in range(8):
                            i = pi * 8 + dc8
                            dch = half * 8 + dc8
                            if i >= 2:
                                Sev.wait(sy, i - 1)
                            Sxr.dma(sy, xres[:, (i % 2) * 512:(i % 2) * 512 + 512],
                                    xw[b, dch * 128:(dch + 1) * 128,
                                       nt * 512:(nt + 1) * 512])
                            if i >= 1:
                                Sev.wait(sy, i)
                                pi2, pd = (i - 1) // 8, (i - 1) % 8
                                pdch = (pi2 // 4) * 8 + pd
                                pnt = pi2 % 4
                                Sxw.dma(sy, dst[b, pdch * 128:(pdch + 1) * 128,
                                                pnt * 512:(pnt + 1) * 512],
                                        xout[:, ((i - 1) % 2) * 512:
                                             ((i - 1) % 2) * 512 + 512])
                    Sev.wait(sy, 64)
                    Sxw.dma(sy, dst[b, 15 * 128:16 * 128, 3 * 512:4 * 512],
                            xout[:, 512:1024])
                    Sxw.waitall(sy)
                    Sgl.waitall(sy)
                    Swl.waitall(sy)
                    Sxr.waitall(sy)

                def f_tens(t):
                    for pi in range(8):
                        half, nt = pi // 4, pi % 4
                        wb = WUP * pi
                        if pi >= 1:
                            Sev.wait(t, 8 * pi)
                        for j in range(160):
                            Lj = j // 8
                            if j % 8 == 0:
                                Sgl.wdma(t, pi * 20 + Lj)
                            rhs = gbuf[:, (Lj % 2) * 4096 + (j % 8) * 512:
                                       (Lj % 2) * 4096 + (j % 8) * 512 + 512]
                            if j < 32:
                                kch = j
                                Swl.wdma(t, wb + 15)
                                lhs = lambda dc8, kch=kch: wtile(
                                    (dc8 // 4) * 2 + (0 if kch < 16 else 1),
                                    kch % 16, 512, (dc8 % 4) * 128)
                                dcs = range(8)
                            else:
                                e, kch = (j - 32) // 8, (j - 32) % 8
                                Swl.wdma(t, wb + 16 + 4 * (e + 1) - 1)
                                lhs = lambda dc8, e=e, kch=kch: wtile(
                                    (4 + e) % 5, kch, 1024, dc8 * 128)
                                dcs = range(8)
                            for dc8 in dcs:
                                li = t.matmul(P[dc8][:, 0:512], lhs(dc8), rhs,
                                              start=(j == 0), stop=(j == 159))
                            Smm.inc(li)

                def f_vec(v):
                    for dc8 in range(8):
                        v.memset(P[dc8][:, 0:512], 0.0)
                    for pi in range(8):
                        for dc8 in range(8):
                            i = pi * 8 + dc8
                            Smm.wait(v, pi * 160 + 160)
                            Sxr.wdma(v, i)
                            if i >= 2:
                                Sxw.wdma(v, i - 2)
                            Sev.inc(v.tensor_tensor(
                                xout[:, (dc8 % 2) * 512:(dc8 % 2) * 512 + 512],
                                xres[:, (dc8 % 2) * 512:(dc8 % 2) * 512 + 512],
                                P[dc8][:, 0:512], ALU.add))

                kb.phase(f"fo{tag}", sync=f_sync, tensor=f_tens, vector=f_vec)


        def b_chunk_phase(b):
            with ExitStack() as ps:
                abf = xarena
                qf = abf[:, 0:2048]
                kf = abf[:, 2048:4096]
                ktm = abf[:, 4096:6144]
                vtm = abf[:, 6144:8192]
                scb = abf[:, 8192:10240]
                kksb = abf[:, 10240:11264]
                qdb = abf[:, 11264:12288]
                S_bb = abf[:, 12288:16384]
                wfree = wbuf[:, 0:40960].bitcast(F32)
                dtb = wfree[:, 0:4096]
                csb2 = wfree[:, 4096:8192]
                S_b = wfree[:, 8192:12288]
                xres = ps.enter_context(nc.sbuf_tensor(f"xrb{b}", [128, 2 * 512], F32))
                xout = ps.enter_context(nc.sbuf_tensor(f"xob{b}", [128, 2 * 512], F32))
                idx = np.arange(CB, dtype=np.float64)
                betas = np.linspace(np.log2(1.0 / 32.0), np.log2(1.0 / 512.0), HB)
                gdec = 1.0 - np.exp2(betas)
                cdv = [float((gdec[h] ** CB)) for h in range(HB)]

                Sld, Sxr, Sxw, Scr = (kb.qstrm() for _ in range(4))
                Sbc5, Scs, Sz = (kb.strm() for _ in range(3))
                Smm1, Svd, Smm2, Svo, Smm3, Svu, Sqk = (kb.strm() for _ in range(7))

                def f_gp(g_):
                    g_.memset(S_b[:, :], 0.0)
                    Sz.inc(g_.memset(S_bb[:, :], 0.0))

                def f_sync(sy):
                    for hh in range(HB):
                        if hh >= 2:
                            Sbc5.wait(sy, hh - 1)
                        Scr.dma(sy, stage[0:1, (hh % 2) * 512:(hh % 2) * 512 + 512],
                                csr[0:1, hh * 512:(hh + 1) * 512])
                    for ch in range(NCB * HB):
                        c, h = ch // HB, ch % HB
                        r = ch % 2
                        if ch >= 2:
                            Svu.wait(sy, 2 * (ch - 1))
                        Sld.dma(sy, qf[:, r * 1024:r * 1024 + 1024],
                                sA[4 * c + h // 2][(2 * h) % 4:(2 * h) % 4 + 2]
                                .rearrange("b p c -> p b c"))
                        Sld.dma(sy, kf[:, r * 1024:r * 1024 + 1024],
                                sB[4 * c + h // 2][(2 * h) % 4:(2 * h) % 4 + 2]
                                .rearrange("b p c -> p b c"))
                        Sld.dma(sy, ktm[:, r * 1024:r * 1024 + 1024],
                                sC[(h // 2) * 4 + c]
                                [:, :, (h % 2) * 256:(h % 2) * 256 + 256]
                                .rearrange("b p c -> p b c"))
                        Sld.dma(sy, vtm[:, r * 1024:r * 1024 + 1024],
                                sD[(h // 2) * 4 + c]
                                [:, :, (h % 2) * 256:(h % 2) * 256 + 256]
                                .rearrange("b p c -> p b c"))
                        Sld.dma(sy, dtb[:, r * 2048:r * 2048 + 2048],
                                dbTm[h].rearrange("(b p) c -> p b c", p=128))
                        for fs in range(2):
                            i = ch * 2 + fs
                            dch = 2 * h + fs
                            if i >= 2:
                                Svo.wait(sy, i - 1)
                            Sxr.dma(sy, xres[:, (i % 2) * 512:(i % 2) * 512 + 512],
                                    xw[b, dch * 128:(dch + 1) * 128,
                                       c * 512:(c + 1) * 512])
                            if i >= 1:
                                Svo.wait(sy, i)
                                pch, pfs = (i - 1) // 2, (i - 1) % 2
                                pc, ph = pch // HB, pch % HB
                                pdch = 2 * ph + pfs
                                Sxw.dma(sy, xw[b, pdch * 128:(pdch + 1) * 128,
                                               pc * 512:(pc + 1) * 512],
                                        xout[:, ((i - 1) % 2) * 512:
                                             ((i - 1) % 2) * 512 + 512])
                    Svo.wait(sy, 64)
                    Sxw.dma(sy, xw[b, 15 * 128:16 * 128, 3 * 512:4 * 512],
                            xout[:, 512:1024])
                    Sxw.waitall(sy)
                    Sld.waitall(sy)
                    Scr.waitall(sy)
                    Sxr.waitall(sy)

                def f_tens(t):
                    Sz.wait(t, 1)
                    for hh in range(HB):
                        Scr.wdma(t, hh)
                        if hh >= 2:
                            Scs.wait(t, hh - 1)
                        Sbc5.inc(t.matmul(
                            P[6 + hh % 2][:, 0:512], ones_r_f[0:1, :],
                            stage[0:1, (hh % 2) * 512:(hh % 2) * 512 + 512],
                            start=True, stop=True))
                    for ch in range(NCB * HB):
                        c, h = ch // HB, ch % HB
                        r = ch % 2
                        Sld.wdma(t, 5 * (ch + 1) - 1)
                        Sqk.wait(t, 6 * (ch + 1))
                        if ch >= 1:
                            Svd.wait(t, 4 * ch)
                        for jt in range(4):
                            for es_ in range(2):
                                li = t.matmul(P[jt][:, 0:512],
                                              kf[:, (r * 2 + es_) * 512 + jt * 128:
                                                 (r * 2 + es_) * 512 + jt * 128 + 128],
                                              qf[:, (r * 2 + es_) * 512:
                                                 (r * 2 + es_) * 512 + 512],
                                              start=(es_ == 0), stop=(es_ == 1))
                            Smm1.inc(li)
                        Svd.wait(t, 4 * (ch + 1))
                        if c >= 1:
                            Svu.wait(t, 2 * ((c - 1) * HB + h + 1))
                        if ch >= 1:
                            Svo.wait(t, 2 * ch)
                        for fs in range(2):
                            for jt in range(4):
                                t.matmul(P[4 + fs][:, 0:512],
                                         vtm[:, r * 1024 + jt * 256 + fs * 128:
                                             r * 1024 + jt * 256 + fs * 128 + 128],
                                         scb[:, jt * 512:(jt + 1) * 512],
                                         start=(jt == 0), stop=False)
                            for es_ in range(2):
                                li = t.matmul(P[4 + fs][:, 0:512],
                                              S_bb[:, (h * 2 + es_) * 256 + fs * 128:
                                                   (h * 2 + es_) * 256 + fs * 128 + 128],
                                              qdb[:, es_ * 512:(es_ + 1) * 512],
                                              start=False, stop=(es_ == 1))
                            Smm2.inc(li)
                        if ch >= 1:
                            Svu.wait(t, 2 * ch)
                        for es_ in range(2):
                            for jt in range(4):
                                li = t.matmul(P[6 + es_][:, 0:256],
                                              kksb[:, jt * 256 + es_ * 128:
                                                   jt * 256 + es_ * 128 + 128],
                                              vtm[:, r * 1024 + jt * 256:
                                                  r * 1024 + jt * 256 + 256],
                                              start=(jt == 0), stop=(jt == 3))
                            Smm3.inc(li)

                def f_vec(v):
                    Sz.wait(v, 1)
                    for hh in range(HB):
                        Sbc5.wait(v, hh + 1)
                        Scs.inc(v.tensor_copy(csb2[:, hh * 512:(hh + 1) * 512],
                                              P[6 + hh % 2][:, 0:512]))
                    for ch in range(NCB * HB):
                        c, h = ch // HB, ch % HB
                        r = ch % 2
                        Sld.wdma(v, 5 * (ch + 1) - 1)
                        if ch >= 1:
                            Smm2.wait(v, 2 * ch)
                            Smm3.wait(v, 2 * ch)
                        for es_ in range(2):
                            v.tensor_tensor(qdb[:, es_ * 512:(es_ + 1) * 512],
                                            qf[:, (r * 2 + es_) * 512:
                                               (r * 2 + es_) * 512 + 512],
                                            csb2[:, h * 512:(h + 1) * 512], ALU.mult)
                        for jt in range(4):
                            li = v.tensor_scalar(kksb[:, jt * 256:(jt + 1) * 256],
                                                 ktm[:, r * 1024 + jt * 256:
                                                     r * 1024 + jt * 256 + 256],
                                                 ksbt_sb[:, jt * 8 + h:jt * 8 + h + 1],
                                                 None, ALU.mult)
                        Sqk.inc(li, 6)
                        for jt in range(4):
                            Smm1.wait(v, 4 * ch + jt + 1)
                            Svd.inc(v.tensor_tensor(
                                scb[:, jt * 512:(jt + 1) * 512], P[jt][:, 0:512],
                                dtb[:, r * 2048 + jt * 512:r * 2048 + jt * 512 + 512],
                                ALU.mult))
                        for fs in range(2):
                            i = ch * 2 + fs
                            Smm2.wait(v, 2 * ch + fs + 1)
                            Sxr.wdma(v, i)
                            if i >= 2:
                                Sxw.wdma(v, i - 2)
                            Svo.inc(v.tensor_tensor(
                                xout[:, (fs % 2) * 512:(fs % 2) * 512 + 512],
                                xres[:, (i % 2) * 512:(i % 2) * 512 + 512],
                                P[4 + fs][:, 0:512], ALU.add))
                        for es_ in range(2):
                            Smm3.wait(v, 2 * ch + es_ + 1)
                            o = (h * 2 + es_) * 256
                            v.scalar_tensor_tensor(S_b[:, o:o + 256], S_b[:, o:o + 256],
                                                   cdv[h], P[6 + es_][:, 0:256],
                                                   ALU.mult, ALU.add)
                            Svu.inc(v.tensor_copy(S_bb[:, o:o + 256], S_b[:, o:o + 256]))

                kb.phase(f"bch{b}", gpsimd=f_gp, sync=f_sync, tensor=f_tens,
                         vector=f_vec)

        def finish(src_for_out=None):
            def f_g(g):
                a = Strm(kb.aux)
                if src_for_out is not None:
                    fs = src_for_out.rearrange("a b c -> (a b c)")
                    fd = out.rearrange("a b c -> (a b c)")
                    n = 2 * D * T
                    CH = 1 << 20
                    for o in range(0, n, CH):
                        a.dma_batched(g, fd[o:min(n, o + CH)],
                                      fs[o:min(n, o + CH)])
                if dbg is not None:
                    dsrc = {"xw": xw, "sA": sA, "sB": sB, "sC": sC, "sD": sD,
                            "sE": sE, "facd": facd, "cwd": cwd, "ge": ge_sl,
                            "gs": gs_sl, "xlnd": xlnd}[dump]
                    nel = int(np.prod(dbg.shape))
                    r = {1: "a -> (a)", 2: "a b -> (a b)", 3: "a b c -> (a b c)",
                         4: "a b c d -> (a b c d)", 5: "a b c d e -> (a b c d e)"}
                    fs = dsrc.rearrange(r[len(dsrc.shape)])
                    fd = dbg.rearrange(r[len(dbg.shape)])
                    CH = 1 << 20
                    for o in range(0, nel, CH):
                        a.dma_batched(g, fd[o:min(nel, o + CH)],
                                      fs[o:min(nel, o + CH)])
                g.wait_ge(a.s.h, a.s.n)
            kb.phase("finish", gpsimd=f_g)

        def dump_xln():
            if dump == "xlnd":
                def f_g(g):
                    a = Strm(kb.aux)
                    a.dma(g, xlnd[:, :], xln[:, :])
                    g.wait_ge(a.s.h, a.s.n)
                kb.phase("dumpxln", gpsimd=f_g)

        seq = []

        def step(name, fn):
            seq.append((name, fn))

        for b in range(2):
            step(f"aln{b}", lambda b=b: ln_phase(f"a{b}", b, xin))
            step(f"axd{b}", lambda b=b: dump_xln())
            step(f"ald{b}", lambda b=b: a_ld_phase(b))
            step(f"aqd{b}", lambda b=b: proj_fm(f"qd{b}", "wq", "qd2", [sA, sB]))
            step(f"akd{b}", lambda b=b: proj_fm(f"kd{b}", "wk", "kd", [sC]))
            step(f"akr{b}", lambda b=b: proj_tm(f"kr{b}", "wk", "kdr", sD))
            step(f"avt{b}", lambda b=b: proj_tm(f"vt{b}", "wv", "vtm", sE))
            step(f"ach{b}", lambda b=b: a_chunk_phase(b))
            for r in range(2):
                step(f"fln{b}{r}", lambda b=b, r=r: ln_phase(f"f{b}{r}", b, xw))
                step(f"frt{b}{r}", lambda b=b, r=r: router_phase(f"{b}{r}", b))
                step(f"fgs{b}{r}", lambda b=b, r=r: g_gen_phase(f"s{b}{r}", True))
                step(f"fge{b}{r}", lambda b=b, r=r: g_gen_phase(f"e{b}{r}", False))
                step(f"fo3{b}{r}", lambda b=b, r=r: f_out_phase(
                    f"{b}{r}", b, xw if r == 0 else out))
                if r == 0:
                    step(f"bln{b}", lambda b=b: ln_phase(f"b{b}", b, xw))
                    step(f"bqf{b}", lambda b=b: proj_fm(f"bq{b}", "bwq", "biasq", [sA]))
                    step(f"bkf{b}", lambda b=b: proj_fm(f"bk{b}", "bwk", "biask", [sB]))
                    step(f"bkt{b}", lambda b=b: proj_tm(f"bkt{b}", "bwk", "btm", sC, 0, swap=True))
                    step(f"bvt{b}", lambda b=b: proj_tm(f"bvt{b}", "bwv", "btm", sD, 1, swap=True))
                    step(f"bch{b}", lambda b=b: b_chunk_phase(b))

        if stop_after == "w0":
            seq = []
        ran = []
        for name, fn in seq:
            fn()
            ran.append(name)
            if stop_after is not None and name == stop_after:
                break
        finish(src_for_out=xw if stop_after is not None else None)
    return nc


def _consts():
    idx = np.arange(CB, dtype=np.float64)
    betas = np.linspace(np.log2(1.0 / 32.0), np.log2(1.0 / 512.0), HB)
    g = (1.0 - np.exp2(betas))
    Dm = np.where(idx[:, None] >= idx[None, :],
                  g[:, None, None] ** (idx[:, None] - idx[None, :]), 0.0)
    dbT = np.ascontiguousarray(Dm.transpose(0, 2, 1)).astype(np.float32)
    csr = (g[:, None] ** (idx + 1)).astype(np.float32).reshape(1, -1)
    ks = (g[:, None] ** (CB - 1 - idx)).astype(np.float32)
    ksbt = np.zeros((128, 32), np.float32)
    for js in range(4):
        for h in range(HB):
            ksbt[:, js * 8 + h] = ks[h, js * 128:(js + 1) * 128]
    cd = (g ** CB).astype(np.float32)
    return dbT, csr, ksbt, cd


def prep_core_inputs(inputs, core):
    b0 = core * 2
    x = np.ascontiguousarray(
        np.transpose(inputs["L0_attn_a_x"][b0:b0 + 2], (0, 2, 1))).astype(np.float32)
    dbT, csr, ksbt, _cd = _consts()
    wfmv = np.ascontiguousarray(
        inputs["L0_attn_a_Wf"].reshape(16, 128, 16).transpose(1, 0, 2).reshape(128, 256))
    wrmv = np.ascontiguousarray(
        inputs["L0_ffn_w_router"].reshape(16, 128, 16).transpose(1, 0, 2).reshape(128, 256))
    normwm = np.ascontiguousarray(inputs["L0_attn_a_norm_w"].reshape(16, 128).T)
    biasm = np.stack([np.ascontiguousarray(inputs["L0_attn_b_bq"].reshape(16, 128).T),
                      np.ascontiguousarray(inputs["L0_attn_b_bk"].reshape(16, 128).T)])
    biasrow = np.stack([inputs["L0_attn_b_bk"], inputs["L0_attn_b_bv"]]).astype(np.float32)
    im = {
        "x": x,
        "wq": inputs["L0_attn_a_Wq"], "wk": inputs["L0_attn_a_Wk"],
        "wv": inputs["L0_attn_a_Wv"], "wo": inputs["L0_attn_a_Wo"],
        "bwq": inputs["L0_attn_b_Wq"], "bwk": inputs["L0_attn_b_Wk"],
        "bwv": inputs["L0_attn_b_Wv"],
        "we1": inputs["L0_ffn_we1"].reshape(NE * D, HID),
        "we3": inputs["L0_ffn_we3"].reshape(NE * D, HID),
        "we2": inputs["L0_ffn_we2"].reshape(NE * HID, D),
        "ws1": inputs["L0_ffn_ws1"], "ws3": inputs["L0_ffn_ws3"],
        "ws2": inputs["L0_ffn_ws2"],
        "wfm": wfmv, "wrm": wrmv, "normwm": normwm,
        "biasm": biasm.astype(np.float32), "biasrow": biasrow,
        "identm": np.eye(128, dtype=np.float32),
        "dbTm": dbT, "csr": csr, "ksbtm": ksbt,
    }
    return {k: np.ascontiguousarray(np.asarray(v, dtype=v.dtype if hasattr(v, "dtype") else np.float32))
            for k, v in im.items()}


def kernel(**inputs):
    nc = build_program()
    in_maps = [prep_core_inputs(inputs, c) for c in range(8)]
    res = run_bass_kernel_spmd(nc, in_maps, core_ids=list(range(8)))
    outs = []
    for c in range(8):
        o = res.results[c]["out"]
        outs.append(np.transpose(o, (0, 2, 1)))
    return np.ascontiguousarray(np.concatenate(outs, axis=0))


# version 08
# speedup vs baseline: 1.1253x; 1.0613x over previous
import numpy as np
from contextlib import ExitStack

import concourse.bass as bass
import concourse.mybir as mybir
from concourse.bass_utils import run_bass_kernel_spmd

F32 = mybir.dt.float32
BF16 = mybir.dt.bfloat16
AF = mybir.ActivationFunctionType
ALU = mybir.AluOpType
AX = mybir.AxisListType

ENGS = ("tensor", "vector", "scalar", "gpsimd", "sync")

D, T = 2048, 2048
NCH, NT = 16, 4
CA, NCA, HA = 128, 16, 16
CB, NCB, HB = 512, 4, 8
NE, HID, HS = 16, 1024, 4096

CPV = [1e-5, 1e-6, -0.5, 1.0, -1.0, 0.0, 0.2, 1.0 / 2048.0]
CP = {v: i for i, v in enumerate(CPV)}


class Sem:
    __slots__ = ("h", "n")

    def __init__(self, h):
        self.h = h
        self.n = 0


class Strm:

    def __init__(self, sems):
        if not isinstance(sems, list):
            sems = [sems]
        self.sems = sems
        self.bases = [s.n for s in sems]
        self.s = sems[0]
        self.base = self.bases[0]
        self.k = 0
        self.cnt = [0] * len(sems)

    def inc(self, inst, k=1):
        inst.then_inc(self.s.h, k)
        self.s.n += k

    def dma(self, q, dst, src):
        n = len(self.sems)
        p = self.k % n
        sem = self.sems[p]
        if n > 1 and self.cnt[p] >= 1:
            q.wait_ge(sem.h, self.bases[p] + 16 * self.cnt[p])
        q.dma_start(dst, src).then_inc(sem.h, 16)
        sem.n += 16
        self.cnt[p] += 1
        self.k += 1

    def dma_batched(self, q, dst, src, depth=8):
        if (self.s.n - self.base) // 16 % depth == 0 and self.s.n > self.base:
            q.wait_ge(self.s.h, self.s.n)
        q.dma_start(dst, src).then_inc(self.s.h, 16)
        self.s.n += 16
        self.k += 1

    def drain(self, q):
        q.wait_ge(self.s.h, self.s.n)

    def wdma(self, eng, j):
        n = len(self.sems)
        for p in range(n):
            c = (j - p) // n + 1 if j >= p else 0
            if c > 0:
                eng.wait_ge(self.sems[p].h, self.bases[p] + 16 * c)

    def waitall(self, eng):
        for s in self.sems:
            eng.wait_ge(s.h, s.n)

    def wait(self, eng, i):
        eng.wait_ge(self.s.h, self.base + i)


class KB:
    def __init__(self, nc, es):
        self.nc = nc
        self.es = es
        self.pi = 0
        self.pb = Sem(es.enter_context(nc.semaphore("pb")))
        self.aux = Sem(es.enter_context(nc.semaphore("aux")))
        self.pool = [Sem(es.enter_context(nc.semaphore(f"g{i}")))
                     for i in range(56)]
        self.pooli = 0

    def strm(self):
        s = Strm(self.pool[self.pooli % len(self.pool)])
        self.pooli += 1
        return s

    def dstrm(self, n=2):
        sems = []
        for _ in range(n):
            sems.append(self.pool[self.pooli % len(self.pool)])
            self.pooli += 1
        return Strm(sems)

    def qstrm(self):
        return self.dstrm(4)

    def sb(self, name, shape, dt):
        return self.es.enter_context(self.nc.sbuf_tensor(name, shape, dt))

    def phase(self, name, **bodies):
        i = self.pi
        pb = self.pb
        with self.nc.Block(f"ph{i}_{name}") as blk:
            for en in ENGS:
                fn = bodies.get(en)
                pre = bodies.get("pre_" + en)

                def mk(en=en, fn=fn, pre=pre):
                    def body(eng):
                        if pre is not None:
                            pre(eng)
                        eng.wait_ge(pb.h, 5 * i)
                        if fn is not None:
                            fn(eng)
                        eng.drain()
                        eng.sem_inc(pb.h, 1)
                    return body
                getattr(blk, en)(mk())
        pb.n += 5
        self.pi += 1


def build_program(stop_after=None, dump=None):
    nc = bass.Bass("TRN2", target_bir_lowering=False)

    xin = nc.dram_tensor("x", [2, D, T], F32, kind="ExternalInput").ap()
    out = nc.dram_tensor("out", [2, D, T], F32, kind="ExternalOutput").ap()
    xw = nc.dram_tensor("xw", [2, D, T], F32, kind="Internal").ap()

    wshapes = {
        "wq": [D, D], "wk": [D, D], "wv": [D, D], "wo": [D, D],
        "bwq": [D, D], "bwk": [D, D], "bwv": [D, D],
        "we1": [NE * D, HID], "we3": [NE * D, HID], "we2": [NE * HID, D],
        "ws1": [D, HS], "ws3": [D, HS], "ws2": [HS, D],
    }
    wf32 = {k: nc.dram_tensor(k, s, F32, kind="ExternalInput").ap()
            for k, s in wshapes.items()}
    wbfd = {k: nc.dram_tensor(k + "_bf", s, BF16, kind="Internal").ap()
            for k, s in wshapes.items()}

    wfm = nc.dram_tensor("wfm", [128, 256], F32, kind="ExternalInput").ap()
    wrm = nc.dram_tensor("wrm", [128, 256], F32, kind="ExternalInput").ap()
    normwm = nc.dram_tensor("normwm", [128, 16], F32, kind="ExternalInput").ap()
    biasm = nc.dram_tensor("biasm", [2, 128, 16], F32, kind="ExternalInput").ap()
    biasrow = nc.dram_tensor("biasrow", [2, D], F32, kind="ExternalInput").ap()
    identm = nc.dram_tensor("identm", [128, 128], F32, kind="ExternalInput").ap()
    dbTm = nc.dram_tensor("dbTm", [HB, CB, CB], F32, kind="ExternalInput").ap()
    csr = nc.dram_tensor("csr", [1, HB * CB], F32, kind="ExternalInput").ap()
    ksbtm = nc.dram_tensor("ksbtm", [128, 32], F32, kind="ExternalInput").ap()

    def mkslab(name):
        return nc.dram_tensor(name, [NCH, NT, 128, 512], BF16, kind="Internal").ap()
    sA, sB, sC, sD, sE = (mkslab(n) for n in ["sA", "sB", "sC", "sD", "sE"])
    ge_sl = nc.dram_tensor("ge", [NE, NT, 8, 128, 512], BF16, kind="Internal").ap()
    gs_sl = nc.dram_tensor("gs", [NT, 32, 128, 512], BF16, kind="Internal").ap()
    facd = nc.dram_tensor("facd", [3, HA, T], F32, kind="Internal").ap()
    rowd = nc.dram_tensor("rowd", [16, 16], F32, kind="Internal").ap()
    cwd = nc.dram_tensor("cwd", [NE, T], F32, kind="Internal").ap()
    xlnd = nc.dram_tensor("xlnd", [128, NCH * T], BF16, kind="Internal").ap()

    dbg = None
    if dump is not None:
        dshape = {"xw": [2 * D, T], "sA": [NCH * NT * 128, 512],
                  "sB": [NCH * NT * 128, 512], "sC": [NCH * NT * 128, 512],
                  "sD": [NCH * NT * 128, 512], "sE": [NCH * NT * 128, 512],
                  "facd": [3 * HA, T], "cwd": [NE, T], "xlnd": [128, NCH * T],
                  "ge": [NE * NT * 8 * 128, 512], "gs": [NT * 32 * 128, 512]}[dump]
        dbg = nc.dram_tensor("dbg", dshape, F32, kind="ExternalOutput").ap()

    es = ExitStack()
    with es:
        kb = KB(nc, es)
        P = [es.enter_context(nc.psum_tensor(f"pk{i}", [128, 512], F32))
             for i in range(8)]

        xarena = kb.sb("xarena", [128, NCH * T], BF16)
        xln = xarena
        wbuf = kb.sb("wbuf", [128, 5 * 8192], BF16)
        ones_r_bf = kb.sb("ones_r_bf", [1, 128], BF16)
        ones_r_f = kb.sb("ones_r_f", [1, 128], F32)
        ones_c_bf = kb.sb("ones_c_bf", [128, 1], BF16)
        cpool = kb.sb("cpool", [128, 8], F32)
        ident = kb.sb("ident", [128, 128], F32)
        stage = kb.sb("stage", [1, 2 * 512], F32)
        nwm_sb = kb.sb("nwm_sb", [128, 16], F32)
        bqm_sb = kb.sb("bqm_sb", [128, 32], F32)
        brow_sb = kb.sb("brow_sb", [1, 2 * D], BF16)
        wfm_sb = kb.sb("wfm_sb", [128, 256], BF16)
        wrm_sb = kb.sb("wrm_sb", [128, 256], BF16)
        smal = kb.sb("smal", [16, 64], F32)
        smalT = kb.sb("smalT", [16, 16], F32)
        zz16 = kb.sb("zz16", [16, 128], F32)
        fkr_tm = kb.sb("fkr_tm", [128, NCA * 16], F32)
        etot_b = kb.sb("etot_b", [128, NCA * 16], F32)
        S_a = kb.sb("S_a", [128, HA * 128], F32)
        S_ab = kb.sb("S_ab", [128, HA * 128], BF16)
        ksbt_sb = kb.sb("ksbt_sb", [128, 32], F32)
        SM_S, SM_TOT, SM_ETOT, SM_NEGS = 0, 16, 32, 48

        def init_g(g):
            g.memset(ones_r_bf[0:1, :], 1.0)
            g.memset(ones_r_f[0:1, :], 1.0)
            g.memset(ones_c_bf[:, 0:1], 1.0)
            g.memset(zz16[:, :], 0.0)
            for v, i in CP.items():
                g.memset(cpool[:, i:i + 1], float(v))
            a = Strm(kb.aux)
            a.dma(g, ident[:, :], identm[:, :])
            a.dma(g, nwm_sb[:, :], normwm[:, :])
            a.dma(g, bqm_sb[:, 0:16], biasm[0])
            a.dma(g, bqm_sb[:, 16:32], biasm[1])
            a.dma(g, brow_sb[0:1, :], biasrow.rearrange("a b -> (a b)"))
            a.dma(g, wfm_sb[:, :], wfm[:, :])
            a.dma(g, wrm_sb[:, :], wrm[:, :])
            a.dma(g, ksbt_sb[:, :], ksbtm[:, :])
            g.wait_ge(a.s.h, a.s.n)

        kb.phase("init", gpsimd=init_g)

        wc_done = {}

        def w0_g(g):
            a = Strm(kb.aux)
            for k in ["wq", "wk", "wv", "wo", "ws1", "ws3", "we1", "we3",
                      "we2", "ws2", "bwq", "bwk", "bwv"]:
                src, dst = wf32[k], wbfd[k]
                n = int(np.prod(wshapes[k]))
                fs, fd = src.rearrange("a b -> (a b)"), dst.rearrange("a b -> (a b)")
                CH = 1 << 20
                for o in range(0, n, CH):
                    a.dma_batched(g, fd[o:min(n, o + CH)], fs[o:min(n, o + CH)])
                a.drain(g)
                wc_done[k] = kb.aux.n

        kb.phase("w0", gpsimd=w0_g)

        SLOTW = 8192

        def load_half(q, st, wt, kdim, c0, colw, slot):
            kch = kdim // 128
            assert kch * colw <= SLOTW
            ap = wt[:, c0:c0 + colw].rearrange("(k p) n -> p k n", p=128)
            step = max(1, kch // 4)
            for ks in range(0, kch, step):
                ke = min(kch, ks + step)
                st.dma(q, wbuf[:, slot * SLOTW + ks * colw:slot * SLOTW + ke * colw],
                       ap[:, ks:ke, :])

        def wtile(slot, kchi, colw, c0, w=128):
            o = slot * SLOTW + kchi * colw + c0
            return wbuf[:, o:o + w]

        def ln_phase(tag, b, src):
            with ExitStack() as ps:
                xtile = ps.enter_context(nc.sbuf_tensor(f"xt{tag}", [128, 2 * 512], F32))
                ntmp = ps.enter_context(nc.sbuf_tensor(f"ntp{tag}", [128, 512], F32))
                xbf = wbuf[:, 2 * SLOTW:2 * SLOTW + 8192]
                x2 = wbuf[:, 3 * SLOTW:3 * SLOTW + 8192]
                srow = ps.enter_context(nc.sbuf_tensor(f"srw{tag}", [1, 4 * 512], F32))
                Sx = kb.qstrm()
                Scast, Ssq, Smm, Ssrow, Svar, Srstd, Snb, Sbc, Snorm = (
                    kb.strm() for _ in range(9))

                def f_sync(sy):
                    for nt in range(NT):
                        for dch in range(NCH):
                            i = nt * NCH + dch
                            if i >= 2:
                                Scast.wait(sy, i - 1)
                                Ssq.wait(sy, i - 1)
                            Sx.dma(sy, xtile[:, (i % 2) * 512:(i % 2) * 512 + 512],
                                   src[b, dch * 128:(dch + 1) * 128,
                                       nt * 512:(nt + 1) * 512])

                def f_vec(v):
                    for nt in range(NT):
                        for dch in range(NCH):
                            i = nt * NCH + dch
                            Sx.wdma(v, i)
                            sl = slice((i % 2) * 512, (i % 2) * 512 + 512)
                            Scast.inc(v.tensor_copy(
                                xbf[:, dch * 512:(dch + 1) * 512], xtile[:, sl]))
                        Ssrow.wait(v, nt + 1)
                        Svar.inc(v.tensor_sub(srow[0:1, 512:1024],
                                              srow[0:1, 512:1024],
                                              srow[0:1, 2 * 512:3 * 512]))
                        Srstd.wait(v, nt + 1)
                        Snb.inc(v.scalar_tensor_tensor(
                            srow[0:1, 3 * 512:4 * 512], srow[0:1, 0:512], -1.0,
                            srow[0:1, 2 * 512:3 * 512], ALU.mult, ALU.mult))
                        Sbc.wait(v, nt + 1)
                        for dch in range(NCH):
                            v.tensor_tensor(ntmp[:, 0:512],
                                            xbf[:, dch * 512:(dch + 1) * 512],
                                            P[6][:, 0:512], ALU.mult)
                            o = dch * T + nt * 512
                            li = v.tensor_tensor(xln[:, o:o + 512], ntmp[:, 0:512],
                                                 P[7][:, 0:512], ALU.add)
                        Snorm.inc(li)

                def f_scal(s):
                    for nt in range(NT):
                        for dch in range(NCH):
                            i = nt * NCH + dch
                            Sx.wdma(s, i)
                            sl = slice((i % 2) * 512, (i % 2) * 512 + 512)
                            Ssq.inc(s.activation(x2[:, dch * 512:(dch + 1) * 512],
                                                 xtile[:, sl], AF.Square))
                        Smm.wait(s, 2 * (nt + 1))
                        if nt > 0:
                            Snb.wait(s, nt)
                            Sbc.wait(s, nt)
                        s.mul(srow[0:1, 0:512], P[4][0:1, 0:512], 1.0 / 2048.0)
                        s.mul(srow[0:1, 512:1024], P[5][0:1, 0:512], 1.0 / 2048.0)
                        Ssrow.inc(s.activation(srow[0:1, 2 * 512:3 * 512],
                                               srow[0:1, 0:512], AF.Square))
                        Svar.wait(s, nt + 1)
                        s.activation(srow[0:1, 2 * 512:3 * 512],
                                     srow[0:1, 512:1024], AF.Ln,
                                     bias=cpool[0:1, CP[1e-5]:CP[1e-5] + 1])
                        Srstd.inc(s.activation(
                            srow[0:1, 2 * 512:3 * 512], srow[0:1, 2 * 512:3 * 512],
                            AF.Exp, scale=cpool[0:1, CP[-0.5]:CP[-0.5] + 1]))

                def f_tens(t):
                    for nt in range(NT):
                        Scast.wait(t, NCH * (nt + 1))
                        Ssq.wait(t, NCH * (nt + 1))
                        if nt > 0:
                            Ssrow.wait(t, nt)
                        for dch in range(NCH):
                            li = t.matmul(P[4][0:1, 0:512], ones_c_bf[:, 0:1],
                                          xbf[:, dch * 512:(dch + 1) * 512],
                                          start=(dch == 0), stop=(dch == NCH - 1))
                        Smm.inc(li)
                        for dch in range(NCH):
                            li = t.matmul(P[5][0:1, 0:512], ones_c_bf[:, 0:1],
                                          x2[:, dch * 512:(dch + 1) * 512],
                                          start=(dch == 0), stop=(dch == NCH - 1))
                        Smm.inc(li)
                        Snb.wait(t, nt + 1)
                        if nt > 0:
                            Snorm.wait(t, nt)
                        t.matmul(P[6][:, 0:512], ones_r_f[0:1, :],
                                 srow[0:1, 2 * 512:3 * 512], start=True, stop=True)
                        Sbc.inc(t.matmul(P[7][:, 0:512], ones_r_f[0:1, :],
                                         srow[0:1, 3 * 512:4 * 512],
                                         start=True, stop=True))

                kb.phase(f"ln{tag}", sync=f_sync, vector=f_vec, scalar=f_scal,
                         tensor=f_tens)

        def a_ld_phase(b):
            with ExitStack() as ps:
                wfree = wbuf[:, 0:32768].bitcast(F32)
                ldq = [ps.enter_context(nc.sbuf_tensor(f"ldq{i}_{b}", [16, T], F32))
                       for i in range(2)]
                ldq += [wfree[0:16, i * T:(i + 1) * T] for i in range(4)]
                LD, CUM, FQS, FKS, FQI, FKR = range(6)
                Smm = kb.strm()
                Sz = kb.strm()
                Sls = kb.strm()
                Ssm = kb.strm()
                Sfq = kb.strm()
                Str = kb.strm()
                Sev = kb.strm()
                Srow = kb.dstrm()
                Sbc2 = kb.strm()
                Sebc = kb.strm()
                Sst = kb.dstrm()

                def f_tens(t):
                    for nt in range(NT):
                        if nt >= 2:
                            Sz.wait(t, nt - 1)
                        for kch in range(NCH):
                            li = t.matmul(P[nt % 2][0:16, 0:512],
                                          wfm_sb[:, kch * 16:(kch + 1) * 16],
                                          xln[:, kch * T + nt * 512:
                                              kch * T + nt * 512 + 512],
                                          start=(kch == 0), stop=(kch == NCH - 1))
                        Smm.inc(li)
                    Sfq.wait(t, NCA + 1)
                    for c in range(NCA):
                        if c >= 2:
                            Sev.wait(t, c - 1)
                        Str.inc(t.transpose(P[2 + c % 2][0:128, 0:16],
                                              ldq[FKR][:, c * 128:(c + 1) * 128],
                                              ident[0:16, 0:16]))
                    Sev.wait(t, NCA)
                    Str.inc(t.transpose(P[2][0:16, 0:16],
                                         smal[:, SM_ETOT:SM_ETOT + 16],
                                         ident[0:16, 0:16]))
                    for c in range(NCA):
                        Srow.wdma(t, c)
                        if c >= 2:
                            Sebc.wait(t, c - 1)
                        Sbc2.inc(t.matmul(P[4 + c % 2][:, 0:16], ones_r_f[0:1, :],
                                           stage[0:1, (c % 2) * 512:
                                                 (c % 2) * 512 + 16],
                                           start=True, stop=True))

                def f_scal(s):
                    for nt in range(NT):
                        Smm.wait(s, nt + 1)
                        sl = slice(nt * 512, (nt + 1) * 512)
                        Sz.inc(s.copy(ldq[LD][:, sl], P[nt % 2][0:16, 0:512]))
                    s.activation(ldq[FQS][:, :], ldq[LD][:, :], AF.Abs)
                    s.activation(ldq[FQS][:, :], ldq[FQS][:, :], AF.Exp,
                                 scale=cpool[0:16, CP[-1.0]:CP[-1.0] + 1])
                    Sls.inc(s.activation(ldq[FQS][:, :], ldq[FQS][:, :], AF.Ln,
                                         bias=cpool[0:16, CP[1.0]:CP[1.0] + 1]))
                    Ssm.wait(s, 1)
                    s.activation(ldq[FQI][:, :], ldq[CUM][:, :], AF.Exp)
                    Sfq.inc(s.activation(smal[:, SM_ETOT:SM_ETOT + 16],
                                         smal[:, SM_TOT:SM_TOT + 16], AF.Exp))
                    for c in range(NCA):
                        sl = slice(c * 128, (c + 1) * 128)
                        s.activation(ldq[FQS][:, sl], ldq[CUM][:, sl], AF.Exp,
                                     bias=smal[:, SM_NEGS + c:SM_NEGS + c + 1])
                        s.activation(ldq[FKS][:, sl], ldq[CUM][:, sl], AF.Exp,
                                     scale=cpool[0:16, CP[-1.0]:CP[-1.0] + 1],
                                     bias=smal[:, SM_S + c:SM_S + c + 1])
                        Sfq.inc(s.activation(
                            ldq[FKR][:, sl], ldq[CUM][:, sl], AF.Exp,
                            scale=cpool[0:16, CP[-1.0]:CP[-1.0] + 1],
                            bias=smal[:, SM_TOT + c:SM_TOT + c + 1]))

                def f_vec(v):
                    Sls.wait(v, 1)
                    v.tensor_scalar_min(ldq[FKS][:, :], ldq[LD][:, :], 0.0)
                    v.tensor_sub(ldq[LD][:, :], ldq[FKS][:, :], ldq[FQS][:, :])
                    for c in range(NCA):
                        sl = slice(c * 128, (c + 1) * 128)
                        v.tensor_tensor_scan(ldq[CUM][:, sl], ldq[LD][:, sl],
                                             zz16[:, :], 0.0, ALU.add, ALU.add)
                    v.tensor_copy(smal[:, SM_S:SM_S + 16], ldq[CUM][:, 63:T:128])
                    v.tensor_copy(smal[:, SM_TOT:SM_TOT + 16], ldq[CUM][:, 127:T:128])
                    Ssm.inc(v.tensor_scalar_mul(smal[:, SM_NEGS:SM_NEGS + 16],
                                                smal[:, SM_S:SM_S + 16], -1.0))
                    for c in range(NCA):
                        Str.wait(v, c + 1)
                        Sev.inc(v.tensor_copy(fkr_tm[:, c * 16:(c + 1) * 16],
                                              P[2 + c % 2][0:128, 0:16]))
                    Str.wait(v, NCA + 1)
                    Sev.inc(v.tensor_copy(smalT[:, :], P[2][0:16, 0:16]))
                    for c in range(NCA):
                        Sbc2.wait(v, c + 1)
                        Sebc.inc(v.tensor_copy(etot_b[:, c * 16:(c + 1) * 16],
                                               P[4 + c % 2][:, 0:16]))

                def f_sync(sy):
                    Sfq.wait(sy, NCA + 1)
                    Sst.dma(sy, facd[0], ldq[FQS][:, :])
                    Sst.dma(sy, facd[1], ldq[FKS][:, :])
                    Sst.dma(sy, facd[2], ldq[FQI][:, :])
                    Sev.wait(sy, NCA + 1)
                    Sst.dma(sy, rowd[:, :], smalT[:, :])
                    Sst.waitall(sy)
                    for c in range(NCA):
                        if c >= 2:
                            Sbc2.wait(sy, c - 1)
                        Srow.dma(sy, stage[0:1, (c % 2) * 512:(c % 2) * 512 + 16],
                                 rowd[c:c + 1, :])
                kb.phase(f"ald{b}", tensor=f_tens, scalar=f_scal, vector=f_vec,
                         sync=f_sync)

        def proj_fm(tag, wname, kind, slabs):
            with ExitStack() as ps:
                tmp = ps.enter_context(nc.sbuf_tensor(f"tp{tag}", [128, 4 * 512], F32))
                osb = ps.enter_context(nc.sbuf_tensor(f"os{tag}", [128, 4 * 512], BF16))
                Swl, Sst, Srow = kb.qstrm(), kb.qstrm(), kb.qstrm()
                Smm, Ssil, Sev, Sbc = (kb.strm() for _ in range(4))
                nfac = 2 if kind == "qd2" else (1 if kind == "kd" else 0)
                fmap = [0, 2] if kind == "qd2" else [1]

                def f_sync(sy):
                    sy.wait_ge(kb.aux.h, wc_done[wname])
                    load_half(sy, Swl, wbfd[wname], D, 0, 512, 0)
                    rows_done = [0]

                    def rows(hi):
                        for i in range(rows_done[0], min(64 * nfac, hi)):
                            if i >= 2:
                                Sbc.wait(sy, i - 1)
                            ti, f = i // nfac, i % nfac
                            m, nt = ti // NT, ti % NT
                            Srow.dma(sy,
                                     stage[0:1, (i % 2) * 512:(i % 2) * 512 + 512],
                                     facd[fmap[f], m:m + 1,
                                          nt * 512:(nt + 1) * 512])
                        rows_done[0] = max(rows_done[0], min(64 * nfac, hi))
                    if nfac:
                        rows(2)
                    load_half(sy, Swl, wbfd[wname], D, 512, 512, 1)
                    for ti in range(64):
                        if ti >= 2:
                            store_tile(sy, ti - 2)
                        if nfac:
                            rows((ti + 3) * nfac)
                        if ti % 16 == 15 and ti // 16 + 2 < 4:
                            p = ti // 16 + 2
                            Smm.wait(sy, 16 * (p - 1))
                            load_half(sy, Swl, wbfd[wname], D, p * 512, 512, p % 2)
                    store_tile(sy, 62)
                    store_tile(sy, 63)
                    Sst.waitall(sy)
                    Swl.waitall(sy)
                    if nfac:
                        Srow.waitall(sy)

                def f_tens(t):
                    for piece in range(4):
                        for m4 in range(4):
                            m = piece * 4 + m4
                            for nt in range(NT):
                                ti = m * NT + nt
                                bank = ti % 6
                                if ti >= 6:
                                    Ssil.wait(t, ti - 5)
                                for kch in range(NCH):
                                    if m4 == 0 and nt == 0 and kch % 4 == 0:
                                        Swl.wdma(t, 4 * piece + kch // 4)
                                    li = t.matmul(P[bank][:, 0:512],
                                                  wtile(piece % 2, kch, 512, m4 * 128),
                                                  xln[:, kch * T + nt * 512:
                                                      kch * T + nt * 512 + 512],
                                                  start=(kch == 0),
                                                  stop=(kch == NCH - 1))
                                Smm.inc(li)
                                for f in range(nfac):
                                    i = ti * nfac + f
                                    Srow.wdma(t, i)
                                    if i >= 2:
                                        Sev.wait(t, i - 1)
                                    Sbc.inc(t.matmul(
                                        P[6 + i % 2][:, 0:512], ones_r_f[0:1, :],
                                        stage[0:1, (i % 2) * 512:
                                              (i % 2) * 512 + 512],
                                        start=True, stop=True))

                def f_scal(s):
                    for ti in range(64):
                        bank = ti % 6
                        Smm.wait(s, ti + 1)
                        if nfac:
                            if ti >= 4:
                                Sev.wait(s, (ti - 3) * nfac)
                            Ssil.inc(s.activation(
                                tmp[:, (ti % 4) * 512:(ti % 4) * 512 + 512],
                                P[bank][:, 0:512], AF.Silu))
                        else:
                            if ti >= 4:
                                Sst.wdma(s, ti - 4)
                            m = ti // NT
                            bc = (0 if kind == "biasq" else 16) + m
                            Ssil.inc(s.activation(
                                osb[:, (ti % 4) * 512:(ti % 4) * 512 + 512],
                                P[bank][:, 0:512], AF.Identity,
                                bias=bqm_sb[:, bc:bc + 1]))

                def f_vec(v):
                    if not nfac:
                        return
                    for ti in range(64):
                        Ssil.wait(v, ti + 1)
                        if ti >= 2:
                            Sst.wdma(v, nfac * (ti - 1) - 1)
                        for f in range(nfac):
                            i = ti * nfac + f
                            Sbc.wait(v, i + 1)
                            Sev.inc(v.tensor_tensor(
                                osb[:, ((ti % 2) * 2 + f) * 512:
                                    ((ti % 2) * 2 + f) * 512 + 512],
                                tmp[:, (ti % 4) * 512:(ti % 4) * 512 + 512],
                                P[6 + i % 2][:, 0:512], ALU.mult))

                def store_tile(sy, ti):
                    m, nt = ti // NT, ti % NT
                    if nfac:
                        Sev.wait(sy, (ti + 1) * nfac)
                        for f in range(nfac):
                            Sst.dma(sy, slabs[f][4 * nt + m // 4][m % 4],
                                    osb[:, ((ti % 2) * 2 + f) * 512:
                                        ((ti % 2) * 2 + f) * 512 + 512])
                    else:
                        Ssil.wait(sy, ti + 1)
                        Sst.dma(sy, slabs[0][4 * nt + m // 4][m % 4],
                                osb[:, (ti % 4) * 512:(ti % 4) * 512 + 512])

                kb.phase(f"pfm{tag}", sync=f_sync, tensor=f_tens,
                         scalar=f_scal, vector=f_vec)

        def proj_tm(tag, wname, kind, slabo, browi=0, swap=False):
            with ExitStack() as ps:
                tmp = ps.enter_context(nc.sbuf_tensor(f"tp{tag}", [128, 4 * 512], F32))
                osb = ps.enter_context(nc.sbuf_tensor(f"os{tag}", [128, 4 * 512], BF16))
                Swl, Sst = kb.qstrm(), kb.qstrm()
                Smm, Ssil, Sev = (kb.strm() for _ in range(3))

                def f_sync(sy):
                    sy.wait_ge(kb.aux.h, wc_done[wname])
                    load_half(sy, Swl, wbfd[wname], D, 0, 512, 0)
                    load_half(sy, Swl, wbfd[wname], D, 512, 512, 1)
                    for ti in range(64):
                        ntd, tokch = ti // 16, ti % 16
                        if kind == "kdr":
                            Sev.wait(sy, ti + 1)
                        else:
                            Ssil.wait(sy, ti + 1)
                        dst_ = (slabo[4 * ntd + tokch // 4][tokch % 4] if swap
                                else slabo[tokch][ntd])
                        Sst.dma(sy, dst_,
                                osb[:, (ti % 4) * 512:(ti % 4) * 512 + 512])
                        if ti % 16 == 1 and 2 <= ti // 16 + 1 < 4:
                            p = ti // 16 + 1
                            load_half(sy, Swl, wbfd[wname], D, p * 512, 512, p % 2)
                    Sst.waitall(sy)
                    Swl.waitall(sy)

                def f_tens(t):
                    for ntd in range(4):
                        for tokch in range(16):
                            ti = ntd * 16 + tokch
                            bank = ti % 6
                            if ti >= 6:
                                if kind == "kdr":
                                    Sev.wait(t, ti - 5)
                                else:
                                    Ssil.wait(t, ti - 5)
                            for kch in range(NCH):
                                if tokch == 0 and kch % 4 == 0:
                                    Swl.wdma(t, 4 * ntd + kch // 4)
                                st = (kch == 0)
                                sp = (kch == NCH - 1) and kind != "btm"
                                li = t.matmul(P[bank][:, 0:512],
                                              xln[:, kch * T + tokch * 128:
                                                  kch * T + tokch * 128 + 128],
                                              wtile(ntd % 2, kch, 512, 0, 512),
                                              start=st, stop=sp)
                            if kind == "btm":
                                li = t.matmul(P[bank][:, 0:512], ones_r_bf[0:1, :],
                                              brow_sb[0:1, browi * D + ntd * 512:
                                                      browi * D + (ntd + 1) * 512],
                                              start=False, stop=True)
                            Smm.inc(li)

                def f_scal(s):
                    for ti in range(64):
                        bank = ti % 6
                        Smm.wait(s, ti + 1)
                        if kind == "kdr":
                            if ti >= 4:
                                Sev.wait(s, ti - 3)
                            Ssil.inc(s.activation(
                                tmp[:, (ti % 4) * 512:(ti % 4) * 512 + 512],
                                P[bank][:, 0:512], AF.Silu))
                        else:
                            if ti >= 4:
                                Sst.wdma(s, ti - 4)
                            Ssil.inc(s.activation(
                                osb[:, (ti % 4) * 512:(ti % 4) * 512 + 512],
                                P[bank][:, 0:512], AF.Identity))

                def f_vec(v):
                    if kind != "kdr":
                        return
                    for ti in range(64):
                        Ssil.wait(v, ti + 1)
                        if ti >= 4:
                            Sst.wdma(v, ti - 4)
                        ntd, tokch = ti // 16, ti % 16
                        for hh in range(4):
                            h = ntd * 4 + hh
                            li = v.tensor_scalar(
                                osb[:, (ti % 4) * 512 + hh * 128:
                                    (ti % 4) * 512 + (hh + 1) * 128],
                                tmp[:, (ti % 4) * 512 + hh * 128:
                                    (ti % 4) * 512 + (hh + 1) * 128],
                                fkr_tm[:, tokch * 16 + h:tokch * 16 + h + 1],
                                None, ALU.mult)
                        Sev.inc(li)

                kb.phase(f"ptm{tag}", sync=f_sync, tensor=f_tens,
                         scalar=f_scal, vector=f_vec)


        def a_chunk_phase(b):
            with ExitStack() as ps:
                fmb = xarena[:, 0:24576]
                tmb = xarena[:, 24576:32768]
                scf = ps.enter_context(nc.sbuf_tensor(f"scf{b}", [128, 2 * 128], F32))
                scm = ps.enter_context(nc.sbuf_tensor(f"scm{b}", [128, 2 * 128], BF16))
                o_buf = ps.enter_context(nc.sbuf_tensor(f"obf{b}", [128, 16 * 128], F32))
                o2b = ps.enter_context(nc.sbuf_tensor(f"o2b{b}", [128, 16 * 128], BF16))
                onb = wbuf[:, 4 * SLOTW:5 * SLOTW]
                rrow = ps.enter_context(nc.sbuf_tensor(f"rr{b}", [1, 128], F32))
                ntp = ps.enter_context(nc.sbuf_tensor(f"ntp{b}", [128, 2 * 128], F32))
                xres = ps.enter_context(nc.sbuf_tensor(f"xr{b}", [128, 2 * 512], F32))
                xout = ps.enter_context(nc.sbuf_tensor(f"xo{b}", [128, 2 * 512], F32))

                Sfm, Stm, Swo, Sxr, Sxw = (kb.qstrm() for _ in range(5))
                Smm1, Ses, Seg, Smm2, Smm3 = (kb.strm() for _ in range(5))
                Sov, Svu, Ssq2, Smm4, Srs, Sbc3, Sno = (kb.strm() for _ in range(7))
                Smm5, Svr, Sz0 = (kb.strm() for _ in range(3))

                def fmtile(tno, h, c):
                    o = (tno * 16 + h) * 512 + (c % 4) * 128
                    return fmb[:, o:o + 128]

                def tmtile(tno, c, h):
                    o = (c % 2) * 2 * 2048 + tno * 2048 + (h // 4) * 512 + (h % 4) * 128
                    return tmb[:, o:o + 128]

                def f_sync(sy):
                    for g in range(4):
                        if g >= 1:
                            Smm3.wait(sy, 64 * g)
                        if g == 0:
                            for qh in range(4):
                                for tno, sl in enumerate([sA, sB, sC]):
                                    Sfm.dma(sy, fmb[:, tno * 8192 + qh * 2048:
                                                    tno * 8192 + qh * 2048 + 2048],
                                            sl[qh].rearrange("b p c -> p b c"))
                        else:
                            for tno, sl in enumerate([sA, sB, sC]):
                                Sfm.dma(sy, fmb[:, tno * 16 * 512:(tno + 1) * 16 * 512],
                                        sl[4 * g:4 * g + 4].rearrange("a b p c -> p a b c"))
                        if g == 0:
                            for c in (0, 1):
                                for tno, sl in enumerate([sD, sE]):
                                    o = (c % 2) * 4096 + tno * 2048
                                    Stm.dma(sy, tmb[:, o:o + 2048],
                                            sl[c].rearrange("b p c -> p b c"))
                            sy.wait_ge(kb.aux.h, wc_done["wo"])
                            for p in range(4):
                                load_half(sy, Swo, wbfd["wo"], D, p * 512, 512, p)
                        for c in range(4 * g, 4 * g + 4):
                            cc = c + 2
                            if cc < NCA:
                                Smm3.wait(sy, 16 * (cc - 1))
                                for tno, sl in enumerate([sD, sE]):
                                    o = (cc % 2) * 4096 + tno * 2048
                                    Stm.dma(sy, tmb[:, o:o + 2048],
                                            sl[cc].rearrange("b p c -> p b c"))
                        for dc in range(16):
                            i = 16 * g + dc
                            if i >= 2:
                                Svr.wait(sy, i - 1)
                            Sxr.dma(sy, xres[:, (dc % 2) * 512:(dc % 2) * 512 + 512],
                                    xin[b, dc * 128:(dc + 1) * 128,
                                        g * 512:(g + 1) * 512])
                            if i >= 1:
                                Svr.wait(sy, i)
                                pdc = (dc - 1) % 16
                                pg = g if dc >= 1 else g - 1
                                Sxw.dma(sy, xw[b, pdc * 128:(pdc + 1) * 128,
                                               pg * 512:(pg + 1) * 512],
                                        xout[:, (pdc % 2) * 512:(pdc % 2) * 512 + 512])
                    Svr.wait(sy, 64)
                    Sxw.dma(sy, xw[b, 15 * 128:16 * 128, 3 * 512:4 * 512],
                            xout[:, 512:1024])
                    Sxw.waitall(sy)
                    Sfm.waitall(sy)
                    Stm.waitall(sy)
                    Swo.waitall(sy)
                    Sxr.waitall(sy)

                def f_gp(g_):
                    g_.memset(S_a[:, :], 0.0)
                    Sz0.inc(g_.memset(S_ab[:, :], 0.0))
                    for c in range(NCA):
                        for h in range(HA):
                            i = c * 16 + h
                            Ses.wait(g_, i + 1)
                            Seg.inc(g_.affine_select(
                                scm[:, (h % 2) * 128:(h % 2) * 128 + 128],
                                scf[:, (h % 2) * 128:(h % 2) * 128 + 128],
                                [[1, 128]], ALU.is_ge, 0.0, base=0,
                                channel_multiplier=-1))

                def f_tens(t):
                    Sz0.wait(t, 1)
                    for c in range(NCA):
                        g = c // 4
                        if g >= 1:
                            Sfm.wdma(t, 12 + 3 * g - 1)
                        Stm.wdma(t, 2 * (c + 1) - 1)
                        for h in range(HA):
                            if c == 0 and h % 4 == 0:
                                Sfm.wdma(t, 3 * (h // 4) + 2)
                            i = c * 16 + h
                            if i >= 2:
                                Ses.wait(t, i - 1)
                            Smm1.inc(t.matmul(P[h % 2][:, 0:128], fmtile(2, h, c),
                                               fmtile(0, h, c), start=True, stop=True))
                            Seg.wait(t, i + 1)
                            if c > 0:
                                Svu.wait(t, (c - 1) * 16 + h + 1)
                            if i >= 2:
                                Sov.wait(t, i - 1)
                            t.matmul(P[2 + h % 2][:, 0:128], tmtile(1, c, h),
                                     scm[:, (h % 2) * 128:(h % 2) * 128 + 128],
                                     start=True, stop=False)
                            Smm2.inc(t.matmul(P[2 + h % 2][:, 0:128],
                                               S_ab[:, h * 128:(h + 1) * 128],
                                               fmtile(1, h, c), start=False, stop=True))
                            if i >= 2:
                                Svu.wait(t, i - 1)
                            Smm3.inc(t.matmul(P[4 + h % 2][:, 0:128], tmtile(0, c, h),
                                               tmtile(1, c, h), start=True, stop=True))
                        Ssq2.wait(t, 16 * (c + 1))
                        if c > 0:
                            Srs.wait(t, c)
                        if c % 4 == 0 and c > 0:
                            Svr.wait(t, 16 * (c // 4))
                        for h in range(HA):
                            li = t.matmul(P[6][0:1, 0:128], ones_c_bf[:, 0:1],
                                          o2b[:, h * 128:(h + 1) * 128],
                                          start=(h == 0), stop=(h == HA - 1))
                        Smm4.inc(li)
                        Srs.wait(t, c + 1)
                        if c > 0:
                            Sno.wait(t, 16 * c)
                        Sbc3.inc(t.matmul(P[7][:, 0:128], ones_r_f[0:1, :],
                                           rrow[0:1, :], start=True, stop=True))
                        if c % 4 == 3:
                            g = c // 4
                            Sno.wait(t, 64 * (g + 1))
                            Swo.waitall(t)
                            for dc in range(16):
                                i = 16 * g + dc
                                if i >= 2:
                                    Svr.wait(t, i - 1)
                                for kch in range(NCH):
                                    li = t.matmul(P[6 + dc % 2][:, 0:512],
                                                  wtile(dc // 4, kch, 512,
                                                        (dc % 4) * 128),
                                                  onb[:, kch * 512:(kch + 1) * 512],
                                                  start=(kch == 0),
                                                  stop=(kch == NCH - 1))
                                Smm5.inc(li)

                def f_scal(s):
                    for c in range(NCA):
                        for h in range(HA):
                            i = c * 16 + h
                            Smm1.wait(s, i + 1)
                            if i >= 2:
                                Seg.wait(s, i - 1)
                            Ses.inc(s.copy(scf[:, (h % 2) * 128:(h % 2) * 128 + 128],
                                           P[h % 2][:, 0:128]))
                        Sov.wait(s, 16 * (c + 1))
                        for h in range(HA):
                            Ssq2.inc(s.activation(o2b[:, h * 128:(h + 1) * 128],
                                                  o_buf[:, h * 128:(h + 1) * 128],
                                                  AF.Square))
                        Smm4.wait(s, c + 1)
                        s.copy(rrow[0:1, :], P[6][0:1, 0:128])
                        s.activation(rrow[0:1, :], rrow[0:1, :], AF.Ln,
                                     scale=cpool[0:1, CP[1.0 / 2048.0]:
                                                 CP[1.0 / 2048.0] + 1],
                                     bias=cpool[0:1, CP[1e-6]:CP[1e-6] + 1])
                        Srs.inc(s.activation(rrow[0:1, :], rrow[0:1, :], AF.Exp,
                                             scale=cpool[0:1, CP[-0.5]:CP[-0.5] + 1]))

                def f_vec(v):
                    Sz0.wait(v, 1)
                    for c in range(NCA):
                        g = c // 4
                        if c >= 1:
                            Ssq2.wait(v, 16 * c)
                        for h in range(HA):
                            i = c * 16 + h
                            Smm2.wait(v, i + 1)
                            Sov.inc(v.tensor_copy(o_buf[:, h * 128:(h + 1) * 128],
                                                  P[2 + h % 2][:, 0:128]))
                            Smm3.wait(v, i + 1)
                            v.scalar_tensor_tensor(
                                S_a[:, h * 128:(h + 1) * 128],
                                S_a[:, h * 128:(h + 1) * 128],
                                etot_b[:, c * 16 + h:c * 16 + h + 1],
                                P[4 + h % 2][:, 0:128], ALU.mult, ALU.add)
                            Svu.inc(v.tensor_copy(S_ab[:, h * 128:(h + 1) * 128],
                                                  S_a[:, h * 128:(h + 1) * 128]))
                        Sbc3.wait(v, c + 1)
                        if c >= 4:
                            Smm5.wait(v, 16 * (c // 4))
                        for dch in range(16):
                            v.tensor_tensor(ntp[:, (dch % 2) * 128:(dch % 2) * 128 + 128],
                                            o_buf[:, dch * 128:(dch + 1) * 128],
                                            P[7][:, 0:128], ALU.mult)
                            Sno.inc(v.tensor_scalar(
                                onb[:, dch * 512 + (c % 4) * 128:
                                    dch * 512 + (c % 4) * 128 + 128],
                                ntp[:, (dch % 2) * 128:(dch % 2) * 128 + 128],
                                nwm_sb[:, dch:dch + 1], None, ALU.mult))
                        if c % 4 == 3:
                            for dc in range(16):
                                i = 16 * g + dc
                                Smm5.wait(v, i + 1)
                                Sxr.wdma(v, i)
                                if i >= 2:
                                    Sxw.wdma(v, i - 2)
                                Svr.inc(v.tensor_tensor(
                                    xout[:, (dc % 2) * 512:(dc % 2) * 512 + 512],
                                    xres[:, (dc % 2) * 512:(dc % 2) * 512 + 512],
                                    P[6 + dc % 2][:, 0:512], ALU.add))

                kb.phase(f"ach{b}", sync=f_sync, gpsimd=f_gp, tensor=f_tens,
                         scalar=f_scal, vector=f_vec)


        def router_phase(tag, b):
            with ExitStack() as ps:
                cw = ps.enter_context(nc.sbuf_tensor(f"cwk{tag}", [128, 12 * 16], F32))
                cwt = ps.enter_context(nc.sbuf_tensor(f"cwt{tag}", [128, 16 * 16], F32))
                cwT = ps.enter_context(nc.sbuf_tensor(f"cwT{tag}", [16, T], F32))
                LG, LGE, MX, NMX, SM_, CUR, MASK, MI, PM, DEN, RD = range(11)

                def sl(i, w=16):
                    return cw[:, i * 16:i * 16 + w]

                def col(i):
                    return cw[:, i * 16:i * 16 + 1]

                Sst = kb.dstrm()
                Smm, Sv1, Ss1, Sv2, Str, Sv3 = (kb.strm() for _ in range(6))

                def f_tens(t):
                    def transpose_of(tc):
                        Sv2.wait(t, tc + 1)
                        if tc >= 2:
                            Sv3.wait(t, tc - 1)
                        Str.inc(t.transpose(P[4 + tc % 2][0:16, 0:128],
                                             cwt[:, tc * 16:(tc + 1) * 16],
                                             ident[:, :]))

                    for tc in range(16):
                        if tc >= 2:
                            Sv1.wait(t, tc - 1)
                        for kch in range(NCH):
                            li = t.matmul(P[tc % 2][:, 0:16],
                                          xln[:, kch * T + tc * 128:
                                              kch * T + tc * 128 + 128],
                                          wrm_sb[:, kch * 16:(kch + 1) * 16],
                                          start=(kch == 0), stop=(kch == NCH - 1))
                        Smm.inc(li)
                        if tc >= 1:
                            transpose_of(tc - 1)
                    transpose_of(15)

                def f_scal(s):
                    for tc in range(16):
                        Sv1.wait(s, tc + 1)
                        Ss1.inc(s.activation(sl(LGE), sl(LG), AF.Exp,
                                             bias=col(NMX)))

                def f_vec(v):
                    Svx = kb.strm()
                    nch_ = [0]

                    def ch(inst, strm=None):
                        if strm is not None:
                            strm.inc(inst)
                            v.wait_ge(strm.s.h, strm.s.n)
                        else:
                            nch_[0] += 1
                            Svx.inc(inst)
                            Svx.wait(v, nch_[0])

                    for tc in range(16):
                        Smm.wait(v, tc + 1)
                        ch(v.tensor_copy(sl(LG), P[tc % 2][:, 0:16]))
                        ch(v.reduce_max(col(MX), sl(LG), axis=AX.X))
                        ch(v.tensor_scalar_mul(col(NMX), col(MX), -1.0), Sv1)
                        Ss1.wait(v, tc + 1)
                        ch(v.reduce_sum(col(SM_), sl(LGE), axis=AX.X))
                        ch(v.reciprocal(col(RD), col(SM_)))
                        ch(v.tensor_scalar(sl(LG), sl(LGE), col(RD), None, ALU.mult))
                        ch(v.tensor_copy(sl(CUR), sl(LG)))
                        ch(v.tensor_scalar_mul(sl(MASK), sl(LG), 0.0))
                        for _ in range(4):
                            ch(v.reduce_max(col(MX), sl(CUR), axis=AX.X))
                            ch(v.tensor_scalar(sl(MI), sl(CUR), col(MX), None,
                                               ALU.is_ge))
                            ch(v.tensor_add(sl(MASK), sl(MASK), sl(MI)))
                            ch(v.scalar_tensor_tensor(sl(CUR), sl(MI), -2.0,
                                                      sl(CUR), ALU.mult, ALU.add))
                        ch(v.tensor_mul(sl(PM), sl(LG), sl(MASK)))
                        ch(v.reduce_sum(col(DEN), sl(PM), axis=AX.X))
                        ch(v.reciprocal(col(RD), col(DEN)))
                        ch(v.tensor_scalar_mul(col(RD), col(RD), 0.8))
                        ch(v.tensor_scalar(cwt[:, tc * 16:(tc + 1) * 16],
                                           sl(PM), col(RD), None, ALU.mult), Sv2)
                        Str.wait(v, tc + 1)
                        Sv3.inc(v.tensor_copy(cwT[0:16, tc * 128:(tc + 1) * 128],
                                              P[4 + tc % 2][0:16, 0:128]))

                def f_sync(sy):
                    Sv3.wait(sy, 16)
                    Sst.dma(sy, cwd[:, :], cwT[:, :])
                    Sst.waitall(sy)

                kb.phase(f"rt{tag}", tensor=f_tens, scalar=f_scal, vector=f_vec,
                         sync=f_sync)

        def g_gen_phase(tag, shared):
            with ExitStack() as ps:
                tmp = ps.enter_context(nc.sbuf_tensor(f"tpg{tag}", [128, 4 * 512], F32))
                tmp2 = ps.enter_context(nc.sbuf_tensor(f"t2g{tag}", [128, 2 * 512], F32))
                osb = ps.enter_context(nc.sbuf_tensor(f"osg{tag}", [128, 4 * 512], BF16))
                cwb = None if shared else ps.enter_context(
                    nc.sbuf_tensor(f"cwb{tag}", [128, 4 * 512], F32))
                Swl, Sst, Srw = kb.qstrm(), kb.qstrm(), kb.qstrm()
                Smm, Ssil, Sev, Sev2 = (kb.strm() for _ in range(4))
                Sbc4, Scw = (kb.strm() for _ in range(2))
                if shared:
                    Sev2 = Sev
                npc = 8 if shared else 32
                w1n, w3n = ("ws1", "ws3") if shared else ("we1", "we3")

                def wsrc(wn, q):
                    wt = wbfd[wn]
                    if shared:
                        return wt, q * 512
                    e, p = q // 2, q % 2
                    return wt[e * D:(e + 1) * D, :], p * 512

                def dst_of(gi):
                    qq, tt = gi // 16, gi % 16
                    m4, nt = tt // 4, tt % 4
                    if shared:
                        return gs_sl[nt][qq * 4 + m4]
                    return ge_sl[qq // 2][nt][(qq % 2) * 4 + m4]

                def f_sync(sy):
                    sy.wait_ge(kb.aux.h, max(wc_done[w1n], wc_done[w3n]))
                    for q in range(npc):
                        if q >= 2:
                            Smm.wait(sy, 16 * (q - 1))
                        t1, c1 = wsrc(w1n, q)
                        t3, c3 = wsrc(w3n, q)
                        load_half(sy, Swl, t1, D, c1, 512, (2 * q) % 4)
                        load_half(sy, Swl, t3, D, c3, 512, (2 * q + 1) % 4)
                        if not shared and q % 2 == 0:
                            e = q // 2
                            for nt in range(NT):
                                i = e * NT + nt
                                if i >= 2:
                                    Sbc4.wait(sy, i - 1)
                                Srw.dma(sy, stage[0:1, (i % 2) * 512:
                                                  (i % 2) * 512 + 512],
                                        cwd[e:e + 1, nt * 512:(nt + 1) * 512])
                        for ti in range(16):
                            gi = q * 16 + ti
                            if gi >= 4:
                                Sev2.wait(sy, gi - 3)
                                Sst.dma(sy, dst_of(gi - 4),
                                        osb[:, ((gi - 4) % 4) * 512:
                                            ((gi - 4) % 4) * 512 + 512])
                    for gi in range(npc * 16 - 4, npc * 16):
                        Sev2.wait(sy, gi + 1)
                        Sst.dma(sy, dst_of(gi),
                                osb[:, (gi % 4) * 512:(gi % 4) * 512 + 512])
                    Sst.waitall(sy)
                    Swl.waitall(sy)
                    if not shared:
                        Srw.waitall(sy)

                def f_tens(t):
                    for q in range(npc):
                        if not shared and q % 2 == 0:
                            e = q // 2
                            for nt in range(NT):
                                i = e * NT + nt
                                Srw.wdma(t, i)
                                if i >= 2:
                                    Scw.wait(t, i - 1)
                                Sbc4.inc(t.matmul(
                                    P[6 + i % 2][:, 0:512], ones_r_f[0:1, :],
                                    stage[0:1, (i % 2) * 512:(i % 2) * 512 + 512],
                                    start=True, stop=True))
                        for ti in range(16):
                            gi = q * 16 + ti
                            m4, nt = ti // 4, ti % 4
                            b1, b3 = gi % 3, 3 + gi % 3
                            if gi >= 3:
                                Ssil.wait(t, gi - 2)
                            for kch in range(NCH):
                                if ti == 0 and kch % 4 == 0:
                                    Swl.wdma(t, 8 * q + kch // 4)
                                t.matmul(P[b1][:, 0:512],
                                         wtile((2 * q) % 4, kch, 512, m4 * 128),
                                         xln[:, kch * T + nt * 512:
                                             kch * T + nt * 512 + 512],
                                         start=(kch == 0), stop=(kch == NCH - 1))
                            if gi >= 3:
                                Sev.wait(t, gi - 2)
                            for kch in range(NCH):
                                if ti == 0 and kch % 4 == 0:
                                    Swl.wdma(t, 8 * q + 4 + kch // 4)
                                li = t.matmul(P[b3][:, 0:512],
                                              wtile((2 * q + 1) % 4, kch, 512,
                                                    m4 * 128),
                                              xln[:, kch * T + nt * 512:
                                                  kch * T + nt * 512 + 512],
                                              start=(kch == 0),
                                              stop=(kch == NCH - 1))
                            Smm.inc(li)

                def f_scal(s):
                    for gi in range(npc * 16):
                        Smm.wait(s, gi + 1)
                        if gi >= 4:
                            Sev2.wait(s, gi - 3)
                        Ssil.inc(s.activation(tmp[:, (gi % 4) * 512:(gi % 4) * 512 + 512],
                                              P[gi % 3][:, 0:512], AF.Silu))

                def f_vec(v):
                    for gi in range(npc * 16):
                        q, ti = gi // 16, gi % 16
                        nt = ti % 4
                        b3 = 3 + gi % 3
                        if not shared and ti == 0 and q % 2 == 0:
                            e = q // 2
                            for i in range(e * NT, (e + 1) * NT):
                                Sbc4.wait(v, i + 1)
                                Scw.inc(v.tensor_copy(
                                    cwb[:, (i % 4) * 512:(i % 4) * 512 + 512],
                                    P[6 + i % 2][:, 0:512]))
                        Smm.wait(v, gi + 1)
                        if gi >= 4:
                            Sst.wdma(v, gi - 4)
                        if shared:
                            Ssil.wait(v, gi + 1)
                            li = v.scalar_tensor_tensor(
                                osb[:, (gi % 4) * 512:(gi % 4) * 512 + 512],
                                tmp[:, (gi % 4) * 512:(gi % 4) * 512 + 512], 0.2,
                                P[b3][:, 0:512], ALU.mult, ALU.mult)
                            Sev.inc(li)
                        else:
                            Sev.inc(v.tensor_tensor(
                                tmp2[:, (gi % 2) * 512:(gi % 2) * 512 + 512],
                                P[b3][:, 0:512],
                                cwb[:, nt * 512:(nt + 1) * 512], ALU.mult))
                            Ssil.wait(v, gi + 1)
                            Sev2.inc(v.tensor_tensor(
                                osb[:, (gi % 4) * 512:(gi % 4) * 512 + 512],
                                tmp[:, (gi % 4) * 512:(gi % 4) * 512 + 512],
                                tmp2[:, (gi % 2) * 512:(gi % 2) * 512 + 512],
                                ALU.mult))

                kb.phase(f"gg{tag}", sync=f_sync, tensor=f_tens, scalar=f_scal,
                         vector=f_vec)

        def f_out_phase(tag, b, dst):
            with ExitStack() as ps:
                gbuf = ps.enter_context(nc.sbuf_tensor(f"gb{tag}", [128, 2 * 4096], BF16))
                xres = ps.enter_context(nc.sbuf_tensor(f"xrf{tag}", [128, 2 * 512], F32))
                xout = ps.enter_context(nc.sbuf_tensor(f"xof{tag}", [128, 2 * 512], F32))
                Swl, Sgl, Sxr, Sxw = (kb.qstrm() for _ in range(4))
                Smm, Sev = (kb.strm() for _ in range(2))
                WUP = 80

                def f_sync(sy):
                    sy.wait_ge(kb.aux.h, max(wc_done["ws2"], wc_done["we2"]))
                    for pi in range(8):
                        half, nt = pi // 4, pi % 4
                        mb = pi * 160
                        for qq in range(2):
                            if pi > 0:
                                Smm.wait(sy, (pi - 1) * 160 + (136 if qq == 0 else 152))
                            c0 = half * 1024 + qq * 512
                            load_half(sy, Swl, wbfd["ws2"][0:2048, :], 2048, c0,
                                      512, qq * 2)
                            load_half(sy, Swl, wbfd["ws2"][2048:4096, :], 2048, c0,
                                      512, qq * 2 + 1)
                        for L in range(20):
                            if pi > 0 or L >= 2:
                                Smm.wait(sy, mb + 8 * (L - 1))
                            if L < 4:
                                src_ = gs_sl[nt][8 * L:8 * L + 8]
                            else:
                                src_ = ge_sl[L - 4][nt]
                            Sgl.dma(sy, gbuf[:, (L % 2) * 4096:(L % 2) * 4096 + 4096],
                                    src_.rearrange("b p c -> p b c"))
                            if L >= 4:
                                e = L - 3
                                if e < NE:
                                    if e >= 5:
                                        Smm.wait(sy, mb + 32 + 8 * (e - 4))
                                    else:
                                        Smm.wait(sy, mb + 32)
                                    load_half(sy, Swl,
                                              wbfd["we2"][e * HID:(e + 1) * HID, :],
                                              HID, half * 1024, 1024, (4 + e) % 5)
                            if L == 0:
                                if pi > 0:
                                    Smm.wait(sy, pi * 160)
                                load_half(sy, Swl, wbfd["we2"][0:HID, :],
                                          HID, half * 1024, 1024, 4)
                        for dc8 in range(8):
                            i = pi * 8 + dc8
                            dch = half * 8 + dc8
                            if i >= 2:
                                Sev.wait(sy, i - 1)
                            Sxr.dma(sy, xres[:, (i % 2) * 512:(i % 2) * 512 + 512],
                                    xw[b, dch * 128:(dch + 1) * 128,
                                       nt * 512:(nt + 1) * 512])
                            if i >= 1:
                                Sev.wait(sy, i)
                                pi2, pd = (i - 1) // 8, (i - 1) % 8
                                pdch = (pi2 // 4) * 8 + pd
                                pnt = pi2 % 4
                                Sxw.dma(sy, dst[b, pdch * 128:(pdch + 1) * 128,
                                                pnt * 512:(pnt + 1) * 512],
                                        xout[:, ((i - 1) % 2) * 512:
                                             ((i - 1) % 2) * 512 + 512])
                    Sev.wait(sy, 64)
                    Sxw.dma(sy, dst[b, 15 * 128:16 * 128, 3 * 512:4 * 512],
                            xout[:, 512:1024])
                    Sxw.waitall(sy)
                    Sgl.waitall(sy)
                    Swl.waitall(sy)
                    Sxr.waitall(sy)

                def f_tens(t):
                    for pi in range(8):
                        half, nt = pi // 4, pi % 4
                        wb = WUP * pi
                        if pi >= 1:
                            Sev.wait(t, 8 * pi)
                        for j in range(160):
                            Lj = j // 8
                            if j % 8 == 0:
                                Sgl.wdma(t, pi * 20 + Lj)
                            rhs = gbuf[:, (Lj % 2) * 4096 + (j % 8) * 512:
                                       (Lj % 2) * 4096 + (j % 8) * 512 + 512]
                            if j < 32:
                                kch = j
                                if j % 4 == 0:
                                    lo = 0 if j < 16 else 4
                                    Swl.wdma(t, wb + lo + (j % 16) // 4)
                                    Swl.wdma(t, wb + 8 + lo + (j % 16) // 4)
                                lhs = lambda dc8, kch=kch: wtile(
                                    (dc8 // 4) * 2 + (0 if kch < 16 else 1),
                                    kch % 16, 512, (dc8 % 4) * 128)
                                dcs = range(8)
                            else:
                                e, kch = (j - 32) // 8, (j - 32) % 8
                                Swl.wdma(t, wb + 16 + 4 * (e + 1) - 1)
                                lhs = lambda dc8, e=e, kch=kch: wtile(
                                    (4 + e) % 5, kch, 1024, dc8 * 128)
                                dcs = range(8)
                            for dc8 in dcs:
                                li = t.matmul(P[dc8][:, 0:512], lhs(dc8), rhs,
                                              start=(j == 0), stop=(j == 159))
                            Smm.inc(li)

                def f_vec(v):
                    for dc8 in range(8):
                        v.memset(P[dc8][:, 0:512], 0.0)
                    for pi in range(8):
                        for dc8 in range(8):
                            i = pi * 8 + dc8
                            Smm.wait(v, pi * 160 + 160)
                            Sxr.wdma(v, i)
                            if i >= 2:
                                Sxw.wdma(v, i - 2)
                            Sev.inc(v.tensor_tensor(
                                xout[:, (dc8 % 2) * 512:(dc8 % 2) * 512 + 512],
                                xres[:, (dc8 % 2) * 512:(dc8 % 2) * 512 + 512],
                                P[dc8][:, 0:512], ALU.add))

                kb.phase(f"fo{tag}", sync=f_sync, tensor=f_tens, vector=f_vec)


        def b_chunk_phase(b):
            with ExitStack() as ps:
                abf = xarena
                qf = abf[:, 0:2048]
                kf = abf[:, 2048:4096]
                ktm = abf[:, 4096:6144]
                vtm = abf[:, 6144:8192]
                scb = abf[:, 8192:10240]
                kksb = abf[:, 10240:11264]
                qdb = abf[:, 11264:12288]
                S_bb = abf[:, 12288:16384]
                wfree = wbuf[:, 0:40960].bitcast(F32)
                dtb = wfree[:, 0:4096]
                csb2 = wfree[:, 4096:8192]
                S_b = wfree[:, 8192:12288]
                xres = ps.enter_context(nc.sbuf_tensor(f"xrb{b}", [128, 2 * 512], F32))
                xout = ps.enter_context(nc.sbuf_tensor(f"xob{b}", [128, 2 * 512], F32))
                idx = np.arange(CB, dtype=np.float64)
                betas = np.linspace(np.log2(1.0 / 32.0), np.log2(1.0 / 512.0), HB)
                gdec = 1.0 - np.exp2(betas)
                cdv = [float((gdec[h] ** CB)) for h in range(HB)]

                Sld, Sxr, Sxw, Scr = (kb.qstrm() for _ in range(4))
                Sbc5, Scs, Sz = (kb.strm() for _ in range(3))
                Smm1, Svd, Smm2, Svo, Smm3, Svu, Sqk = (kb.strm() for _ in range(7))

                def f_gp(g_):
                    g_.memset(S_b[:, :], 0.0)
                    Sz.inc(g_.memset(S_bb[:, :], 0.0))

                def f_sync(sy):
                    for hh in range(HB):
                        if hh >= 2:
                            Sbc5.wait(sy, hh - 1)
                        Scr.dma(sy, stage[0:1, (hh % 2) * 512:(hh % 2) * 512 + 512],
                                csr[0:1, hh * 512:(hh + 1) * 512])
                    for ch in range(NCB * HB):
                        c, h = ch // HB, ch % HB
                        r = ch % 2
                        if ch >= 2:
                            Svu.wait(sy, 2 * (ch - 1))
                        Sld.dma(sy, qf[:, r * 1024:r * 1024 + 1024],
                                sA[4 * c + h // 2][(2 * h) % 4:(2 * h) % 4 + 2]
                                .rearrange("b p c -> p b c"))
                        Sld.dma(sy, kf[:, r * 1024:r * 1024 + 1024],
                                sB[4 * c + h // 2][(2 * h) % 4:(2 * h) % 4 + 2]
                                .rearrange("b p c -> p b c"))
                        Sld.dma(sy, ktm[:, r * 1024:r * 1024 + 1024],
                                sC[(h // 2) * 4 + c]
                                [:, :, (h % 2) * 256:(h % 2) * 256 + 256]
                                .rearrange("b p c -> p b c"))
                        Sld.dma(sy, vtm[:, r * 1024:r * 1024 + 1024],
                                sD[(h // 2) * 4 + c]
                                [:, :, (h % 2) * 256:(h % 2) * 256 + 256]
                                .rearrange("b p c -> p b c"))
                        Sld.dma(sy, dtb[:, r * 2048:r * 2048 + 2048],
                                dbTm[h].rearrange("(b p) c -> p b c", p=128))
                        for fs in range(2):
                            i = ch * 2 + fs
                            dch = 2 * h + fs
                            if i >= 2:
                                Svo.wait(sy, i - 1)
                            Sxr.dma(sy, xres[:, (i % 2) * 512:(i % 2) * 512 + 512],
                                    xw[b, dch * 128:(dch + 1) * 128,
                                       c * 512:(c + 1) * 512])
                            if i >= 1:
                                Svo.wait(sy, i)
                                pch, pfs = (i - 1) // 2, (i - 1) % 2
                                pc, ph = pch // HB, pch % HB
                                pdch = 2 * ph + pfs
                                Sxw.dma(sy, xw[b, pdch * 128:(pdch + 1) * 128,
                                               pc * 512:(pc + 1) * 512],
                                        xout[:, ((i - 1) % 2) * 512:
                                             ((i - 1) % 2) * 512 + 512])
                    Svo.wait(sy, 64)
                    Sxw.dma(sy, xw[b, 15 * 128:16 * 128, 3 * 512:4 * 512],
                            xout[:, 512:1024])
                    Sxw.waitall(sy)
                    Sld.waitall(sy)
                    Scr.waitall(sy)
                    Sxr.waitall(sy)

                def f_tens(t):
                    Sz.wait(t, 1)
                    for hh in range(HB):
                        Scr.wdma(t, hh)
                        if hh >= 2:
                            Scs.wait(t, hh - 1)
                        Sbc5.inc(t.matmul(
                            P[6 + hh % 2][:, 0:512], ones_r_f[0:1, :],
                            stage[0:1, (hh % 2) * 512:(hh % 2) * 512 + 512],
                            start=True, stop=True))
                    for ch in range(NCB * HB):
                        c, h = ch // HB, ch % HB
                        r = ch % 2
                        Sld.wdma(t, 5 * (ch + 1) - 1)
                        Sqk.wait(t, 6 * (ch + 1))
                        if ch >= 1:
                            Svd.wait(t, 4 * ch)
                        for jt in range(4):
                            for es_ in range(2):
                                li = t.matmul(P[jt][:, 0:512],
                                              kf[:, (r * 2 + es_) * 512 + jt * 128:
                                                 (r * 2 + es_) * 512 + jt * 128 + 128],
                                              qf[:, (r * 2 + es_) * 512:
                                                 (r * 2 + es_) * 512 + 512],
                                              start=(es_ == 0), stop=(es_ == 1))
                            Smm1.inc(li)
                        Svd.wait(t, 4 * (ch + 1))
                        if c >= 1:
                            Svu.wait(t, 2 * ((c - 1) * HB + h + 1))
                        if ch >= 1:
                            Svo.wait(t, 2 * ch)
                        for fs in range(2):
                            for jt in range(4):
                                t.matmul(P[4 + fs][:, 0:512],
                                         vtm[:, r * 1024 + jt * 256 + fs * 128:
                                             r * 1024 + jt * 256 + fs * 128 + 128],
                                         scb[:, jt * 512:(jt + 1) * 512],
                                         start=(jt == 0), stop=False)
                            for es_ in range(2):
                                li = t.matmul(P[4 + fs][:, 0:512],
                                              S_bb[:, (h * 2 + es_) * 256 + fs * 128:
                                                   (h * 2 + es_) * 256 + fs * 128 + 128],
                                              qdb[:, es_ * 512:(es_ + 1) * 512],
                                              start=False, stop=(es_ == 1))
                            Smm2.inc(li)
                        if ch >= 1:
                            Svu.wait(t, 2 * ch)
                        for es_ in range(2):
                            for jt in range(4):
                                li = t.matmul(P[6 + es_][:, 0:256],
                                              kksb[:, jt * 256 + es_ * 128:
                                                   jt * 256 + es_ * 128 + 128],
                                              vtm[:, r * 1024 + jt * 256:
                                                  r * 1024 + jt * 256 + 256],
                                              start=(jt == 0), stop=(jt == 3))
                            Smm3.inc(li)

                def f_vec(v):
                    Sz.wait(v, 1)
                    for hh in range(HB):
                        Sbc5.wait(v, hh + 1)
                        Scs.inc(v.tensor_copy(csb2[:, hh * 512:(hh + 1) * 512],
                                              P[6 + hh % 2][:, 0:512]))
                    for ch in range(NCB * HB):
                        c, h = ch // HB, ch % HB
                        r = ch % 2
                        Sld.wdma(v, 5 * (ch + 1) - 1)
                        if ch >= 1:
                            Smm2.wait(v, 2 * ch)
                            Smm3.wait(v, 2 * ch)
                        for es_ in range(2):
                            v.tensor_tensor(qdb[:, es_ * 512:(es_ + 1) * 512],
                                            qf[:, (r * 2 + es_) * 512:
                                               (r * 2 + es_) * 512 + 512],
                                            csb2[:, h * 512:(h + 1) * 512], ALU.mult)
                        for jt in range(4):
                            li = v.tensor_scalar(kksb[:, jt * 256:(jt + 1) * 256],
                                                 ktm[:, r * 1024 + jt * 256:
                                                     r * 1024 + jt * 256 + 256],
                                                 ksbt_sb[:, jt * 8 + h:jt * 8 + h + 1],
                                                 None, ALU.mult)
                        Sqk.inc(li, 6)
                        for jt in range(4):
                            Smm1.wait(v, 4 * ch + jt + 1)
                            Svd.inc(v.tensor_tensor(
                                scb[:, jt * 512:(jt + 1) * 512], P[jt][:, 0:512],
                                dtb[:, r * 2048 + jt * 512:r * 2048 + jt * 512 + 512],
                                ALU.mult))
                        for fs in range(2):
                            i = ch * 2 + fs
                            Smm2.wait(v, 2 * ch + fs + 1)
                            Sxr.wdma(v, i)
                            if i >= 2:
                                Sxw.wdma(v, i - 2)
                            Svo.inc(v.tensor_tensor(
                                xout[:, (fs % 2) * 512:(fs % 2) * 512 + 512],
                                xres[:, (i % 2) * 512:(i % 2) * 512 + 512],
                                P[4 + fs][:, 0:512], ALU.add))
                        for es_ in range(2):
                            Smm3.wait(v, 2 * ch + es_ + 1)
                            o = (h * 2 + es_) * 256
                            v.scalar_tensor_tensor(S_b[:, o:o + 256], S_b[:, o:o + 256],
                                                   cdv[h], P[6 + es_][:, 0:256],
                                                   ALU.mult, ALU.add)
                            Svu.inc(v.tensor_copy(S_bb[:, o:o + 256], S_b[:, o:o + 256]))

                kb.phase(f"bch{b}", gpsimd=f_gp, sync=f_sync, tensor=f_tens,
                         vector=f_vec)

        def finish(src_for_out=None):
            def f_g(g):
                a = Strm(kb.aux)
                if src_for_out is not None:
                    fs = src_for_out.rearrange("a b c -> (a b c)")
                    fd = out.rearrange("a b c -> (a b c)")
                    n = 2 * D * T
                    CH = 1 << 20
                    for o in range(0, n, CH):
                        a.dma_batched(g, fd[o:min(n, o + CH)],
                                      fs[o:min(n, o + CH)])
                if dbg is not None:
                    dsrc = {"xw": xw, "sA": sA, "sB": sB, "sC": sC, "sD": sD,
                            "sE": sE, "facd": facd, "cwd": cwd, "ge": ge_sl,
                            "gs": gs_sl, "xlnd": xlnd}[dump]
                    nel = int(np.prod(dbg.shape))
                    r = {1: "a -> (a)", 2: "a b -> (a b)", 3: "a b c -> (a b c)",
                         4: "a b c d -> (a b c d)", 5: "a b c d e -> (a b c d e)"}
                    fs = dsrc.rearrange(r[len(dsrc.shape)])
                    fd = dbg.rearrange(r[len(dbg.shape)])
                    CH = 1 << 20
                    for o in range(0, nel, CH):
                        a.dma_batched(g, fd[o:min(nel, o + CH)],
                                      fs[o:min(nel, o + CH)])
                g.wait_ge(a.s.h, a.s.n)
            kb.phase("finish", gpsimd=f_g)

        def dump_xln():
            if dump == "xlnd":
                def f_g(g):
                    a = Strm(kb.aux)
                    a.dma(g, xlnd[:, :], xln[:, :])
                    g.wait_ge(a.s.h, a.s.n)
                kb.phase("dumpxln", gpsimd=f_g)

        seq = []

        def step(name, fn):
            seq.append((name, fn))

        for b in range(2):
            step(f"aln{b}", lambda b=b: ln_phase(f"a{b}", b, xin))
            step(f"axd{b}", lambda b=b: dump_xln())
            step(f"ald{b}", lambda b=b: a_ld_phase(b))
            step(f"aqd{b}", lambda b=b: proj_fm(f"qd{b}", "wq", "qd2", [sA, sB]))
            step(f"akd{b}", lambda b=b: proj_fm(f"kd{b}", "wk", "kd", [sC]))
            step(f"akr{b}", lambda b=b: proj_tm(f"kr{b}", "wk", "kdr", sD))
            step(f"avt{b}", lambda b=b: proj_tm(f"vt{b}", "wv", "vtm", sE))
            step(f"ach{b}", lambda b=b: a_chunk_phase(b))
            for r in range(2):
                step(f"fln{b}{r}", lambda b=b, r=r: ln_phase(f"f{b}{r}", b, xw))
                step(f"frt{b}{r}", lambda b=b, r=r: router_phase(f"{b}{r}", b))
                step(f"fgs{b}{r}", lambda b=b, r=r: g_gen_phase(f"s{b}{r}", True))
                step(f"fge{b}{r}", lambda b=b, r=r: g_gen_phase(f"e{b}{r}", False))
                step(f"fo3{b}{r}", lambda b=b, r=r: f_out_phase(
                    f"{b}{r}", b, xw if r == 0 else out))
                if r == 0:
                    step(f"bln{b}", lambda b=b: ln_phase(f"b{b}", b, xw))
                    step(f"bqf{b}", lambda b=b: proj_fm(f"bq{b}", "bwq", "biasq", [sA]))
                    step(f"bkf{b}", lambda b=b: proj_fm(f"bk{b}", "bwk", "biask", [sB]))
                    step(f"bkt{b}", lambda b=b: proj_tm(f"bkt{b}", "bwk", "btm", sC, 0, swap=True))
                    step(f"bvt{b}", lambda b=b: proj_tm(f"bvt{b}", "bwv", "btm", sD, 1, swap=True))
                    step(f"bch{b}", lambda b=b: b_chunk_phase(b))

        if stop_after == "w0":
            seq = []
        ran = []
        for name, fn in seq:
            fn()
            ran.append(name)
            if stop_after is not None and name == stop_after:
                break
        finish(src_for_out=xw if stop_after is not None else None)
    return nc


def _consts():
    idx = np.arange(CB, dtype=np.float64)
    betas = np.linspace(np.log2(1.0 / 32.0), np.log2(1.0 / 512.0), HB)
    g = (1.0 - np.exp2(betas))
    Dm = np.where(idx[:, None] >= idx[None, :],
                  g[:, None, None] ** (idx[:, None] - idx[None, :]), 0.0)
    dbT = np.ascontiguousarray(Dm.transpose(0, 2, 1)).astype(np.float32)
    csr = (g[:, None] ** (idx + 1)).astype(np.float32).reshape(1, -1)
    ks = (g[:, None] ** (CB - 1 - idx)).astype(np.float32)
    ksbt = np.zeros((128, 32), np.float32)
    for js in range(4):
        for h in range(HB):
            ksbt[:, js * 8 + h] = ks[h, js * 128:(js + 1) * 128]
    cd = (g ** CB).astype(np.float32)
    return dbT, csr, ksbt, cd


def prep_core_inputs(inputs, core):
    b0 = core * 2
    x = np.ascontiguousarray(
        np.transpose(inputs["L0_attn_a_x"][b0:b0 + 2], (0, 2, 1))).astype(np.float32)
    dbT, csr, ksbt, _cd = _consts()
    wfmv = np.ascontiguousarray(
        inputs["L0_attn_a_Wf"].reshape(16, 128, 16).transpose(1, 0, 2).reshape(128, 256))
    wrmv = np.ascontiguousarray(
        inputs["L0_ffn_w_router"].reshape(16, 128, 16).transpose(1, 0, 2).reshape(128, 256))
    normwm = np.ascontiguousarray(inputs["L0_attn_a_norm_w"].reshape(16, 128).T)
    biasm = np.stack([np.ascontiguousarray(inputs["L0_attn_b_bq"].reshape(16, 128).T),
                      np.ascontiguousarray(inputs["L0_attn_b_bk"].reshape(16, 128).T)])
    biasrow = np.stack([inputs["L0_attn_b_bk"], inputs["L0_attn_b_bv"]]).astype(np.float32)
    im = {
        "x": x,
        "wq": inputs["L0_attn_a_Wq"], "wk": inputs["L0_attn_a_Wk"],
        "wv": inputs["L0_attn_a_Wv"], "wo": inputs["L0_attn_a_Wo"],
        "bwq": inputs["L0_attn_b_Wq"], "bwk": inputs["L0_attn_b_Wk"],
        "bwv": inputs["L0_attn_b_Wv"],
        "we1": inputs["L0_ffn_we1"].reshape(NE * D, HID),
        "we3": inputs["L0_ffn_we3"].reshape(NE * D, HID),
        "we2": inputs["L0_ffn_we2"].reshape(NE * HID, D),
        "ws1": inputs["L0_ffn_ws1"], "ws3": inputs["L0_ffn_ws3"],
        "ws2": inputs["L0_ffn_ws2"],
        "wfm": wfmv, "wrm": wrmv, "normwm": normwm,
        "biasm": biasm.astype(np.float32), "biasrow": biasrow,
        "identm": np.eye(128, dtype=np.float32),
        "dbTm": dbT, "csr": csr, "ksbtm": ksbt,
    }
    return {k: np.ascontiguousarray(np.asarray(v, dtype=v.dtype if hasattr(v, "dtype") else np.float32))
            for k, v in im.items()}


def kernel(**inputs):
    nc = build_program()
    in_maps = [prep_core_inputs(inputs, c) for c in range(8)]
    res = run_bass_kernel_spmd(nc, in_maps, core_ids=list(range(8)))
    outs = []
    for c in range(8):
        o = res.results[c]["out"]
        outs.append(np.transpose(o, (0, 2, 1)))
    return np.ascontiguousarray(np.concatenate(outs, axis=0))


# version 09
# speedup vs baseline: 1.2352x; 1.0977x over previous
import numpy as np
from contextlib import ExitStack

import concourse.bass as bass
import concourse.mybir as mybir
from concourse.bass_utils import run_bass_kernel_spmd

F32 = mybir.dt.float32
BF16 = mybir.dt.bfloat16
AF = mybir.ActivationFunctionType
ALU = mybir.AluOpType
AX = mybir.AxisListType

ENGS = ("tensor", "vector", "scalar", "gpsimd", "sync")

D, T = 2048, 2048
NCH, NT = 16, 4
CA, NCA, HA = 128, 16, 16
CB, NCB, HB = 512, 4, 8
NE, HID, HS = 16, 1024, 4096

CPV = [1e-5, 1e-6, -0.5, 1.0, -1.0, 0.0, 0.2, 1.0 / 2048.0]
CP = {v: i for i, v in enumerate(CPV)}


class Sem:
    __slots__ = ("h", "n")

    def __init__(self, h):
        self.h = h
        self.n = 0


class Strm:

    def __init__(self, sems):
        if not isinstance(sems, list):
            sems = [sems]
        self.sems = sems
        self.bases = [s.n for s in sems]
        self.s = sems[0]
        self.base = self.bases[0]
        self.k = 0
        self.cnt = [0] * len(sems)

    def inc(self, inst, k=1):
        inst.then_inc(self.s.h, k)
        self.s.n += k

    def dma(self, q, dst, src):
        n = len(self.sems)
        p = self.k % n
        sem = self.sems[p]
        if n > 1 and self.cnt[p] >= 1:
            q.wait_ge(sem.h, self.bases[p] + 16 * self.cnt[p])
        q.dma_start(dst, src).then_inc(sem.h, 16)
        sem.n += 16
        self.cnt[p] += 1
        self.k += 1

    def dma_batched(self, q, dst, src, depth=8):
        if (self.s.n - self.base) // 16 % depth == 0 and self.s.n > self.base:
            q.wait_ge(self.s.h, self.s.n)
        q.dma_start(dst, src).then_inc(self.s.h, 16)
        self.s.n += 16
        self.k += 1

    def drain(self, q):
        q.wait_ge(self.s.h, self.s.n)

    def wdma(self, eng, j):
        n = len(self.sems)
        for p in range(n):
            c = (j - p) // n + 1 if j >= p else 0
            if c > 0:
                eng.wait_ge(self.sems[p].h, self.bases[p] + 16 * c)

    def waitall(self, eng):
        for s in self.sems:
            eng.wait_ge(s.h, s.n)

    def wait(self, eng, i):
        eng.wait_ge(self.s.h, self.base + i)


class KB:
    def __init__(self, nc, es):
        self.nc = nc
        self.es = es
        self.pi = 0
        self.pb = Sem(es.enter_context(nc.semaphore("pb")))
        self.aux = Sem(es.enter_context(nc.semaphore("aux")))
        self.pool = [Sem(es.enter_context(nc.semaphore(f"g{i}")))
                     for i in range(56)]
        self.pooli = 0

    def strm(self):
        s = Strm(self.pool[self.pooli % len(self.pool)])
        self.pooli += 1
        return s

    def dstrm(self, n=2):
        sems = []
        for _ in range(n):
            sems.append(self.pool[self.pooli % len(self.pool)])
            self.pooli += 1
        return Strm(sems)

    def qstrm(self):
        return self.dstrm(2)

    def sb(self, name, shape, dt):
        return self.es.enter_context(self.nc.sbuf_tensor(name, shape, dt))

    def phase(self, name, **bodies):
        i = self.pi
        pb = self.pb
        with self.nc.Block(f"ph{i}_{name}") as blk:
            for en in ENGS:
                fn = bodies.get(en)
                pre = bodies.get("pre_" + en)

                def mk(en=en, fn=fn, pre=pre):
                    def body(eng):
                        if pre is not None:
                            pre(eng)
                        eng.wait_ge(pb.h, 5 * i)
                        if fn is not None:
                            fn(eng)
                        eng.drain()
                        eng.sem_inc(pb.h, 1)
                    return body
                getattr(blk, en)(mk())
        pb.n += 5
        self.pi += 1


def build_program(stop_after=None, dump=None):
    nc = bass.Bass("TRN2", target_bir_lowering=False)

    xin = nc.dram_tensor("x", [2, D, T], F32, kind="ExternalInput").ap()
    out = nc.dram_tensor("out", [2, D, T], F32, kind="ExternalOutput").ap()
    xw = nc.dram_tensor("xw", [2, D, T], F32, kind="Internal").ap()

    wshapes = {
        "wq": [D, D], "wk": [D, D], "wv": [D, D], "wo": [D, D],
        "bwq": [D, D], "bwk": [D, D], "bwv": [D, D],
        "we1": [NE * D, HID], "we3": [NE * D, HID], "we2": [NE * HID, D],
        "ws1": [D, HS], "ws3": [D, HS], "ws2": [HS, D],
    }
    wf32 = {k: nc.dram_tensor(k, s, F32, kind="ExternalInput").ap()
            for k, s in wshapes.items()}
    wbfd = {k: nc.dram_tensor(k + "_bf", s, BF16, kind="Internal").ap()
            for k, s in wshapes.items()}

    wfm = nc.dram_tensor("wfm", [128, 256], F32, kind="ExternalInput").ap()
    wrm = nc.dram_tensor("wrm", [128, 256], F32, kind="ExternalInput").ap()
    normwm = nc.dram_tensor("normwm", [128, 16], F32, kind="ExternalInput").ap()
    biasm = nc.dram_tensor("biasm", [2, 128, 16], F32, kind="ExternalInput").ap()
    biasrow = nc.dram_tensor("biasrow", [2, D], F32, kind="ExternalInput").ap()
    identm = nc.dram_tensor("identm", [128, 128], F32, kind="ExternalInput").ap()
    dbTm = nc.dram_tensor("dbTm", [HB, CB, CB], F32, kind="ExternalInput").ap()
    csr = nc.dram_tensor("csr", [1, HB * CB], F32, kind="ExternalInput").ap()
    ksbtm = nc.dram_tensor("ksbtm", [128, 32], F32, kind="ExternalInput").ap()

    def mkslab(name):
        return nc.dram_tensor(name, [NCH, NT, 128, 512], BF16, kind="Internal").ap()
    sA, sB, sC, sD, sE = (mkslab(n) for n in ["sA", "sB", "sC", "sD", "sE"])
    ge_sl = nc.dram_tensor("ge", [NE, NT, 8, 128, 512], BF16, kind="Internal").ap()
    gs_sl = nc.dram_tensor("gs", [NT, 32, 128, 512], BF16, kind="Internal").ap()
    facd = nc.dram_tensor("facd", [3, HA, T], F32, kind="Internal").ap()
    rowd = nc.dram_tensor("rowd", [16, 16], F32, kind="Internal").ap()
    cwd = nc.dram_tensor("cwd", [NE, T], F32, kind="Internal").ap()
    xlnd = nc.dram_tensor("xlnd", [128, NCH * T], BF16, kind="Internal").ap()

    dbg = None
    if dump is not None:
        dshape = {"xw": [2 * D, T], "sA": [NCH * NT * 128, 512],
                  "sB": [NCH * NT * 128, 512], "sC": [NCH * NT * 128, 512],
                  "sD": [NCH * NT * 128, 512], "sE": [NCH * NT * 128, 512],
                  "facd": [3 * HA, T], "cwd": [NE, T], "xlnd": [128, NCH * T],
                  "ge": [NE * NT * 8 * 128, 512], "gs": [NT * 32 * 128, 512]}[dump]
        dbg = nc.dram_tensor("dbg", dshape, F32, kind="ExternalOutput").ap()

    es = ExitStack()
    with es:
        kb = KB(nc, es)
        P = [es.enter_context(nc.psum_tensor(f"pk{i}", [128, 512], F32))
             for i in range(8)]

        xarena = kb.sb("xarena", [128, NCH * T], BF16)
        xln = xarena
        wbuf = kb.sb("wbuf", [128, 5 * 8192], BF16)
        ones_r_bf = kb.sb("ones_r_bf", [1, 128], BF16)
        ones_r_f = kb.sb("ones_r_f", [1, 128], F32)
        ones_c_bf = kb.sb("ones_c_bf", [128, 1], BF16)
        cpool = kb.sb("cpool", [128, 8], F32)
        ident = kb.sb("ident", [128, 128], F32)
        stage = kb.sb("stage", [1, 2 * 512], F32)
        nwm_sb = kb.sb("nwm_sb", [128, 16], F32)
        bqm_sb = kb.sb("bqm_sb", [128, 32], F32)
        brow_sb = kb.sb("brow_sb", [1, 2 * D], BF16)
        wfm_sb = kb.sb("wfm_sb", [128, 256], BF16)
        wrm_sb = kb.sb("wrm_sb", [128, 256], BF16)
        smal = kb.sb("smal", [16, 64], F32)
        smalT = kb.sb("smalT", [16, 16], F32)
        zz16 = kb.sb("zz16", [16, 128], F32)
        fkr_tm = kb.sb("fkr_tm", [128, NCA * 16], F32)
        etot_b = kb.sb("etot_b", [128, NCA * 16], F32)
        S_a = kb.sb("S_a", [128, HA * 128], F32)
        S_ab = kb.sb("S_ab", [128, HA * 128], BF16)
        ksbt_sb = kb.sb("ksbt_sb", [128, 32], F32)
        SM_S, SM_TOT, SM_ETOT, SM_NEGS = 0, 16, 32, 48

        def init_g(g):
            g.memset(ones_r_bf[0:1, :], 1.0)
            g.memset(ones_r_f[0:1, :], 1.0)
            g.memset(ones_c_bf[:, 0:1], 1.0)
            g.memset(zz16[:, :], 0.0)
            for v, i in CP.items():
                g.memset(cpool[:, i:i + 1], float(v))
            a = Strm(kb.aux)
            a.dma(g, ident[:, :], identm[:, :])
            a.dma(g, nwm_sb[:, :], normwm[:, :])
            a.dma(g, bqm_sb[:, 0:16], biasm[0])
            a.dma(g, bqm_sb[:, 16:32], biasm[1])
            a.dma(g, brow_sb[0:1, :], biasrow.rearrange("a b -> (a b)"))
            a.dma(g, wfm_sb[:, :], wfm[:, :])
            a.dma(g, wrm_sb[:, :], wrm[:, :])
            a.dma(g, ksbt_sb[:, :], ksbtm[:, :])
            g.wait_ge(a.s.h, a.s.n)

        kb.phase("init", gpsimd=init_g)

        wc_done = {}

        def w0_g(g):
            a = Strm(kb.aux)
            for k in ["wq", "wk", "wv", "wo", "ws1", "ws3", "we1", "we3",
                      "we2", "ws2", "bwq", "bwk", "bwv"]:
                src, dst = wf32[k], wbfd[k]
                n = int(np.prod(wshapes[k]))
                fs, fd = src.rearrange("a b -> (a b)"), dst.rearrange("a b -> (a b)")
                CH = 1 << 20
                for o in range(0, n, CH):
                    a.dma_batched(g, fd[o:min(n, o + CH)], fs[o:min(n, o + CH)])
                a.drain(g)
                wc_done[k] = kb.aux.n

        kb.phase("w0", gpsimd=w0_g)

        SLOTW = 8192

        def load_half(q, st, wt, kdim, c0, colw, slot):
            kch = kdim // 128
            assert kch * colw <= SLOTW
            ap = wt[:, c0:c0 + colw].rearrange("(k p) n -> p k n", p=128)
            step = max(1, kch // 4)
            for ks in range(0, kch, step):
                ke = min(kch, ks + step)
                st.dma(q, wbuf[:, slot * SLOTW + ks * colw:slot * SLOTW + ke * colw],
                       ap[:, ks:ke, :])

        def wtile(slot, kchi, colw, c0, w=128):
            o = slot * SLOTW + kchi * colw + c0
            return wbuf[:, o:o + w]

        def ln_phase(tag, b, src):
            with ExitStack() as ps:
                xtile = ps.enter_context(nc.sbuf_tensor(f"xt{tag}", [128, 2 * 512], F32))
                ntmp = ps.enter_context(nc.sbuf_tensor(f"ntp{tag}", [128, 512], F32))
                xbf = wbuf[:, 2 * SLOTW:2 * SLOTW + 8192]
                x2 = wbuf[:, 3 * SLOTW:3 * SLOTW + 8192]
                srow = ps.enter_context(nc.sbuf_tensor(f"srw{tag}", [1, 4 * 512], F32))
                Sx = kb.qstrm()
                Scast, Ssq, Smm, Ssrow, Svar, Srstd, Snb, Sbc, Snorm = (
                    kb.strm() for _ in range(9))

                def f_sync(sy):
                    for nt in range(NT):
                        for dch in range(NCH):
                            i = nt * NCH + dch
                            if i >= 2:
                                Scast.wait(sy, i - 1)
                                Ssq.wait(sy, i - 1)
                            Sx.dma(sy, xtile[:, (i % 2) * 512:(i % 2) * 512 + 512],
                                   src[b, dch * 128:(dch + 1) * 128,
                                       nt * 512:(nt + 1) * 512])

                def f_vec(v):
                    for nt in range(NT):
                        for dch in range(NCH):
                            i = nt * NCH + dch
                            Sx.wdma(v, i)
                            sl = slice((i % 2) * 512, (i % 2) * 512 + 512)
                            Scast.inc(v.tensor_copy(
                                xbf[:, dch * 512:(dch + 1) * 512], xtile[:, sl]))
                        Ssrow.wait(v, nt + 1)
                        Svar.inc(v.tensor_sub(srow[0:1, 512:1024],
                                              srow[0:1, 512:1024],
                                              srow[0:1, 2 * 512:3 * 512]))
                        Srstd.wait(v, nt + 1)
                        Snb.inc(v.scalar_tensor_tensor(
                            srow[0:1, 3 * 512:4 * 512], srow[0:1, 0:512], -1.0,
                            srow[0:1, 2 * 512:3 * 512], ALU.mult, ALU.mult))
                        Sbc.wait(v, nt + 1)
                        for dch in range(NCH):
                            v.tensor_tensor(ntmp[:, 0:512],
                                            xbf[:, dch * 512:(dch + 1) * 512],
                                            P[6][:, 0:512], ALU.mult)
                            o = dch * T + nt * 512
                            li = v.tensor_tensor(xln[:, o:o + 512], ntmp[:, 0:512],
                                                 P[7][:, 0:512], ALU.add)
                        Snorm.inc(li)

                def f_scal(s):
                    for nt in range(NT):
                        for dch in range(NCH):
                            i = nt * NCH + dch
                            Sx.wdma(s, i)
                            sl = slice((i % 2) * 512, (i % 2) * 512 + 512)
                            Ssq.inc(s.activation(x2[:, dch * 512:(dch + 1) * 512],
                                                 xtile[:, sl], AF.Square))
                        Smm.wait(s, 2 * (nt + 1))
                        if nt > 0:
                            Snb.wait(s, nt)
                            Sbc.wait(s, nt)
                        s.mul(srow[0:1, 0:512], P[4][0:1, 0:512], 1.0 / 2048.0)
                        s.mul(srow[0:1, 512:1024], P[5][0:1, 0:512], 1.0 / 2048.0)
                        Ssrow.inc(s.activation(srow[0:1, 2 * 512:3 * 512],
                                               srow[0:1, 0:512], AF.Square))
                        Svar.wait(s, nt + 1)
                        s.activation(srow[0:1, 2 * 512:3 * 512],
                                     srow[0:1, 512:1024], AF.Ln,
                                     bias=cpool[0:1, CP[1e-5]:CP[1e-5] + 1])
                        Srstd.inc(s.activation(
                            srow[0:1, 2 * 512:3 * 512], srow[0:1, 2 * 512:3 * 512],
                            AF.Exp, scale=cpool[0:1, CP[-0.5]:CP[-0.5] + 1]))

                def f_tens(t):
                    for nt in range(NT):
                        Scast.wait(t, NCH * (nt + 1))
                        Ssq.wait(t, NCH * (nt + 1))
                        if nt > 0:
                            Ssrow.wait(t, nt)
                        for dch in range(NCH):
                            li = t.matmul(P[4][0:1, 0:512], ones_c_bf[:, 0:1],
                                          xbf[:, dch * 512:(dch + 1) * 512],
                                          start=(dch == 0), stop=(dch == NCH - 1))
                        Smm.inc(li)
                        for dch in range(NCH):
                            li = t.matmul(P[5][0:1, 0:512], ones_c_bf[:, 0:1],
                                          x2[:, dch * 512:(dch + 1) * 512],
                                          start=(dch == 0), stop=(dch == NCH - 1))
                        Smm.inc(li)
                        Snb.wait(t, nt + 1)
                        if nt > 0:
                            Snorm.wait(t, nt)
                        t.matmul(P[6][:, 0:512], ones_r_f[0:1, :],
                                 srow[0:1, 2 * 512:3 * 512], start=True, stop=True)
                        Sbc.inc(t.matmul(P[7][:, 0:512], ones_r_f[0:1, :],
                                         srow[0:1, 3 * 512:4 * 512],
                                         start=True, stop=True))

                kb.phase(f"ln{tag}", sync=f_sync, vector=f_vec, scalar=f_scal,
                         tensor=f_tens)

        def a_ld_phase(b):
            with ExitStack() as ps:
                wfree = wbuf[:, 0:32768].bitcast(F32)
                ldq = [ps.enter_context(nc.sbuf_tensor(f"ldq{i}_{b}", [16, T], F32))
                       for i in range(2)]
                ldq += [wfree[0:16, i * T:(i + 1) * T] for i in range(4)]
                LD, CUM, FQS, FKS, FQI, FKR = range(6)
                Smm = kb.strm()
                Sz = kb.strm()
                Sls = kb.strm()
                Ssm = kb.strm()
                Sfq = kb.strm()
                Str = kb.strm()
                Sev = kb.strm()
                Srow = kb.dstrm()
                Sbc2 = kb.strm()
                Sebc = kb.strm()
                Sst = kb.dstrm()

                def f_tens(t):
                    for nt in range(NT):
                        if nt >= 2:
                            Sz.wait(t, nt - 1)
                        for kch in range(NCH):
                            li = t.matmul(P[nt % 2][0:16, 0:512],
                                          wfm_sb[:, kch * 16:(kch + 1) * 16],
                                          xln[:, kch * T + nt * 512:
                                              kch * T + nt * 512 + 512],
                                          start=(kch == 0), stop=(kch == NCH - 1))
                        Smm.inc(li)
                    Sfq.wait(t, NCA + 1)
                    for c in range(NCA):
                        if c >= 2:
                            Sev.wait(t, c - 1)
                        Str.inc(t.transpose(P[2 + c % 2][0:128, 0:16],
                                              ldq[FKR][:, c * 128:(c + 1) * 128],
                                              ident[0:16, 0:16]))
                    Sev.wait(t, NCA)
                    Str.inc(t.transpose(P[2][0:16, 0:16],
                                         smal[:, SM_ETOT:SM_ETOT + 16],
                                         ident[0:16, 0:16]))
                    for c in range(NCA):
                        Srow.wdma(t, c)
                        if c >= 2:
                            Sebc.wait(t, c - 1)
                        Sbc2.inc(t.matmul(P[4 + c % 2][:, 0:16], ones_r_f[0:1, :],
                                           stage[0:1, (c % 2) * 512:
                                                 (c % 2) * 512 + 16],
                                           start=True, stop=True))

                def f_scal(s):
                    for nt in range(NT):
                        Smm.wait(s, nt + 1)
                        sl = slice(nt * 512, (nt + 1) * 512)
                        Sz.inc(s.copy(ldq[LD][:, sl], P[nt % 2][0:16, 0:512]))
                    s.activation(ldq[FQS][:, :], ldq[LD][:, :], AF.Abs)
                    s.activation(ldq[FQS][:, :], ldq[FQS][:, :], AF.Exp,
                                 scale=cpool[0:16, CP[-1.0]:CP[-1.0] + 1])
                    Sls.inc(s.activation(ldq[FQS][:, :], ldq[FQS][:, :], AF.Ln,
                                         bias=cpool[0:16, CP[1.0]:CP[1.0] + 1]))
                    Ssm.wait(s, 1)
                    s.activation(ldq[FQI][:, :], ldq[CUM][:, :], AF.Exp)
                    Sfq.inc(s.activation(smal[:, SM_ETOT:SM_ETOT + 16],
                                         smal[:, SM_TOT:SM_TOT + 16], AF.Exp))
                    for c in range(NCA):
                        sl = slice(c * 128, (c + 1) * 128)
                        s.activation(ldq[FQS][:, sl], ldq[CUM][:, sl], AF.Exp,
                                     bias=smal[:, SM_NEGS + c:SM_NEGS + c + 1])
                        s.activation(ldq[FKS][:, sl], ldq[CUM][:, sl], AF.Exp,
                                     scale=cpool[0:16, CP[-1.0]:CP[-1.0] + 1],
                                     bias=smal[:, SM_S + c:SM_S + c + 1])
                        Sfq.inc(s.activation(
                            ldq[FKR][:, sl], ldq[CUM][:, sl], AF.Exp,
                            scale=cpool[0:16, CP[-1.0]:CP[-1.0] + 1],
                            bias=smal[:, SM_TOT + c:SM_TOT + c + 1]))

                def f_vec(v):
                    Sls.wait(v, 1)
                    v.tensor_scalar_min(ldq[FKS][:, :], ldq[LD][:, :], 0.0)
                    v.tensor_sub(ldq[LD][:, :], ldq[FKS][:, :], ldq[FQS][:, :])
                    for c in range(NCA):
                        sl = slice(c * 128, (c + 1) * 128)
                        v.tensor_tensor_scan(ldq[CUM][:, sl], ldq[LD][:, sl],
                                             zz16[:, :], 0.0, ALU.add, ALU.add)
                    v.tensor_copy(smal[:, SM_S:SM_S + 16], ldq[CUM][:, 63:T:128])
                    v.tensor_copy(smal[:, SM_TOT:SM_TOT + 16], ldq[CUM][:, 127:T:128])
                    Ssm.inc(v.tensor_scalar_mul(smal[:, SM_NEGS:SM_NEGS + 16],
                                                smal[:, SM_S:SM_S + 16], -1.0))
                    for c in range(NCA):
                        Str.wait(v, c + 1)
                        Sev.inc(v.tensor_copy(fkr_tm[:, c * 16:(c + 1) * 16],
                                              P[2 + c % 2][0:128, 0:16]))
                    Str.wait(v, NCA + 1)
                    Sev.inc(v.tensor_copy(smalT[:, :], P[2][0:16, 0:16]))
                    for c in range(NCA):
                        Sbc2.wait(v, c + 1)
                        Sebc.inc(v.tensor_copy(etot_b[:, c * 16:(c + 1) * 16],
                                               P[4 + c % 2][:, 0:16]))

                def f_sync(sy):
                    Sfq.wait(sy, NCA + 1)
                    Sst.dma(sy, facd[0], ldq[FQS][:, :])
                    Sst.dma(sy, facd[1], ldq[FKS][:, :])
                    Sst.dma(sy, facd[2], ldq[FQI][:, :])
                    Sev.wait(sy, NCA + 1)
                    Sst.dma(sy, rowd[:, :], smalT[:, :])
                    Sst.waitall(sy)
                    for c in range(NCA):
                        if c >= 2:
                            Sbc2.wait(sy, c - 1)
                        Srow.dma(sy, stage[0:1, (c % 2) * 512:(c % 2) * 512 + 16],
                                 rowd[c:c + 1, :])
                kb.phase(f"ald{b}", tensor=f_tens, scalar=f_scal, vector=f_vec,
                         sync=f_sync)

        def proj_fm(tag, wname, kind, slabs):
            with ExitStack() as ps:
                tmp = ps.enter_context(nc.sbuf_tensor(f"tp{tag}", [128, 4 * 512], F32))
                osb = ps.enter_context(nc.sbuf_tensor(f"os{tag}", [128, 4 * 512], BF16))
                Swl, Sst, Srow = kb.qstrm(), kb.qstrm(), kb.qstrm()
                Smm, Ssil, Sev, Sbc = (kb.strm() for _ in range(4))
                nfac = 2 if kind == "qd2" else (1 if kind == "kd" else 0)
                fmap = [0, 2] if kind == "qd2" else [1]

                def f_sync(sy):
                    sy.wait_ge(kb.aux.h, wc_done[wname])
                    load_half(sy, Swl, wbfd[wname], D, 0, 512, 0)
                    rows_done = [0]

                    def rows(hi):
                        for i in range(rows_done[0], min(64 * nfac, hi)):
                            if i >= 2:
                                Sbc.wait(sy, i - 1)
                            ti, f = i // nfac, i % nfac
                            m, nt = ti // NT, ti % NT
                            Srow.dma(sy,
                                     stage[0:1, (i % 2) * 512:(i % 2) * 512 + 512],
                                     facd[fmap[f], m:m + 1,
                                          nt * 512:(nt + 1) * 512])
                        rows_done[0] = max(rows_done[0], min(64 * nfac, hi))
                    if nfac:
                        rows(2)
                    load_half(sy, Swl, wbfd[wname], D, 512, 512, 1)
                    for ti in range(64):
                        if ti >= 2:
                            store_tile(sy, ti - 2)
                        if nfac:
                            rows((ti + 3) * nfac)
                        if ti % 16 == 15 and ti // 16 + 2 < 4:
                            p = ti // 16 + 2
                            Smm.wait(sy, 16 * (p - 1))
                            load_half(sy, Swl, wbfd[wname], D, p * 512, 512, p % 2)
                    store_tile(sy, 62)
                    store_tile(sy, 63)
                    Sst.waitall(sy)
                    Swl.waitall(sy)
                    if nfac:
                        Srow.waitall(sy)

                def f_tens(t):
                    for piece in range(4):
                        Swl.wdma(t, 4 * (piece + 1) - 1)
                        for m4 in range(4):
                            m = piece * 4 + m4
                            for nt in range(NT):
                                ti = m * NT + nt
                                bank = ti % 6
                                if ti >= 6:
                                    Ssil.wait(t, ti - 5)
                                for kch in range(NCH):
                                    li = t.matmul(P[bank][:, 0:512],
                                                  wtile(piece % 2, kch, 512, m4 * 128),
                                                  xln[:, kch * T + nt * 512:
                                                      kch * T + nt * 512 + 512],
                                                  start=(kch == 0),
                                                  stop=(kch == NCH - 1))
                                Smm.inc(li)
                                for f in range(nfac):
                                    i = ti * nfac + f
                                    Srow.wdma(t, i)
                                    if i >= 2:
                                        Sev.wait(t, i - 1)
                                    Sbc.inc(t.matmul(
                                        P[6 + i % 2][:, 0:512], ones_r_f[0:1, :],
                                        stage[0:1, (i % 2) * 512:
                                              (i % 2) * 512 + 512],
                                        start=True, stop=True))

                def f_scal(s):
                    for ti in range(64):
                        bank = ti % 6
                        Smm.wait(s, ti + 1)
                        if nfac:
                            if ti >= 4:
                                Sev.wait(s, (ti - 3) * nfac)
                            Ssil.inc(s.activation(
                                tmp[:, (ti % 4) * 512:(ti % 4) * 512 + 512],
                                P[bank][:, 0:512], AF.Silu))
                        else:
                            if ti >= 4:
                                Sst.wdma(s, ti - 4)
                            m = ti // NT
                            bc = (0 if kind == "biasq" else 16) + m
                            Ssil.inc(s.activation(
                                osb[:, (ti % 4) * 512:(ti % 4) * 512 + 512],
                                P[bank][:, 0:512], AF.Identity,
                                bias=bqm_sb[:, bc:bc + 1]))

                def f_vec(v):
                    if not nfac:
                        return
                    for ti in range(64):
                        Ssil.wait(v, ti + 1)
                        if ti >= 2:
                            Sst.wdma(v, nfac * (ti - 1) - 1)
                        for f in range(nfac):
                            i = ti * nfac + f
                            Sbc.wait(v, i + 1)
                            Sev.inc(v.tensor_tensor(
                                osb[:, ((ti % 2) * 2 + f) * 512:
                                    ((ti % 2) * 2 + f) * 512 + 512],
                                tmp[:, (ti % 4) * 512:(ti % 4) * 512 + 512],
                                P[6 + i % 2][:, 0:512], ALU.mult))

                def store_tile(sy, ti):
                    m, nt = ti // NT, ti % NT
                    if nfac:
                        Sev.wait(sy, (ti + 1) * nfac)
                        for f in range(nfac):
                            Sst.dma(sy, slabs[f][4 * nt + m // 4][m % 4],
                                    osb[:, ((ti % 2) * 2 + f) * 512:
                                        ((ti % 2) * 2 + f) * 512 + 512])
                    else:
                        Ssil.wait(sy, ti + 1)
                        Sst.dma(sy, slabs[0][4 * nt + m // 4][m % 4],
                                osb[:, (ti % 4) * 512:(ti % 4) * 512 + 512])

                kb.phase(f"pfm{tag}", sync=f_sync, tensor=f_tens,
                         scalar=f_scal, vector=f_vec)

        def proj_tm(tag, wname, kind, slabo, browi=0, swap=False):
            with ExitStack() as ps:
                tmp = ps.enter_context(nc.sbuf_tensor(f"tp{tag}", [128, 4 * 512], F32))
                osb = ps.enter_context(nc.sbuf_tensor(f"os{tag}", [128, 4 * 512], BF16))
                Swl, Sst = kb.qstrm(), kb.qstrm()
                Smm, Ssil, Sev = (kb.strm() for _ in range(3))

                def f_sync(sy):
                    sy.wait_ge(kb.aux.h, wc_done[wname])
                    load_half(sy, Swl, wbfd[wname], D, 0, 512, 0)
                    load_half(sy, Swl, wbfd[wname], D, 512, 512, 1)
                    for ti in range(64):
                        ntd, tokch = ti // 16, ti % 16
                        if kind == "kdr":
                            Sev.wait(sy, ti + 1)
                        else:
                            Ssil.wait(sy, ti + 1)
                        dst_ = (slabo[4 * ntd + tokch // 4][tokch % 4] if swap
                                else slabo[tokch][ntd])
                        Sst.dma(sy, dst_,
                                osb[:, (ti % 4) * 512:(ti % 4) * 512 + 512])
                        if ti % 16 == 1 and 2 <= ti // 16 + 1 < 4:
                            p = ti // 16 + 1
                            load_half(sy, Swl, wbfd[wname], D, p * 512, 512, p % 2)
                    Sst.waitall(sy)
                    Swl.waitall(sy)

                def f_tens(t):
                    for ntd in range(4):
                        Swl.wdma(t, 4 * (ntd + 1) - 1)
                        for tokch in range(16):
                            ti = ntd * 16 + tokch
                            bank = ti % 6
                            if ti >= 6:
                                if kind == "kdr":
                                    Sev.wait(t, ti - 5)
                                else:
                                    Ssil.wait(t, ti - 5)
                            for kch in range(NCH):
                                st = (kch == 0)
                                sp = (kch == NCH - 1) and kind != "btm"
                                li = t.matmul(P[bank][:, 0:512],
                                              xln[:, kch * T + tokch * 128:
                                                  kch * T + tokch * 128 + 128],
                                              wtile(ntd % 2, kch, 512, 0, 512),
                                              start=st, stop=sp)
                            if kind == "btm":
                                li = t.matmul(P[bank][:, 0:512], ones_r_bf[0:1, :],
                                              brow_sb[0:1, browi * D + ntd * 512:
                                                      browi * D + (ntd + 1) * 512],
                                              start=False, stop=True)
                            Smm.inc(li)

                def f_scal(s):
                    for ti in range(64):
                        bank = ti % 6
                        Smm.wait(s, ti + 1)
                        if kind == "kdr":
                            if ti >= 4:
                                Sev.wait(s, ti - 3)
                            Ssil.inc(s.activation(
                                tmp[:, (ti % 4) * 512:(ti % 4) * 512 + 512],
                                P[bank][:, 0:512], AF.Silu))
                        else:
                            if ti >= 4:
                                Sst.wdma(s, ti - 4)
                            Ssil.inc(s.activation(
                                osb[:, (ti % 4) * 512:(ti % 4) * 512 + 512],
                                P[bank][:, 0:512], AF.Identity))

                def f_vec(v):
                    if kind != "kdr":
                        return
                    for ti in range(64):
                        Ssil.wait(v, ti + 1)
                        if ti >= 4:
                            Sst.wdma(v, ti - 4)
                        ntd, tokch = ti // 16, ti % 16
                        for hh in range(4):
                            h = ntd * 4 + hh
                            li = v.tensor_scalar(
                                osb[:, (ti % 4) * 512 + hh * 128:
                                    (ti % 4) * 512 + (hh + 1) * 128],
                                tmp[:, (ti % 4) * 512 + hh * 128:
                                    (ti % 4) * 512 + (hh + 1) * 128],
                                fkr_tm[:, tokch * 16 + h:tokch * 16 + h + 1],
                                None, ALU.mult)
                        Sev.inc(li)

                kb.phase(f"ptm{tag}", sync=f_sync, tensor=f_tens,
                         scalar=f_scal, vector=f_vec)


        def a_chunk_phase(b):
            with ExitStack() as ps:
                fmb = xarena[:, 0:24576]
                tmb = xarena[:, 24576:32768]
                scf = ps.enter_context(nc.sbuf_tensor(f"scf{b}", [128, 2 * 128], F32))
                scm = ps.enter_context(nc.sbuf_tensor(f"scm{b}", [128, 2 * 128], BF16))
                o_buf = ps.enter_context(nc.sbuf_tensor(f"obf{b}", [128, 16 * 128], F32))
                o2b = ps.enter_context(nc.sbuf_tensor(f"o2b{b}", [128, 16 * 128], BF16))
                onb = wbuf[:, 4 * SLOTW:5 * SLOTW]
                rrow = ps.enter_context(nc.sbuf_tensor(f"rr{b}", [1, 128], F32))
                ntp = ps.enter_context(nc.sbuf_tensor(f"ntp{b}", [128, 2 * 128], F32))
                xres = ps.enter_context(nc.sbuf_tensor(f"xr{b}", [128, 2 * 512], F32))
                xout = ps.enter_context(nc.sbuf_tensor(f"xo{b}", [128, 2 * 512], F32))

                Sfm, Stm, Swo, Sxr, Sxw = (kb.qstrm() for _ in range(5))
                Smm1, Ses, Seg, Smm2, Smm3 = (kb.strm() for _ in range(5))
                Sov, Svu, Ssq2, Smm4, Srs, Sbc3, Sno = (kb.strm() for _ in range(7))
                Smm5, Svr, Sz0 = (kb.strm() for _ in range(3))

                def fmtile(tno, h, c):
                    o = (tno * 16 + h) * 512 + (c % 4) * 128
                    return fmb[:, o:o + 128]

                def tmtile(tno, c, h):
                    o = (c % 2) * 2 * 2048 + tno * 2048 + (h // 4) * 512 + (h % 4) * 128
                    return tmb[:, o:o + 128]

                def f_sync(sy):
                    for g in range(4):
                        if g >= 1:
                            Smm3.wait(sy, 64 * g)
                        for tno, sl in enumerate([sA, sB, sC]):
                            Sfm.dma(sy, fmb[:, tno * 16 * 512:(tno + 1) * 16 * 512],
                                    sl[4 * g:4 * g + 4].rearrange("a b p c -> p a b c"))
                        if g == 0:
                            for c in (0, 1):
                                for tno, sl in enumerate([sD, sE]):
                                    o = (c % 2) * 4096 + tno * 2048
                                    Stm.dma(sy, tmb[:, o:o + 2048],
                                            sl[c].rearrange("b p c -> p b c"))
                            sy.wait_ge(kb.aux.h, wc_done["wo"])
                            for p in range(4):
                                load_half(sy, Swo, wbfd["wo"], D, p * 512, 512, p)
                        for c in range(4 * g, 4 * g + 4):
                            cc = c + 2
                            if cc < NCA:
                                Smm3.wait(sy, 16 * (cc - 1))
                                for tno, sl in enumerate([sD, sE]):
                                    o = (cc % 2) * 4096 + tno * 2048
                                    Stm.dma(sy, tmb[:, o:o + 2048],
                                            sl[cc].rearrange("b p c -> p b c"))
                        for dc in range(16):
                            i = 16 * g + dc
                            if i >= 2:
                                Svr.wait(sy, i - 1)
                            Sxr.dma(sy, xres[:, (dc % 2) * 512:(dc % 2) * 512 + 512],
                                    xin[b, dc * 128:(dc + 1) * 128,
                                        g * 512:(g + 1) * 512])
                            if i >= 1:
                                Svr.wait(sy, i)
                                pdc = (dc - 1) % 16
                                pg = g if dc >= 1 else g - 1
                                Sxw.dma(sy, xw[b, pdc * 128:(pdc + 1) * 128,
                                               pg * 512:(pg + 1) * 512],
                                        xout[:, (pdc % 2) * 512:(pdc % 2) * 512 + 512])
                    Svr.wait(sy, 64)
                    Sxw.dma(sy, xw[b, 15 * 128:16 * 128, 3 * 512:4 * 512],
                            xout[:, 512:1024])
                    Sxw.waitall(sy)
                    Sfm.waitall(sy)
                    Stm.waitall(sy)
                    Swo.waitall(sy)
                    Sxr.waitall(sy)

                def f_gp(g_):
                    g_.memset(S_a[:, :], 0.0)
                    Sz0.inc(g_.memset(S_ab[:, :], 0.0))
                    for c in range(NCA):
                        for h in range(HA):
                            i = c * 16 + h
                            Ses.wait(g_, i + 1)
                            Seg.inc(g_.affine_select(
                                scm[:, (h % 2) * 128:(h % 2) * 128 + 128],
                                scf[:, (h % 2) * 128:(h % 2) * 128 + 128],
                                [[1, 128]], ALU.is_ge, 0.0, base=0,
                                channel_multiplier=-1))

                def f_tens(t):
                    Sz0.wait(t, 1)
                    for c in range(NCA):
                        g = c // 4
                        Sfm.wdma(t, 3 * (g + 1) - 1)
                        Stm.wdma(t, 2 * (c + 1) - 1)
                        for h in range(HA):
                            i = c * 16 + h
                            if i >= 2:
                                Ses.wait(t, i - 1)
                            Smm1.inc(t.matmul(P[h % 2][:, 0:128], fmtile(2, h, c),
                                               fmtile(0, h, c), start=True, stop=True))
                            Seg.wait(t, i + 1)
                            if c > 0:
                                Svu.wait(t, (c - 1) * 16 + h + 1)
                            if i >= 2:
                                Sov.wait(t, i - 1)
                            t.matmul(P[2 + h % 2][:, 0:128], tmtile(1, c, h),
                                     scm[:, (h % 2) * 128:(h % 2) * 128 + 128],
                                     start=True, stop=False)
                            Smm2.inc(t.matmul(P[2 + h % 2][:, 0:128],
                                               S_ab[:, h * 128:(h + 1) * 128],
                                               fmtile(1, h, c), start=False, stop=True))
                            if i >= 2:
                                Svu.wait(t, i - 1)
                            Smm3.inc(t.matmul(P[4 + h % 2][:, 0:128], tmtile(0, c, h),
                                               tmtile(1, c, h), start=True, stop=True))
                        Ssq2.wait(t, 16 * (c + 1))
                        if c > 0:
                            Srs.wait(t, c)
                        if c % 4 == 0 and c > 0:
                            Svr.wait(t, 16 * (c // 4))
                        for h in range(HA):
                            li = t.matmul(P[6][0:1, 0:128], ones_c_bf[:, 0:1],
                                          o2b[:, h * 128:(h + 1) * 128],
                                          start=(h == 0), stop=(h == HA - 1))
                        Smm4.inc(li)
                        Srs.wait(t, c + 1)
                        if c > 0:
                            Sno.wait(t, 16 * c)
                        Sbc3.inc(t.matmul(P[7][:, 0:128], ones_r_f[0:1, :],
                                           rrow[0:1, :], start=True, stop=True))
                        if c % 4 == 3:
                            g = c // 4
                            Sno.wait(t, 64 * (g + 1))
                            Swo.waitall(t)
                            for dc in range(16):
                                i = 16 * g + dc
                                if i >= 2:
                                    Svr.wait(t, i - 1)
                                for kch in range(NCH):
                                    li = t.matmul(P[6 + dc % 2][:, 0:512],
                                                  wtile(dc // 4, kch, 512,
                                                        (dc % 4) * 128),
                                                  onb[:, kch * 512:(kch + 1) * 512],
                                                  start=(kch == 0),
                                                  stop=(kch == NCH - 1))
                                Smm5.inc(li)

                def f_scal(s):
                    for c in range(NCA):
                        for h in range(HA):
                            i = c * 16 + h
                            Smm1.wait(s, i + 1)
                            if i >= 2:
                                Seg.wait(s, i - 1)
                            Ses.inc(s.copy(scf[:, (h % 2) * 128:(h % 2) * 128 + 128],
                                           P[h % 2][:, 0:128]))
                        Sov.wait(s, 16 * (c + 1))
                        for h in range(HA):
                            Ssq2.inc(s.activation(o2b[:, h * 128:(h + 1) * 128],
                                                  o_buf[:, h * 128:(h + 1) * 128],
                                                  AF.Square))
                        Smm4.wait(s, c + 1)
                        s.copy(rrow[0:1, :], P[6][0:1, 0:128])
                        s.activation(rrow[0:1, :], rrow[0:1, :], AF.Ln,
                                     scale=cpool[0:1, CP[1.0 / 2048.0]:
                                                 CP[1.0 / 2048.0] + 1],
                                     bias=cpool[0:1, CP[1e-6]:CP[1e-6] + 1])
                        Srs.inc(s.activation(rrow[0:1, :], rrow[0:1, :], AF.Exp,
                                             scale=cpool[0:1, CP[-0.5]:CP[-0.5] + 1]))

                def f_vec(v):
                    Sz0.wait(v, 1)
                    for c in range(NCA):
                        g = c // 4
                        if c >= 1:
                            Ssq2.wait(v, 16 * c)
                        for h in range(HA):
                            i = c * 16 + h
                            Smm2.wait(v, i + 1)
                            Sov.inc(v.tensor_copy(o_buf[:, h * 128:(h + 1) * 128],
                                                  P[2 + h % 2][:, 0:128]))
                            Smm3.wait(v, i + 1)
                            v.scalar_tensor_tensor(
                                S_a[:, h * 128:(h + 1) * 128],
                                S_a[:, h * 128:(h + 1) * 128],
                                etot_b[:, c * 16 + h:c * 16 + h + 1],
                                P[4 + h % 2][:, 0:128], ALU.mult, ALU.add)
                            Svu.inc(v.tensor_copy(S_ab[:, h * 128:(h + 1) * 128],
                                                  S_a[:, h * 128:(h + 1) * 128]))
                        Sbc3.wait(v, c + 1)
                        if c >= 4:
                            Smm5.wait(v, 16 * (c // 4))
                        for dch in range(16):
                            v.tensor_tensor(ntp[:, (dch % 2) * 128:(dch % 2) * 128 + 128],
                                            o_buf[:, dch * 128:(dch + 1) * 128],
                                            P[7][:, 0:128], ALU.mult)
                            Sno.inc(v.tensor_scalar(
                                onb[:, dch * 512 + (c % 4) * 128:
                                    dch * 512 + (c % 4) * 128 + 128],
                                ntp[:, (dch % 2) * 128:(dch % 2) * 128 + 128],
                                nwm_sb[:, dch:dch + 1], None, ALU.mult))
                        if c % 4 == 3:
                            for dc in range(16):
                                i = 16 * g + dc
                                Smm5.wait(v, i + 1)
                                Sxr.wdma(v, i)
                                if i >= 2:
                                    Sxw.wdma(v, i - 2)
                                Svr.inc(v.tensor_tensor(
                                    xout[:, (dc % 2) * 512:(dc % 2) * 512 + 512],
                                    xres[:, (dc % 2) * 512:(dc % 2) * 512 + 512],
                                    P[6 + dc % 2][:, 0:512], ALU.add))

                kb.phase(f"ach{b}", sync=f_sync, gpsimd=f_gp, tensor=f_tens,
                         scalar=f_scal, vector=f_vec)


        def router_phase(tag, b):
            with ExitStack() as ps:
                cw = ps.enter_context(nc.sbuf_tensor(f"cwk{tag}", [128, 12 * 16], F32))
                cwt = ps.enter_context(nc.sbuf_tensor(f"cwt{tag}", [128, 16 * 16], F32))
                cwT = ps.enter_context(nc.sbuf_tensor(f"cwT{tag}", [16, T], F32))
                LG, LGE, MX, NMX, SM_, CUR, MASK, MI, PM, DEN, RD = range(11)

                def sl(i, w=16):
                    return cw[:, i * 16:i * 16 + w]

                def col(i):
                    return cw[:, i * 16:i * 16 + 1]

                Sst = kb.dstrm()
                Smm, Sv1, Ss1, Sv2, Str, Sv3 = (kb.strm() for _ in range(6))

                def f_tens(t):
                    def transpose_of(tc):
                        Sv2.wait(t, tc + 1)
                        if tc >= 2:
                            Sv3.wait(t, tc - 1)
                        Str.inc(t.transpose(P[4 + tc % 2][0:16, 0:128],
                                             cwt[:, tc * 16:(tc + 1) * 16],
                                             ident[:, :]))

                    for tc in range(16):
                        if tc >= 2:
                            Sv1.wait(t, tc - 1)
                        for kch in range(NCH):
                            li = t.matmul(P[tc % 2][:, 0:16],
                                          xln[:, kch * T + tc * 128:
                                              kch * T + tc * 128 + 128],
                                          wrm_sb[:, kch * 16:(kch + 1) * 16],
                                          start=(kch == 0), stop=(kch == NCH - 1))
                        Smm.inc(li)
                        if tc >= 1:
                            transpose_of(tc - 1)
                    transpose_of(15)

                def f_scal(s):
                    for tc in range(16):
                        Sv1.wait(s, tc + 1)
                        Ss1.inc(s.activation(sl(LGE), sl(LG), AF.Exp,
                                             bias=col(NMX)))

                def f_vec(v):
                    Svx = kb.strm()
                    nch_ = [0]

                    def ch(inst, strm=None):
                        if strm is not None:
                            strm.inc(inst)
                            v.wait_ge(strm.s.h, strm.s.n)
                        else:
                            nch_[0] += 1
                            Svx.inc(inst)
                            Svx.wait(v, nch_[0])

                    for tc in range(16):
                        Smm.wait(v, tc + 1)
                        ch(v.tensor_copy(sl(LG), P[tc % 2][:, 0:16]))
                        ch(v.reduce_max(col(MX), sl(LG), axis=AX.X))
                        ch(v.tensor_scalar_mul(col(NMX), col(MX), -1.0), Sv1)
                        Ss1.wait(v, tc + 1)
                        ch(v.reduce_sum(col(SM_), sl(LGE), axis=AX.X))
                        ch(v.reciprocal(col(RD), col(SM_)))
                        ch(v.tensor_scalar(sl(LG), sl(LGE), col(RD), None, ALU.mult))
                        ch(v.tensor_copy(sl(CUR), sl(LG)))
                        ch(v.tensor_scalar_mul(sl(MASK), sl(LG), 0.0))
                        for _ in range(4):
                            ch(v.reduce_max(col(MX), sl(CUR), axis=AX.X))
                            ch(v.tensor_scalar(sl(MI), sl(CUR), col(MX), None,
                                               ALU.is_ge))
                            ch(v.tensor_add(sl(MASK), sl(MASK), sl(MI)))
                            ch(v.scalar_tensor_tensor(sl(CUR), sl(MI), -2.0,
                                                      sl(CUR), ALU.mult, ALU.add))
                        ch(v.tensor_mul(sl(PM), sl(LG), sl(MASK)))
                        ch(v.reduce_sum(col(DEN), sl(PM), axis=AX.X))
                        ch(v.reciprocal(col(RD), col(DEN)))
                        ch(v.tensor_scalar_mul(col(RD), col(RD), 0.8))
                        ch(v.tensor_scalar(cwt[:, tc * 16:(tc + 1) * 16],
                                           sl(PM), col(RD), None, ALU.mult), Sv2)
                        Str.wait(v, tc + 1)
                        Sv3.inc(v.tensor_copy(cwT[0:16, tc * 128:(tc + 1) * 128],
                                              P[4 + tc % 2][0:16, 0:128]))

                def f_sync(sy):
                    Sv3.wait(sy, 16)
                    Sst.dma(sy, cwd[:, :], cwT[:, :])
                    Sst.waitall(sy)

                kb.phase(f"rt{tag}", tensor=f_tens, scalar=f_scal, vector=f_vec,
                         sync=f_sync)

        def g_gen_phase(tag, shared):
            with ExitStack() as ps:
                tmp = ps.enter_context(nc.sbuf_tensor(f"tpg{tag}", [128, 4 * 512], F32))
                tmp2 = ps.enter_context(nc.sbuf_tensor(f"t2g{tag}", [128, 2 * 512], F32))
                osb = ps.enter_context(nc.sbuf_tensor(f"osg{tag}", [128, 4 * 512], BF16))
                cwb = None if shared else ps.enter_context(
                    nc.sbuf_tensor(f"cwb{tag}", [128, 4 * 512], F32))
                Swl, Sst, Srw = kb.qstrm(), kb.qstrm(), kb.qstrm()
                Smm, Ssil, Sev, Sev2 = (kb.strm() for _ in range(4))
                Sbc4, Scw = (kb.strm() for _ in range(2))
                if shared:
                    Sev2 = Sev
                npc = 8 if shared else 32
                w1n, w3n = ("ws1", "ws3") if shared else ("we1", "we3")

                def wsrc(wn, q):
                    wt = wbfd[wn]
                    if shared:
                        return wt, q * 512
                    e, p = q // 2, q % 2
                    return wt[e * D:(e + 1) * D, :], p * 512

                def dst_of(gi):
                    qq, tt = gi // 16, gi % 16
                    m4, nt = tt // 4, tt % 4
                    if shared:
                        return gs_sl[nt][qq * 4 + m4]
                    return ge_sl[qq // 2][nt][(qq % 2) * 4 + m4]

                def f_sync(sy):
                    sy.wait_ge(kb.aux.h, max(wc_done[w1n], wc_done[w3n]))
                    for q in range(npc):
                        if q >= 2:
                            Smm.wait(sy, 16 * (q - 1))
                        t1, c1 = wsrc(w1n, q)
                        t3, c3 = wsrc(w3n, q)
                        load_half(sy, Swl, t1, D, c1, 512, (2 * q) % 4)
                        load_half(sy, Swl, t3, D, c3, 512, (2 * q + 1) % 4)
                        if not shared and q % 2 == 0:
                            e = q // 2
                            for nt in range(NT):
                                i = e * NT + nt
                                if i >= 2:
                                    Sbc4.wait(sy, i - 1)
                                Srw.dma(sy, stage[0:1, (i % 2) * 512:
                                                  (i % 2) * 512 + 512],
                                        cwd[e:e + 1, nt * 512:(nt + 1) * 512])
                        for ti in range(16):
                            gi = q * 16 + ti
                            if gi >= 4:
                                Sev2.wait(sy, gi - 3)
                                Sst.dma(sy, dst_of(gi - 4),
                                        osb[:, ((gi - 4) % 4) * 512:
                                            ((gi - 4) % 4) * 512 + 512])
                    for gi in range(npc * 16 - 4, npc * 16):
                        Sev2.wait(sy, gi + 1)
                        Sst.dma(sy, dst_of(gi),
                                osb[:, (gi % 4) * 512:(gi % 4) * 512 + 512])
                    Sst.waitall(sy)
                    Swl.waitall(sy)
                    if not shared:
                        Srw.waitall(sy)

                def f_tens(t):
                    for q in range(npc):
                        Swl.wdma(t, 8 * (q + 1) - 1)
                        if not shared and q % 2 == 0:
                            e = q // 2
                            for nt in range(NT):
                                i = e * NT + nt
                                Srw.wdma(t, i)
                                if i >= 2:
                                    Scw.wait(t, i - 1)
                                Sbc4.inc(t.matmul(
                                    P[6 + i % 2][:, 0:512], ones_r_f[0:1, :],
                                    stage[0:1, (i % 2) * 512:(i % 2) * 512 + 512],
                                    start=True, stop=True))
                        for ti in range(16):
                            gi = q * 16 + ti
                            m4, nt = ti // 4, ti % 4
                            b1, b3 = gi % 3, 3 + gi % 3
                            if gi >= 3:
                                Ssil.wait(t, gi - 2)
                            for kch in range(NCH):
                                t.matmul(P[b1][:, 0:512],
                                         wtile((2 * q) % 4, kch, 512, m4 * 128),
                                         xln[:, kch * T + nt * 512:
                                             kch * T + nt * 512 + 512],
                                         start=(kch == 0), stop=(kch == NCH - 1))
                            if gi >= 3:
                                Sev.wait(t, gi - 2)
                            for kch in range(NCH):
                                li = t.matmul(P[b3][:, 0:512],
                                              wtile((2 * q + 1) % 4, kch, 512,
                                                    m4 * 128),
                                              xln[:, kch * T + nt * 512:
                                                  kch * T + nt * 512 + 512],
                                              start=(kch == 0),
                                              stop=(kch == NCH - 1))
                            Smm.inc(li)

                def f_scal(s):
                    for gi in range(npc * 16):
                        Smm.wait(s, gi + 1)
                        if gi >= 4:
                            Sev2.wait(s, gi - 3)
                        Ssil.inc(s.activation(tmp[:, (gi % 4) * 512:(gi % 4) * 512 + 512],
                                              P[gi % 3][:, 0:512], AF.Silu))

                def f_vec(v):
                    for gi in range(npc * 16):
                        q, ti = gi // 16, gi % 16
                        nt = ti % 4
                        b3 = 3 + gi % 3
                        if not shared and ti == 0 and q % 2 == 0:
                            e = q // 2
                            for i in range(e * NT, (e + 1) * NT):
                                Sbc4.wait(v, i + 1)
                                Scw.inc(v.tensor_copy(
                                    cwb[:, (i % 4) * 512:(i % 4) * 512 + 512],
                                    P[6 + i % 2][:, 0:512]))
                        Smm.wait(v, gi + 1)
                        if gi >= 4:
                            Sst.wdma(v, gi - 4)
                        if shared:
                            Ssil.wait(v, gi + 1)
                            li = v.scalar_tensor_tensor(
                                osb[:, (gi % 4) * 512:(gi % 4) * 512 + 512],
                                tmp[:, (gi % 4) * 512:(gi % 4) * 512 + 512], 0.2,
                                P[b3][:, 0:512], ALU.mult, ALU.mult)
                            Sev.inc(li)
                        else:
                            Sev.inc(v.tensor_tensor(
                                tmp2[:, (gi % 2) * 512:(gi % 2) * 512 + 512],
                                P[b3][:, 0:512],
                                cwb[:, nt * 512:(nt + 1) * 512], ALU.mult))
                            Ssil.wait(v, gi + 1)
                            Sev2.inc(v.tensor_tensor(
                                osb[:, (gi % 4) * 512:(gi % 4) * 512 + 512],
                                tmp[:, (gi % 4) * 512:(gi % 4) * 512 + 512],
                                tmp2[:, (gi % 2) * 512:(gi % 2) * 512 + 512],
                                ALU.mult))

                kb.phase(f"gg{tag}", sync=f_sync, tensor=f_tens, scalar=f_scal,
                         vector=f_vec)

        def f_out_phase(tag, b, dst):
            with ExitStack() as ps:
                gbuf = ps.enter_context(nc.sbuf_tensor(f"gb{tag}", [128, 2 * 4096], BF16))
                xres = ps.enter_context(nc.sbuf_tensor(f"xrf{tag}", [128, 2 * 512], F32))
                xout = ps.enter_context(nc.sbuf_tensor(f"xof{tag}", [128, 2 * 512], F32))
                Swl, Sgl, Sxr, Sxw = (kb.qstrm() for _ in range(4))
                Smm, Sev = (kb.strm() for _ in range(2))
                WUP = 80

                def f_sync(sy):
                    sy.wait_ge(kb.aux.h, max(wc_done["ws2"], wc_done["we2"]))
                    for pi in range(8):
                        half, nt = pi // 4, pi % 4
                        mb = pi * 160
                        for qq in range(2):
                            if pi > 0:
                                Smm.wait(sy, (pi - 1) * 160 + (136 if qq == 0 else 152))
                            c0 = half * 1024 + qq * 512
                            load_half(sy, Swl, wbfd["ws2"][0:2048, :], 2048, c0,
                                      512, qq * 2)
                            load_half(sy, Swl, wbfd["ws2"][2048:4096, :], 2048, c0,
                                      512, qq * 2 + 1)
                        for L in range(20):
                            if pi > 0 or L >= 2:
                                Smm.wait(sy, mb + 8 * (L - 1))
                            if L < 4:
                                src_ = gs_sl[nt][8 * L:8 * L + 8]
                            else:
                                src_ = ge_sl[L - 4][nt]
                            Sgl.dma(sy, gbuf[:, (L % 2) * 4096:(L % 2) * 4096 + 4096],
                                    src_.rearrange("b p c -> p b c"))
                            if L >= 4:
                                e = L - 3
                                if e < NE:
                                    if e >= 5:
                                        Smm.wait(sy, mb + 32 + 8 * (e - 4))
                                    else:
                                        Smm.wait(sy, mb + 32)
                                    load_half(sy, Swl,
                                              wbfd["we2"][e * HID:(e + 1) * HID, :],
                                              HID, half * 1024, 1024, (4 + e) % 5)
                            if L == 0:
                                if pi > 0:
                                    Smm.wait(sy, pi * 160)
                                load_half(sy, Swl, wbfd["we2"][0:HID, :],
                                          HID, half * 1024, 1024, 4)
                        for dc8 in range(8):
                            i = pi * 8 + dc8
                            dch = half * 8 + dc8
                            if i >= 2:
                                Sev.wait(sy, i - 1)
                            Sxr.dma(sy, xres[:, (i % 2) * 512:(i % 2) * 512 + 512],
                                    xw[b, dch * 128:(dch + 1) * 128,
                                       nt * 512:(nt + 1) * 512])
                            if i >= 1:
                                Sev.wait(sy, i)
                                pi2, pd = (i - 1) // 8, (i - 1) % 8
                                pdch = (pi2 // 4) * 8 + pd
                                pnt = pi2 % 4
                                Sxw.dma(sy, dst[b, pdch * 128:(pdch + 1) * 128,
                                                pnt * 512:(pnt + 1) * 512],
                                        xout[:, ((i - 1) % 2) * 512:
                                             ((i - 1) % 2) * 512 + 512])
                    Sev.wait(sy, 64)
                    Sxw.dma(sy, dst[b, 15 * 128:16 * 128, 3 * 512:4 * 512],
                            xout[:, 512:1024])
                    Sxw.waitall(sy)
                    Sgl.waitall(sy)
                    Swl.waitall(sy)
                    Sxr.waitall(sy)

                def f_tens(t):
                    for pi in range(8):
                        half, nt = pi // 4, pi % 4
                        wb = WUP * pi
                        if pi >= 1:
                            Sev.wait(t, 8 * pi)
                        for j in range(160):
                            Lj = j // 8
                            if j % 8 == 0:
                                Sgl.wdma(t, pi * 20 + Lj)
                            rhs = gbuf[:, (Lj % 2) * 4096 + (j % 8) * 512:
                                       (Lj % 2) * 4096 + (j % 8) * 512 + 512]
                            if j < 32:
                                kch = j
                                Swl.wdma(t, wb + 15)
                                lhs = lambda dc8, kch=kch: wtile(
                                    (dc8 // 4) * 2 + (0 if kch < 16 else 1),
                                    kch % 16, 512, (dc8 % 4) * 128)
                                dcs = range(8)
                            else:
                                e, kch = (j - 32) // 8, (j - 32) % 8
                                Swl.wdma(t, wb + 16 + 4 * (e + 1) - 1)
                                lhs = lambda dc8, e=e, kch=kch: wtile(
                                    (4 + e) % 5, kch, 1024, dc8 * 128)
                                dcs = range(8)
                            for dc8 in dcs:
                                li = t.matmul(P[dc8][:, 0:512], lhs(dc8), rhs,
                                              start=(j == 0), stop=(j == 159))
                            Smm.inc(li)

                def f_vec(v):
                    for dc8 in range(8):
                        v.memset(P[dc8][:, 0:512], 0.0)
                    for pi in range(8):
                        for dc8 in range(8):
                            i = pi * 8 + dc8
                            Smm.wait(v, pi * 160 + 160)
                            Sxr.wdma(v, i)
                            if i >= 2:
                                Sxw.wdma(v, i - 2)
                            Sev.inc(v.tensor_tensor(
                                xout[:, (dc8 % 2) * 512:(dc8 % 2) * 512 + 512],
                                xres[:, (dc8 % 2) * 512:(dc8 % 2) * 512 + 512],
                                P[dc8][:, 0:512], ALU.add))

                kb.phase(f"fo{tag}", sync=f_sync, tensor=f_tens, vector=f_vec)


        def b_chunk_phase(b):
            with ExitStack() as ps:
                abf = xarena
                qf = abf[:, 0:2048]
                kf = abf[:, 2048:4096]
                ktm = abf[:, 4096:6144]
                vtm = abf[:, 6144:8192]
                scb = abf[:, 8192:10240]
                kksb = abf[:, 10240:11264]
                qdb = abf[:, 11264:12288]
                S_bb = abf[:, 12288:16384]
                wfree = wbuf[:, 0:40960].bitcast(F32)
                dtb = wfree[:, 0:4096]
                csb2 = wfree[:, 4096:8192]
                S_b = wfree[:, 8192:12288]
                xres = ps.enter_context(nc.sbuf_tensor(f"xrb{b}", [128, 2 * 512], F32))
                xout = ps.enter_context(nc.sbuf_tensor(f"xob{b}", [128, 2 * 512], F32))
                idx = np.arange(CB, dtype=np.float64)
                betas = np.linspace(np.log2(1.0 / 32.0), np.log2(1.0 / 512.0), HB)
                gdec = 1.0 - np.exp2(betas)
                cdv = [float((gdec[h] ** CB)) for h in range(HB)]

                Sld, Sxr, Sxw, Scr = (kb.qstrm() for _ in range(4))
                Sbc5, Scs, Sz = (kb.strm() for _ in range(3))
                Smm1, Svd, Smm2, Svo, Smm3, Svu, Sqk = (kb.strm() for _ in range(7))

                def f_gp(g_):
                    g_.memset(S_b[:, :], 0.0)
                    Sz.inc(g_.memset(S_bb[:, :], 0.0))

                def f_sync(sy):
                    for hh in range(HB):
                        if hh >= 2:
                            Sbc5.wait(sy, hh - 1)
                        Scr.dma(sy, stage[0:1, (hh % 2) * 512:(hh % 2) * 512 + 512],
                                csr[0:1, hh * 512:(hh + 1) * 512])
                    for ch in range(NCB * HB):
                        c, h = ch // HB, ch % HB
                        r = ch % 2
                        if ch >= 2:
                            Svu.wait(sy, 2 * (ch - 1))
                        Sld.dma(sy, qf[:, r * 1024:r * 1024 + 1024],
                                sA[4 * c + h // 2][(2 * h) % 4:(2 * h) % 4 + 2]
                                .rearrange("b p c -> p b c"))
                        Sld.dma(sy, kf[:, r * 1024:r * 1024 + 1024],
                                sB[4 * c + h // 2][(2 * h) % 4:(2 * h) % 4 + 2]
                                .rearrange("b p c -> p b c"))
                        Sld.dma(sy, ktm[:, r * 1024:r * 1024 + 1024],
                                sC[(h // 2) * 4 + c]
                                [:, :, (h % 2) * 256:(h % 2) * 256 + 256]
                                .rearrange("b p c -> p b c"))
                        Sld.dma(sy, vtm[:, r * 1024:r * 1024 + 1024],
                                sD[(h // 2) * 4 + c]
                                [:, :, (h % 2) * 256:(h % 2) * 256 + 256]
                                .rearrange("b p c -> p b c"))
                        Sld.dma(sy, dtb[:, r * 2048:r * 2048 + 2048],
                                dbTm[h].rearrange("(b p) c -> p b c", p=128))
                        for fs in range(2):
                            i = ch * 2 + fs
                            dch = 2 * h + fs
                            if i >= 2:
                                Svo.wait(sy, i - 1)
                            Sxr.dma(sy, xres[:, (i % 2) * 512:(i % 2) * 512 + 512],
                                    xw[b, dch * 128:(dch + 1) * 128,
                                       c * 512:(c + 1) * 512])
                            if i >= 1:
                                Svo.wait(sy, i)
                                pch, pfs = (i - 1) // 2, (i - 1) % 2
                                pc, ph = pch // HB, pch % HB
                                pdch = 2 * ph + pfs
                                Sxw.dma(sy, xw[b, pdch * 128:(pdch + 1) * 128,
                                               pc * 512:(pc + 1) * 512],
                                        xout[:, ((i - 1) % 2) * 512:
                                             ((i - 1) % 2) * 512 + 512])
                    Svo.wait(sy, 64)
                    Sxw.dma(sy, xw[b, 15 * 128:16 * 128, 3 * 512:4 * 512],
                            xout[:, 512:1024])
                    Sxw.waitall(sy)
                    Sld.waitall(sy)
                    Scr.waitall(sy)
                    Sxr.waitall(sy)

                def f_tens(t):
                    Sz.wait(t, 1)
                    for hh in range(HB):
                        Scr.wdma(t, hh)
                        if hh >= 2:
                            Scs.wait(t, hh - 1)
                        Sbc5.inc(t.matmul(
                            P[6 + hh % 2][:, 0:512], ones_r_f[0:1, :],
                            stage[0:1, (hh % 2) * 512:(hh % 2) * 512 + 512],
                            start=True, stop=True))
                    for ch in range(NCB * HB):
                        c, h = ch // HB, ch % HB
                        r = ch % 2
                        Sld.wdma(t, 5 * (ch + 1) - 1)
                        Sqk.wait(t, 6 * (ch + 1))
                        if ch >= 1:
                            Svd.wait(t, 4 * ch)
                        for jt in range(4):
                            for es_ in range(2):
                                li = t.matmul(P[jt][:, 0:512],
                                              kf[:, (r * 2 + es_) * 512 + jt * 128:
                                                 (r * 2 + es_) * 512 + jt * 128 + 128],
                                              qf[:, (r * 2 + es_) * 512:
                                                 (r * 2 + es_) * 512 + 512],
                                              start=(es_ == 0), stop=(es_ == 1))
                            Smm1.inc(li)
                        Svd.wait(t, 4 * (ch + 1))
                        if c >= 1:
                            Svu.wait(t, 2 * ((c - 1) * HB + h + 1))
                        if ch >= 1:
                            Svo.wait(t, 2 * ch)
                        for fs in range(2):
                            for jt in range(4):
                                t.matmul(P[4 + fs][:, 0:512],
                                         vtm[:, r * 1024 + jt * 256 + fs * 128:
                                             r * 1024 + jt * 256 + fs * 128 + 128],
                                         scb[:, jt * 512:(jt + 1) * 512],
                                         start=(jt == 0), stop=False)
                            for es_ in range(2):
                                li = t.matmul(P[4 + fs][:, 0:512],
                                              S_bb[:, (h * 2 + es_) * 256 + fs * 128:
                                                   (h * 2 + es_) * 256 + fs * 128 + 128],
                                              qdb[:, es_ * 512:(es_ + 1) * 512],
                                              start=False, stop=(es_ == 1))
                            Smm2.inc(li)
                        if ch >= 1:
                            Svu.wait(t, 2 * ch)
                        for es_ in range(2):
                            for jt in range(4):
                                li = t.matmul(P[6 + es_][:, 0:256],
                                              kksb[:, jt * 256 + es_ * 128:
                                                   jt * 256 + es_ * 128 + 128],
                                              vtm[:, r * 1024 + jt * 256:
                                                  r * 1024 + jt * 256 + 256],
                                              start=(jt == 0), stop=(jt == 3))
                            Smm3.inc(li)

                def f_vec(v):
                    Sz.wait(v, 1)
                    for hh in range(HB):
                        Sbc5.wait(v, hh + 1)
                        Scs.inc(v.tensor_copy(csb2[:, hh * 512:(hh + 1) * 512],
                                              P[6 + hh % 2][:, 0:512]))
                    for ch in range(NCB * HB):
                        c, h = ch // HB, ch % HB
                        r = ch % 2
                        Sld.wdma(v, 5 * (ch + 1) - 1)
                        if ch >= 1:
                            Smm2.wait(v, 2 * ch)
                            Smm3.wait(v, 2 * ch)
                        for es_ in range(2):
                            v.tensor_tensor(qdb[:, es_ * 512:(es_ + 1) * 512],
                                            qf[:, (r * 2 + es_) * 512:
                                               (r * 2 + es_) * 512 + 512],
                                            csb2[:, h * 512:(h + 1) * 512], ALU.mult)
                        for jt in range(4):
                            li = v.tensor_scalar(kksb[:, jt * 256:(jt + 1) * 256],
                                                 ktm[:, r * 1024 + jt * 256:
                                                     r * 1024 + jt * 256 + 256],
                                                 ksbt_sb[:, jt * 8 + h:jt * 8 + h + 1],
                                                 None, ALU.mult)
                        Sqk.inc(li, 6)
                        for jt in range(4):
                            Smm1.wait(v, 4 * ch + jt + 1)
                            Svd.inc(v.tensor_tensor(
                                scb[:, jt * 512:(jt + 1) * 512], P[jt][:, 0:512],
                                dtb[:, r * 2048 + jt * 512:r * 2048 + jt * 512 + 512],
                                ALU.mult))
                        for fs in range(2):
                            i = ch * 2 + fs
                            Smm2.wait(v, 2 * ch + fs + 1)
                            Sxr.wdma(v, i)
                            if i >= 2:
                                Sxw.wdma(v, i - 2)
                            Svo.inc(v.tensor_tensor(
                                xout[:, (fs % 2) * 512:(fs % 2) * 512 + 512],
                                xres[:, (i % 2) * 512:(i % 2) * 512 + 512],
                                P[4 + fs][:, 0:512], ALU.add))
                        for es_ in range(2):
                            Smm3.wait(v, 2 * ch + es_ + 1)
                            o = (h * 2 + es_) * 256
                            v.scalar_tensor_tensor(S_b[:, o:o + 256], S_b[:, o:o + 256],
                                                   cdv[h], P[6 + es_][:, 0:256],
                                                   ALU.mult, ALU.add)
                            Svu.inc(v.tensor_copy(S_bb[:, o:o + 256], S_b[:, o:o + 256]))

                kb.phase(f"bch{b}", gpsimd=f_gp, sync=f_sync, tensor=f_tens,
                         vector=f_vec)

        def finish(src_for_out=None):
            def f_g(g):
                a = Strm(kb.aux)
                if src_for_out is not None:
                    fs = src_for_out.rearrange("a b c -> (a b c)")
                    fd = out.rearrange("a b c -> (a b c)")
                    n = 2 * D * T
                    CH = 1 << 20
                    for o in range(0, n, CH):
                        a.dma_batched(g, fd[o:min(n, o + CH)],
                                      fs[o:min(n, o + CH)])
                if dbg is not None:
                    dsrc = {"xw": xw, "sA": sA, "sB": sB, "sC": sC, "sD": sD,
                            "sE": sE, "facd": facd, "cwd": cwd, "ge": ge_sl,
                            "gs": gs_sl, "xlnd": xlnd}[dump]
                    nel = int(np.prod(dbg.shape))
                    r = {1: "a -> (a)", 2: "a b -> (a b)", 3: "a b c -> (a b c)",
                         4: "a b c d -> (a b c d)", 5: "a b c d e -> (a b c d e)"}
                    fs = dsrc.rearrange(r[len(dsrc.shape)])
                    fd = dbg.rearrange(r[len(dbg.shape)])
                    CH = 1 << 20
                    for o in range(0, nel, CH):
                        a.dma_batched(g, fd[o:min(nel, o + CH)],
                                      fs[o:min(nel, o + CH)])
                g.wait_ge(a.s.h, a.s.n)
            kb.phase("finish", gpsimd=f_g)

        def dump_xln():
            if dump == "xlnd":
                def f_g(g):
                    a = Strm(kb.aux)
                    a.dma(g, xlnd[:, :], xln[:, :])
                    g.wait_ge(a.s.h, a.s.n)
                kb.phase("dumpxln", gpsimd=f_g)

        seq = []

        def step(name, fn):
            seq.append((name, fn))

        for b in range(2):
            step(f"aln{b}", lambda b=b: ln_phase(f"a{b}", b, xin))
            step(f"axd{b}", lambda b=b: dump_xln())
            step(f"ald{b}", lambda b=b: a_ld_phase(b))
            step(f"aqd{b}", lambda b=b: proj_fm(f"qd{b}", "wq", "qd2", [sA, sB]))
            step(f"akd{b}", lambda b=b: proj_fm(f"kd{b}", "wk", "kd", [sC]))
            step(f"akr{b}", lambda b=b: proj_tm(f"kr{b}", "wk", "kdr", sD))
            step(f"avt{b}", lambda b=b: proj_tm(f"vt{b}", "wv", "vtm", sE))
            step(f"ach{b}", lambda b=b: a_chunk_phase(b))
            for r in range(2):
                step(f"fln{b}{r}", lambda b=b, r=r: ln_phase(f"f{b}{r}", b, xw))
                step(f"frt{b}{r}", lambda b=b, r=r: router_phase(f"{b}{r}", b))
                step(f"fgs{b}{r}", lambda b=b, r=r: g_gen_phase(f"s{b}{r}", True))
                step(f"fge{b}{r}", lambda b=b, r=r: g_gen_phase(f"e{b}{r}", False))
                step(f"fo3{b}{r}", lambda b=b, r=r: f_out_phase(
                    f"{b}{r}", b, xw if r == 0 else out))
                if r == 0:
                    step(f"bln{b}", lambda b=b: ln_phase(f"b{b}", b, xw))
                    step(f"bqf{b}", lambda b=b: proj_fm(f"bq{b}", "bwq", "biasq", [sA]))
                    step(f"bkf{b}", lambda b=b: proj_fm(f"bk{b}", "bwk", "biask", [sB]))
                    step(f"bkt{b}", lambda b=b: proj_tm(f"bkt{b}", "bwk", "btm", sC, 0, swap=True))
                    step(f"bvt{b}", lambda b=b: proj_tm(f"bvt{b}", "bwv", "btm", sD, 1, swap=True))
                    step(f"bch{b}", lambda b=b: b_chunk_phase(b))

        if stop_after == "w0":
            seq = []
        ran = []
        for name, fn in seq:
            fn()
            ran.append(name)
            if stop_after is not None and name == stop_after:
                break
        finish(src_for_out=xw if stop_after is not None else None)
    return nc


def _consts():
    idx = np.arange(CB, dtype=np.float64)
    betas = np.linspace(np.log2(1.0 / 32.0), np.log2(1.0 / 512.0), HB)
    g = (1.0 - np.exp2(betas))
    Dm = np.where(idx[:, None] >= idx[None, :],
                  g[:, None, None] ** (idx[:, None] - idx[None, :]), 0.0)
    dbT = np.ascontiguousarray(Dm.transpose(0, 2, 1)).astype(np.float32)
    csr = (g[:, None] ** (idx + 1)).astype(np.float32).reshape(1, -1)
    ks = (g[:, None] ** (CB - 1 - idx)).astype(np.float32)
    ksbt = np.zeros((128, 32), np.float32)
    for js in range(4):
        for h in range(HB):
            ksbt[:, js * 8 + h] = ks[h, js * 128:(js + 1) * 128]
    cd = (g ** CB).astype(np.float32)
    return dbT, csr, ksbt, cd


def prep_core_inputs(inputs, core):
    b0 = core * 2
    x = np.ascontiguousarray(
        np.transpose(inputs["L0_attn_a_x"][b0:b0 + 2], (0, 2, 1))).astype(np.float32)
    dbT, csr, ksbt, _cd = _consts()
    wfmv = np.ascontiguousarray(
        inputs["L0_attn_a_Wf"].reshape(16, 128, 16).transpose(1, 0, 2).reshape(128, 256))
    wrmv = np.ascontiguousarray(
        inputs["L0_ffn_w_router"].reshape(16, 128, 16).transpose(1, 0, 2).reshape(128, 256))
    normwm = np.ascontiguousarray(inputs["L0_attn_a_norm_w"].reshape(16, 128).T)
    biasm = np.stack([np.ascontiguousarray(inputs["L0_attn_b_bq"].reshape(16, 128).T),
                      np.ascontiguousarray(inputs["L0_attn_b_bk"].reshape(16, 128).T)])
    biasrow = np.stack([inputs["L0_attn_b_bk"], inputs["L0_attn_b_bv"]]).astype(np.float32)
    im = {
        "x": x,
        "wq": inputs["L0_attn_a_Wq"], "wk": inputs["L0_attn_a_Wk"],
        "wv": inputs["L0_attn_a_Wv"], "wo": inputs["L0_attn_a_Wo"],
        "bwq": inputs["L0_attn_b_Wq"], "bwk": inputs["L0_attn_b_Wk"],
        "bwv": inputs["L0_attn_b_Wv"],
        "we1": inputs["L0_ffn_we1"].reshape(NE * D, HID),
        "we3": inputs["L0_ffn_we3"].reshape(NE * D, HID),
        "we2": inputs["L0_ffn_we2"].reshape(NE * HID, D),
        "ws1": inputs["L0_ffn_ws1"], "ws3": inputs["L0_ffn_ws3"],
        "ws2": inputs["L0_ffn_ws2"],
        "wfm": wfmv, "wrm": wrmv, "normwm": normwm,
        "biasm": biasm.astype(np.float32), "biasrow": biasrow,
        "identm": np.eye(128, dtype=np.float32),
        "dbTm": dbT, "csr": csr, "ksbtm": ksbt,
    }
    return {k: np.ascontiguousarray(np.asarray(v, dtype=v.dtype if hasattr(v, "dtype") else np.float32))
            for k, v in im.items()}


def kernel(**inputs):
    nc = build_program()
    in_maps = [prep_core_inputs(inputs, c) for c in range(8)]
    res = run_bass_kernel_spmd(nc, in_maps, core_ids=list(range(8)))
    outs = []
    for c in range(8):
        o = res.results[c]["out"]
        outs.append(np.transpose(o, (0, 2, 1)))
    return np.ascontiguousarray(np.concatenate(outs, axis=0))
